# version 06
# baseline (speedup 1.0000x reference)
import numpy as np
import ml_dtypes
from contextlib import ExitStack

import concourse.bass as bass
import concourse.mybir as mybir
from concourse.bass_utils import run_bass_kernel_spmd

F32 = mybir.dt.float32
BF16 = mybir.dt.bfloat16
I32 = mybir.dt.int32
AF = mybir.ActivationFunctionType
ALU = mybir.AluOpType
AX = mybir.AxisListType

NCORES = 8
B = 4
S = 8192
D = 1024
CH = S // NCORES
NT = B * CH
EPS0 = 1e-5
SQ2 = float(np.sqrt(2.0))



def ln_vec(v, lnv, lns, i, lnv0, lns0, src, dst, bn6, bn2, rstd, eps):
    v.bn_stats(bn6[:, 0:6], src[:, 0:512]).then_inc(lnv.sem, 1)
    v.bn_stats(bn6[:, 6:12], src[:, 512:1024]).then_inc(lnv.sem, 1)
    lnv.n += 2
    v.wait_ge(lnv.sem, lnv.n)
    v.bn_aggr(bn2[:], bn6[:]).then_inc(lnv.sem, 1)
    lnv.n += 1
    v.wait_ge(lnv.sem, lnv.n)
    v.tensor_scalar(rstd[:], bn2[:, 1:2], eps, None, ALU.add).then_inc(lnv.sem, 1)
    lnv.n += 1
    v.wait_ge(lns.sem, lns0 + 3 * (i + 1))
    v.tensor_scalar(dst, src[:], bn2[:, 0:1], rstd[:], ALU.subtract, ALU.mult).then_inc(lnv.sem, 1)
    lnv.n += 1


def ln_scalar(sc, lnv, lns, i, lnv_ready, rstd):
    sc.wait_ge(lnv.sem, lnv_ready)
    sc.activation(rstd[:], rstd[:], AF.Ln, scale=2.0).then_inc(lns.sem, 1)
    lns.n += 1
    sc.wait_ge(lns.sem, lns.n)
    sc.activation(rstd[:], rstd[:], AF.Exp, scale=-0.5).then_inc(lns.sem, 1)
    lns.n += 1
    sc.wait_ge(lns.sem, lns.n)
    sc.mul(rstd[:], rstd[:], SQ2).then_inc(lns.sem, 1)
    lns.n += 1


class Chan:

    def __init__(self, ctx, nc, name):
        self.sem = ctx.enter_context(nc.semaphore(name))
        self.n = 0

    def inc(self, inst, k=1):
        inst.then_inc(self.sem, k)
        self.n += k
        return self.n



def build_l0_qkv(nc=None, sctx=None, pre="", io=None):
    if nc is None:
        nc = bass.Bass("TRN2", target_bir_lowering=False, debug=False)
    if io is None:
        io = dict(
            xT_own=nc.dram_tensor("xT_own", [B, D, CH], F32, kind="ExternalInput").ap(),
            Wq=nc.dram_tensor("Wq", [D, D], F32, kind="ExternalInput").ap(),
            WkT=nc.dram_tensor("WkT", [D, D], F32, kind="ExternalInput").ap(),
            bq_t=nc.dram_tensor("bq_t", [128, 8], F32, kind="ExternalInput").ap(),
            qpT=nc.dram_tensor("qpT", [D, NT], F32, kind="ExternalOutput").ap())
    xT_own, Wq, WkT, bq_t, qpT_o = io["xT_own"], io["Wq"], io["WkT"], io["bq_t"], io["qpT"]

    with ExitStack() as ctx:
        cc = sctx if sctx is not None else ctx
        ec_ = ctx.enter_context
        xTo = ec_(nc.sbuf_tensor(pre + "xTo", [128, 8 * NT], BF16))
        wq = ec_(nc.sbuf_tensor(pre + "wq", [128, 8 * D], BF16))
        wkt = ec_(nc.sbuf_tensor(pre + "wkt", [128, 8 * D], BF16))
        bqv = ec_(nc.sbuf_tensor(pre + "bqv", [128, 8], F32))
        qt_tg = ec_(nc.sbuf_tensor(pre + "qt_tg", [128, 8 * 1024], BF16))
        qp_sb = [ec_(nc.sbuf_tensor(pre + f"qp_sb{i}", [128, 1024], F32)) for i in range(2)]
        ps = [ec_(nc.psum_tensor(pre + f"ps{i}", [128, 1024], F32)) for i in range(2)]
        ld = Chan(cc, nc, pre + "ld")
        mmq = Chan(cc, nc, pre + "mmq")
        evq = Chan(cc, nc, pre + "evq")
        mmp = Chan(cc, nc, pre + "mmp")
        evp = Chan(cc, nc, pre + "evp")
        st = [Chan(cc, nc, pre + f"st{i}") for i in range(2)]
        block = ec_(nc.Block())

        @block.gpsimd
        def _(g):
            ld.inc(g.dma_start(out=wq[:].rearrange("p (c f) -> p c f", f=D),
                               in_=Wq.rearrange("(c p) f -> p c f", p=128)), 16)
            ld.inc(g.dma_start(out=wkt[:].rearrange("p (c f) -> p c f", f=D),
                               in_=WkT.rearrange("(c p) f -> p c f", p=128)), 16)
            ld.inc(g.dma_start(out=bqv[:], in_=bq_t), 16)
            for b in range(B):
                ld.inc(g.dma_start(
                    out=xTo[:].rearrange("p (c s) -> p c s", s=NT)[:, :, b * CH:(b + 1) * CH],
                    in_=xT_own[b].rearrange("(c p) s -> p c s", p=128)), 16)

        LD_ALL = (3 + B) * 16

        @block.tensor
        def _(t):
            t.wait_ge(ld.sem, LD_ALL)
            for tg in range(4):
                t0 = tg * 1024
                if tg > 0:
                    t.wait_ge(evp.sem, tg * 8)
                for ecp in range(8):
                    u = tg * 8 + ecp
                    if u >= 2:
                        t.wait_ge(evq.sem, u - 1)
                    p = ps[u % 2]
                    for dc in range(8):
                        for nh in range(2):
                            mm = t.matmul(
                                p[:, nh * 512:(nh + 1) * 512],
                                wq[:, dc * D + ecp * 128: dc * D + (ecp + 1) * 128],
                                xTo[:, dc * NT + t0 + nh * 512: dc * NT + t0 + (nh + 1) * 512],
                                start=(dc == 0), stop=(dc == 7))
                    mm.then_inc(mmq.sem, 1)
                for dk in range(8):
                    u = tg * 8 + dk
                    t.wait_ge(evq.sem, tg * 8 + 8)
                    if u >= 2:
                        t.wait_ge(evp.sem, u - 1)
                    p = ps[u % 2]
                    for ecp in range(8):
                        for nh in range(2):
                            mm = t.matmul(
                                p[:, nh * 512:(nh + 1) * 512],
                                wkt[:, ecp * D + dk * 128: ecp * D + (dk + 1) * 128],
                                qt_tg[:, ecp * 1024 + nh * 512: ecp * 1024 + (nh + 1) * 512],
                                start=(ecp == 0), stop=(ecp == 7))
                    mm.then_inc(mmp.sem, 1)

        @block.vector
        def _(v):
            for tg in range(4):
                for ecp in range(8):
                    u = tg * 8 + ecp
                    v.wait_ge(mmq.sem, u + 1)
                    v.tensor_scalar(qt_tg[:, ecp * 1024:(ecp + 1) * 1024], ps[u % 2][:],
                                    bqv[:, ecp:ecp + 1], None, ALU.add).then_inc(evq.sem, 1)

        @block.scalar
        def _(sc):
            for tg in range(4):
                for dk in range(8):
                    u = tg * 8 + dk
                    sc.wait_ge(mmp.sem, u + 1)
                    if u >= 2:
                        sc.wait_ge(st[u % 2].sem, 16 * (u // 2))
                    sc.copy(qp_sb[u % 2][:], ps[u % 2][:]).then_inc(evp.sem, 1)
                    sc.wait_ge(evp.sem, u + 1)
                    sc.dma_start(
                        out=qpT_o[dk * 128:(dk + 1) * 128, tg * 1024:(tg + 1) * 1024],
                        in_=qp_sb[u % 2][:]).then_inc(st[u % 2].sem, 16)

        @block.sync
        def _(sy):
            sy.wait_ge(st[0].sem, 16 * 16)
            sy.wait_ge(st[1].sem, 16 * 16)

    return nc



def build_l0_attn(nc=None, sctx=None, pre="", io=None):
    if nc is None:
        nc = bass.Bass("TRN2", target_bir_lowering=False, debug=False)
    if io is None:
        io = dict(
            x_full=nc.dram_tensor("x_full", [B, S, D], F32, kind="ExternalInput").ap(),
            xT_full=nc.dram_tensor("xT_full", [B, D, S], F32, kind="ExternalInput").ap(),
            x_own=nc.dram_tensor("x_own", [B, CH, D], F32, kind="ExternalInput").ap(),
            qpT=nc.dram_tensor("qpT", [D, NT], F32, kind="ExternalInput").ap(),
            Wv=nc.dram_tensor("Wv", [D, D], F32, kind="ExternalInput").ap(),
            bv_r=nc.dram_tensor("bv_r", [1, D], F32, kind="ExternalInput").ap(),
            ones_m=nc.dram_tensor("ones_m", [128, 128], F32, kind="ExternalInput").ap(),
            ident_f=nc.dram_tensor("ident_f", [128, 128], F32, kind="ExternalInput").ap(),
            r1=nc.dram_tensor("r1", [NT, D], F32, kind="ExternalOutput").ap())
    x_full, xT_full, x_own, qpT_i = io["x_full"], io["xT_full"], io["x_own"], io["qpT"]
    Wv, bv_r, ones_m, ident_f, r1_o = io["Wv"], io["bv_r"], io["ones_m"], io["ident_f"], io["r1"]

    NKQ = 8
    KQ = S // NKQ
    KCQ = KQ // 128
    CPQ = KCQ * 2
    CPB = NKQ * CPQ
    CXQ = 8 * 2

    with ExitStack() as ctx:
        cc = sctx if sctx is not None else ctx
        ec_ = ctx.enter_context
        qpt = ec_(nc.sbuf_tensor(pre + "qpt", [128, 8 * CH], BF16))
        xtq = [ec_(nc.sbuf_tensor(pre + f"xtq{i}", [128, 8 * KQ], BF16)) for i in range(2)]
        xbq = [ec_(nc.sbuf_tensor(pre + f"xbq{i}", [128, KCQ * D], BF16)) for i in range(2)]
        att = ec_(nc.sbuf_tensor(pre + "att", [128, KCQ * CH], BF16))
        ctxa = ec_(nc.sbuf_tensor(pre + "ctxa", [128, 8 * CH], F32))
        ctxb = ec_(nc.sbuf_tensor(pre + "ctxb", [128, 8 * CH], BF16))
        wv = ec_(nc.sbuf_tensor(pre + "wv", [128, 8 * D], BF16))
        ones_bf = ec_(nc.sbuf_tensor(pre + "ones_bf", [128, 128], BF16))
        id_f = ec_(nc.sbuf_tensor(pre + "id_f", [128, 128], F32))
        neg32 = ec_(nc.sbuf_tensor(pre + "neg32", [128, 1], F32))
        bv_bf = ec_(nc.sbuf_tensor(pre + "bv_bf", [1, D], BF16))
        bvb = ec_(nc.sbuf_tensor(pre + "bvb", [128, D], F32))
        sums_sb = ec_(nc.sbuf_tensor(pre + "sums_sb", [1, CH], F32))
        recip_sb = ec_(nc.sbuf_tensor(pre + "recip_sb", [1, CH], F32))
        recip_t = ec_(nc.sbuf_tensor(pre + "recip_t", [128, 8], F32))
        xo_t = [ec_(nc.sbuf_tensor(pre + f"xo_t{i}", [128, D], F32)) for i in range(2)]
        radd = ec_(nc.sbuf_tensor(pre + "radd", [128, D], F32))
        r1_t = [ec_(nc.sbuf_tensor(pre + f"r1_t{i}", [128, D], F32)) for i in range(2)]
        bn6 = ec_(nc.sbuf_tensor(pre + "bn6", [128, 12], F32))
        bn2 = ec_(nc.sbuf_tensor(pre + "bn2", [128, 2], F32))
        rstd = ec_(nc.sbuf_tensor(pre + "rstd", [128, 1], F32))

        ps_sc = [ec_(nc.psum_tensor(pre + f"ps_sc{i}", [128, 512], F32)) for i in range(2)]
        ps_cx = [ec_(nc.psum_tensor(pre + f"ps_cx{i}", [128, 512], F32)) for i in range(2)]
        ps_sm = ec_(nc.psum_tensor(pre + "ps_sm", [1, 1024], F32))
        ps_out = ec_(nc.psum_tensor(pre + "ps_out", [128, 1024], F32))

        ldc = Chan(cc, nc, pre + "ldc")
        ldm = Chan(cc, nc, pre + "ldm")
        ldq = Chan(cc, nc, pre + "ldq")
        ltq = [Chan(cc, nc, pre + f"ltq{i}") for i in range(2)]
        lbq = [Chan(cc, nc, pre + f"lbq{i}") for i in range(2)]
        ldi = Chan(cc, nc, pre + "ldi")
        lxo = [Chan(cc, nc, pre + f"lxo{i}") for i in range(2)]
        msc = Chan(cc, nc, pre + "msc")
        msm = Chan(cc, nc, pre + "msm")
        mcx = Chan(cc, nc, pre + "mcx")
        mbv = Chan(cc, nc, pre + "mbv")
        mtr = Chan(cc, nc, pre + "mtr")
        mop = Chan(cc, nc, pre + "mop")
        ex = Chan(cc, nc, pre + "ex")
        ecx = Chan(cc, nc, pre + "ecx")
        ecb = Chan(cc, nc, pre + "ecb")
        erc = Chan(cc, nc, pre + "erc")
        ert = Chan(cc, nc, pre + "ert")
        ebv = Chan(cc, nc, pre + "ebv")
        lnv = Chan(cc, nc, pre + "lnv")
        lns = Chan(cc, nc, pre + "lns")
        st = [Chan(cc, nc, pre + f"st{i}") for i in range(2)]

        block = ec_(nc.Block())

        LV, LS = 7, 3

        @block.gpsimd
        def _(g):
            g.dma_start(out=wv[:].rearrange("p (c f) -> p c f", f=D),
                        in_=Wv.rearrange("(c p) f -> p c f", p=128)).then_inc(ldc.sem, 16)
            g.dma_start(out=ones_bf[:], in_=ones_m).then_inc(ldc.sem, 16)
            g.dma_start(out=bv_bf[:], in_=bv_r).then_inc(ldc.sem, 16)
            g.memset(neg32[:], -32.0).then_inc(ldm.sem, 1)
            for b in range(B):
                if b > 0:
                    g.wait_ge(msc.sem, b * CPB)
                g.dma_start(
                    out=qpt[:].rearrange("p (c q) -> p c q", q=CH),
                    in_=qpT_i[:, b * CH:(b + 1) * CH].rearrange("(c p) q -> p c q", p=128),
                ).then_inc(ldq.sem, 16)
                for kq in range(NKQ):
                    qs = b * NKQ + kq
                    buf = qs % 2
                    if qs >= 2:
                        g.wait_ge(msc.sem, (qs - 1) * CPQ)
                    g.dma_start(
                        out=xtq[buf][:].rearrange("p (c k) -> p c k", k=KQ),
                        in_=xT_full[b, :, kq * KQ:(kq + 1) * KQ].rearrange("(c p) k -> p c k", p=128),
                    ).then_inc(ltq[buf].sem, 16)
                    if qs >= 2:
                        g.wait_ge(mcx.sem, (qs - 1) * CXQ)
                    g.dma_start(
                        out=xbq[buf][:].rearrange("p (c d) -> p c d", d=D),
                        in_=x_full[b, kq * KQ:(kq + 1) * KQ, :].rearrange("(c p) d -> p c d", p=128),
                    ).then_inc(lbq[buf].sem, 16)

        @block.sync
        def _(sy):
            sy.dma_start(out=id_f[:], in_=ident_f).then_inc(ldi.sem, 16)
            for b in range(B):
                for qt in range(8):
                    u = b * 8 + qt
                    if u >= 2:
                        sy.wait_ge(lnv.sem, (u - 2) * LV + 2)
                    sy.dma_start(out=xo_t[u % 2][:],
                                 in_=x_own[b, qt * 128:(qt + 1) * 128, :]).then_inc(lxo[u % 2].sem, 16)
                    sy.wait_ge(lnv.sem, (u + 1) * LV)
                    sy.dma_start(out=r1_o[u * 128:(u + 1) * 128, :],
                                 in_=r1_t[u % 2][:]).then_inc(st[u % 2].sem, 16)
            sy.wait_ge(st[0].sem, 16 * 16)
            sy.wait_ge(st[1].sem, 16 * 16)

        @block.tensor
        def _(t):
            t.wait_ge(ldc.sem, 48)
            for nh in range(2):
                t.matmul(ps_out[:, nh * 512:(nh + 1) * 512], ones_bf[0:1, :],
                         bv_bf[:, nh * 512:(nh + 1) * 512],
                         start=True, stop=True).then_inc(mbv.sem, 1)
            for b in range(B):
                t.wait_ge(ldq.sem, 16 * (b + 1))
                for kq in range(NKQ):
                    qs = b * NKQ + kq
                    t.wait_ge(ltq[qs % 2].sem, 16 * (qs // 2 + 1))
                    for kc in range(KCQ):
                        for qh in range(2):
                            ci = qs * CPQ + kc * 2 + qh
                            if ci >= 2:
                                t.wait_ge(ex.sem, ci - 1)
                            p = ps_sc[ci % 2]
                            for dc in range(8):
                                mm = t.matmul(
                                    p[:],
                                    xtq[qs % 2][:, dc * KQ + kc * 128: dc * KQ + (kc + 1) * 128],
                                    qpt[:, dc * CH + qh * 512: dc * CH + (qh + 1) * 512],
                                    start=(dc == 0), stop=(dc == 7))
                            mm.then_inc(msc.sem, 1)
                            pc = ci - 1
                            if pc >= b * CPB:
                                if pc == b * CPB:
                                    t.wait_ge(erc.sem, 2 * b)
                                t.wait_ge(ex.sem, pc + 1)
                                kcp = (pc % CPQ) // 2
                                qhp = pc % 2
                                t.matmul(ps_sm[0:1, qhp * 512:(qhp + 1) * 512],
                                         ones_bf[:, 0:1],
                                         att[:, kcp * CH + qhp * 512: kcp * CH + (qhp + 1) * 512],
                                         start=(pc < b * CPB + 2),
                                         stop=(pc == (b + 1) * CPB - 2)).then_inc(msm.sem, 1)
                    t.wait_ge(ex.sem, (qs + 1) * CPQ)
                    t.wait_ge(lbq[qs % 2].sem, 16 * (qs // 2 + 1))
                    for dc in range(8):
                        for qh in range(2):
                            gi = qs * CXQ + dc * 2 + qh
                            if gi >= 2:
                                t.wait_ge(ecx.sem, gi - 1)
                            if gi % (CXQ * NKQ) == 0 and gi > 0:
                                t.wait_ge(ert.sem, gi // (CXQ * NKQ))
                            p = ps_cx[gi % 2]
                            for kc in range(KCQ):
                                mm = t.matmul(
                                    p[:],
                                    xbq[qs % 2][:, kc * D + dc * 128: kc * D + (dc + 1) * 128],
                                    att[:, kc * CH + qh * 512: kc * CH + (qh + 1) * 512],
                                    start=(kc == 0), stop=(kc == KCQ - 1))
                            mm.then_inc(mcx.sem, 1)
                lc = (b + 1) * CPB - 1
                t.wait_ge(ex.sem, lc + 1)
                kcp = (lc % CPQ) // 2
                qhp = lc % 2
                t.matmul(ps_sm[0:1, qhp * 512:(qhp + 1) * 512], ones_bf[:, 0:1],
                         att[:, kcp * CH + qhp * 512: kcp * CH + (qhp + 1) * 512],
                         start=False, stop=True).then_inc(msm.sem, 1)
                t.wait_ge(erc.sem, 2 * (b + 1))
                t.wait_ge(ldi.sem, 16)
                if b > 0:
                    t.wait_ge(ert.sem, b)
                for qt in range(8):
                    tr = t.matmul(ps_cx[0][:, qt:qt + 1],
                                  recip_sb[0:1, qt * 128:(qt + 1) * 128],
                                  id_f[0:1, 0:1], start=True, stop=True)
                tr.then_inc(mtr.sem, 1)
                t.wait_ge(ecb.sem, (b + 1) * 8)
                for qt in range(8):
                    u = b * 8 + qt
                    if u >= 1:
                        t.wait_ge(lnv.sem, (u - 1) * LV + 1)
                    else:
                        t.wait_ge(ebv.sem, 1)
                    for ecc in range(8):
                        for nh in range(2):
                            mm = t.matmul(
                                ps_out[:, nh * 512:(nh + 1) * 512],
                                ctxb[:, ecc * CH + qt * 128: ecc * CH + (qt + 1) * 128],
                                wv[:, ecc * D + nh * 512: ecc * D + (nh + 1) * 512],
                                start=(ecc == 0), stop=(ecc == 7))
                    mm.then_inc(mop.sem, 1)

        @block.scalar
        def _(sc):
            sc.wait_ge(ldc.sem, 48)
            sc.wait_ge(ldm.sem, 1)
            for b in range(B):
                for kq in range(NKQ):
                    qs = b * NKQ + kq
                    for kc in range(KCQ):
                        for qh in range(2):
                            ci = qs * CPQ + kc * 2 + qh
                            sc.wait_ge(msc.sem, ci + 1)
                            if qs >= 1:
                                if kc == 0 and qh == 0:
                                    sc.wait_ge(mcx.sem, qs * CXQ)
                                sc.wait_ge(msm.sem, (qs - 1) * CPQ + kc * 2 + qh + 1)
                            sc.activation(att[:, kc * CH + qh * 512: kc * CH + (qh + 1) * 512],
                                          ps_sc[ci % 2][:], AF.Exp,
                                          bias=neg32[:], scale=1.0 / 32.0).then_inc(ex.sem, 1)
                for qt in range(8):
                    u = b * 8 + qt
                    ln_scalar(sc, lnv, lns, u, u * LV + 6, rstd)

        @block.vector
        def _(v):
            v.wait_ge(mbv.sem, 2)
            v.tensor_copy(bvb[:], ps_out[:]).then_inc(ebv.sem, 1)
            for b in range(B):
                for kq in range(NKQ):
                    qs = b * NKQ + kq
                    for dc in range(8):
                        for qh in range(2):
                            gi = qs * CXQ + dc * 2 + qh
                            v.wait_ge(mcx.sem, gi + 1)
                            dst = ctxa[:, dc * CH + qh * 512: dc * CH + (qh + 1) * 512]
                            if kq == 0:
                                v.tensor_copy(dst, ps_cx[gi % 2][:]).then_inc(ecx.sem, 1)
                            else:
                                v.tensor_tensor(dst, dst, ps_cx[gi % 2][:], ALU.add).then_inc(ecx.sem, 1)
                            ecx.n += 1
                v.wait_ge(msm.sem, (b + 1) * CPB)
                v.tensor_copy(sums_sb[:], ps_sm[0:1, :]).then_inc(erc.sem, 1)
                v.wait_ge(erc.sem, 2 * b + 1)
                v.reciprocal(recip_sb[:], sums_sb[:]).then_inc(erc.sem, 1)
                if b > 0:
                    v.wait_ge(mop.sem, b * 8)
                for dc in range(8):
                    v.tensor_copy(ctxb[:, dc * CH:(dc + 1) * CH],
                                  ctxa[:, dc * CH:(dc + 1) * CH]).then_inc(ecb.sem, 1)
                v.wait_ge(mtr.sem, b + 1)
                v.tensor_copy(recip_t[:], ps_cx[0][:, 0:8]).then_inc(ert.sem, 1)
                v.wait_ge(ert.sem, b + 1)
                for qt in range(8):
                    u = b * 8 + qt
                    v.wait_ge(mop.sem, u + 1)
                    v.wait_ge(lxo[u % 2].sem, 16 * (u // 2 + 1))
                    v.wait_ge(lnv.sem, lnv.n)
                    v.scalar_tensor_tensor(radd[:], ps_out[:], recip_t[:, qt:qt + 1],
                                           bvb[:], ALU.mult, ALU.add).then_inc(lnv.sem, 1)
                    lnv.n += 1
                    v.wait_ge(lnv.sem, u * LV + 1)
                    v.tensor_tensor(radd[:], radd[:], xo_t[u % 2][:], ALU.add).then_inc(lnv.sem, 1)
                    lnv.n += 1
                    v.wait_ge(lnv.sem, u * LV + 2)
                    if u >= 2:
                        v.wait_ge(st[u % 2].sem, 16 * (u // 2))
                    ln_vec(v, lnv, lns, u, 0, 0, radd, r1_t[u % 2][:], bn6, bn2, rstd, EPS0)

    return nc



GC1 = 0.7978845608028654
GC2 = GC1 * 0.044715


def build_l0_ffn(nc=None, sctx=None, pre="", io=None):
    if nc is None:
        nc = bass.Bass("TRN2", target_bir_lowering=False, debug=False)
    if io is None:
        io = dict(
            r1=nc.dram_tensor("r1", [NT, D], F32, kind="ExternalInput").ap(),
            U1=nc.dram_tensor("U1", [D, 256], F32, kind="ExternalInput").ap(),
            V1=nc.dram_tensor("V1", [256, 4096], F32, kind="ExternalInput").ap(),
            b1_r=nc.dram_tensor("b1_r", [1, 4096], F32, kind="ExternalInput").ap(),
            U2=nc.dram_tensor("U2", [4096, 256], F32, kind="ExternalInput").ap(),
            V2=nc.dram_tensor("V2", [256, D], F32, kind="ExternalInput").ap(),
            b2_r=nc.dram_tensor("b2_r", [1, D], F32, kind="ExternalInput").ap(),
            ones_m=nc.dram_tensor("ones_m", [128, 128], F32, kind="ExternalInput").ap(),
            ident_f=nc.dram_tensor("ident_f", [128, 128], F32, kind="ExternalInput").ap(),
            r2=nc.dram_tensor("r2", [NT, D], F32, kind="ExternalOutput").ap())
    r1_i, U1, V1, b1_r, U2, V2 = io["r1"], io["U1"], io["V1"], io["b1_r"], io["U2"], io["V2"]
    b2_r, ones_m, ident_f, r2_o = io["b2_r"], io["ones_m"], io["ident_f"], io["r2"]

    NG = 8
    with ExitStack() as ctx:
        cc = sctx if sctx is not None else ctx
        ec_ = ctx.enter_context
        r1g = [ec_(nc.sbuf_tensor(pre + f"r1g{i}", [128, 4 * D], BF16)) for i in range(2)]
        rT = [ec_(nc.sbuf_tensor(pre + f"rT{i}", [128, 8 * 512], BF16)) for i in range(2)]
        u1 = ec_(nc.sbuf_tensor(pre + "u1", [128, 8 * 256], BF16))
        v1 = ec_(nc.sbuf_tensor(pre + "v1", [128, 2 * 4096], BF16))
        u2 = ec_(nc.sbuf_tensor(pre + "u2", [128, 32 * 256], BF16))
        v2 = ec_(nc.sbuf_tensor(pre + "v2", [128, 2 * D], BF16))
        b1row = ec_(nc.sbuf_tensor(pre + "b1row", [1, 4096], BF16))
        ones_row = ec_(nc.sbuf_tensor(pre + "ones_row", [1, 512], BF16))
        b2_bf = ec_(nc.sbuf_tensor(pre + "b2_bf", [1, D], BF16))
        id_bf = ec_(nc.sbuf_tensor(pre + "id_bf", [128, 128], BF16))
        ones_bf = ec_(nc.sbuf_tensor(pre + "ones_bf", [128, 128], BF16))
        pT = ec_(nc.sbuf_tensor(pre + "pT", [128, 2 * 512], BF16))
        h2 = ec_(nc.sbuf_tensor(pre + "h2", [128, 32 * 512], BF16))
        q2T = ec_(nc.sbuf_tensor(pre + "q2T", [128, 2 * 512], BF16))
        zb = [ec_(nc.sbuf_tensor(pre + f"zb{i}", [128, 1024], BF16)) for i in range(2)]
        zsq = [ec_(nc.sbuf_tensor(pre + f"zsq{i}", [128, 1024], BF16)) for i in range(2)]
        um = [ec_(nc.sbuf_tensor(pre + f"um{i}", [128, 1024], BF16)) for i in range(2)]
        th = [ec_(nc.sbuf_tensor(pre + f"th{i}", [128, 1024], BF16)) for i in range(2)]
        r1f = [ec_(nc.sbuf_tensor(pre + f"r1f{i}", [128, D], F32)) for i in range(2)]
        o2b = ec_(nc.sbuf_tensor(pre + "o2b", [128, 4 * D], F32))
        radd = ec_(nc.sbuf_tensor(pre + "radd", [128, D], F32))
        r2t = [ec_(nc.sbuf_tensor(pre + f"r2t{i}", [128, D], F32)) for i in range(2)]
        bn6 = ec_(nc.sbuf_tensor(pre + "bn6", [128, 12], F32))
        bn2 = ec_(nc.sbuf_tensor(pre + "bn2", [128, 2], F32))
        rstd = ec_(nc.sbuf_tensor(pre + "rstd", [128, 1], F32))

        ps_t = ec_(nc.psum_tensor(pre + "ps_t", [128, 512], BF16))
        ps_z = [ec_(nc.psum_tensor(pre + f"ps_z{i}", [128, 1024], F32)) for i in range(2)]
        ps_o = ec_(nc.psum_tensor(pre + "ps_o", [128, 1024], F32))

        ldc = Chan(cc, nc, pre + "ldc")
        ldm = Chan(cc, nc, pre + "ldm")
        ldr = [Chan(cc, nc, pre + f"ldr{i}") for i in range(2)]
        lrf = [Chan(cc, nc, pre + f"lrf{i}") for i in range(2)]
        mtp = Chan(cc, nc, pre + "mtp")
        mP = Chan(cc, nc, pre + "mP")
        mz = Chan(cc, nc, pre + "mz")
        mq = Chan(cc, nc, pre + "mq")
        mo = Chan(cc, nc, pre + "mo")
        etp = Chan(cc, nc, pre + "etp")
        evP = Chan(cc, nc, pre + "evP")
        gv = Chan(cc, nc, pre + "gv")
        eo = Chan(cc, nc, pre + "eo")
        gs = Chan(cc, nc, pre + "gs")
        eq = Chan(cc, nc, pre + "eq")
        lnv = Chan(cc, nc, pre + "lnv")
        lns = Chan(cc, nc, pre + "lns")
        st = [Chan(cc, nc, pre + f"st{i}") for i in range(2)]
        block = ec_(nc.Block())

        LV = 6
        NSP = 16
        NSPT = NG * NSP

        def gsB(s):
            g, sp = s // NSP, s % NSP
            return 3 * NSP * g + (0 if sp == 0 else 3 * sp - 1)

        def th_cnt(s):
            g, sp = s // NSP, s % NSP
            return 3 * NSP * g + (3 * sp + 5 if sp < NSP - 1 else 3 * NSP)

        @block.gpsimd
        def _(g_):
            g_.dma_start(out=u1[:].rearrange("p (c f) -> p c f", f=256),
                         in_=U1.rearrange("(c p) f -> p c f", p=128)).then_inc(ldc.sem, 16)
            g_.dma_start(out=v1[:].rearrange("p (c f) -> p c f", f=4096),
                         in_=V1.rearrange("(c p) f -> p c f", p=128)).then_inc(ldc.sem, 16)
            g_.dma_start(out=u2[:].rearrange("p (c f) -> p c f", f=256),
                         in_=U2.rearrange("(c p) f -> p c f", p=128)).then_inc(ldc.sem, 16)
            g_.dma_start(out=v2[:].rearrange("p (c f) -> p c f", f=D),
                         in_=V2.rearrange("(c p) f -> p c f", p=128)).then_inc(ldc.sem, 16)
            g_.dma_start(out=b1row[:], in_=b1_r).then_inc(ldc.sem, 16)
            g_.dma_start(out=b2_bf[:], in_=b2_r).then_inc(ldc.sem, 16)
            g_.dma_start(out=id_bf[:], in_=ident_f).then_inc(ldc.sem, 16)
            g_.dma_start(out=ones_bf[:], in_=ones_m).then_inc(ldc.sem, 16)
            g_.memset(ones_row[:], 1.0).then_inc(ldm.sem, 1)
            for g in range(NG):
                if g >= 2:
                    g_.wait_ge(mtp.sem, 8 * (g - 1))
                g_.dma_start(
                    out=r1g[g % 2][:].rearrange("p (c d) -> p c d", d=D),
                    in_=r1_i[g * 512:(g + 1) * 512, :].rearrange("(c p) d -> p c d", p=128),
                ).then_inc(ldr[g % 2].sem, 16)
        LDC = 8 * 16

        @block.sync
        def _(sy):
            for g in range(NG):
                for ti in range(4):
                    u = g * 4 + ti
                    if u >= 2:
                        sy.wait_ge(lnv.sem, (u - 2) * LV + 1)
                    sy.dma_start(out=r1f[u % 2][:],
                                 in_=r1_i[u * 128:(u + 1) * 128, :]).then_inc(lrf[u % 2].sem, 16)
                for ti in range(4):
                    u = g * 4 + ti
                    sy.wait_ge(lnv.sem, (u + 1) * LV)
                    sy.dma_start(out=r2_o[u * 128:(u + 1) * 128, :],
                                 in_=r2t[u % 2][:]).then_inc(st[u % 2].sem, 16)
            sy.wait_ge(st[0].sem, 16 * 16)
            sy.wait_ge(st[1].sem, 16 * 16)

        @block.tensor
        def _(t):
            t.wait_ge(ldc.sem, LDC)
            t.wait_ge(ldm.sem, 1)
            for g in range(NG):
                gb = rT[g % 2]
                t.wait_ge(ldr[g % 2].sem, 16 * (g // 2 + 1))
                for dc in range(8):
                    for ti in range(4):
                        if g >= 1 or dc >= 1:
                            t.wait_ge(etp.sem, g * 8 + dc)
                        tr = t.transpose(
                            ps_t[:, ti * 128:(ti + 1) * 128],
                            r1g[g % 2][:, ti * D + dc * 128: ti * D + (dc + 1) * 128],
                            id_bf[:])
                    tr.then_inc(mtp.sem, 1)
                t.wait_ge(etp.sem, (g + 1) * 8)
                if g > 0:
                    t.wait_ge(gv.sem, g * NSP * 3)
                for rc in range(2):
                    i = g * 2 + rc
                    if i >= 2:
                        t.wait_ge(eq.sem, i - 1)
                    for dc in range(8):
                        mm = t.matmul(ps_z[rc][:, 0:512],
                                      u1[:, dc * 256 + rc * 128: dc * 256 + (rc + 1) * 128],
                                      gb[:, dc * 512:(dc + 1) * 512],
                                      start=(dc == 0), stop=(dc == 7))
                    mm.then_inc(mP.sem, 1)
                t.wait_ge(evP.sem, (g + 1) * 2)
                for sp in range(NSP):
                    s = g * NSP + sp
                    if sp >= 2:
                        t.wait_ge(gs.sem, gsB(s - 2) + 1)
                    for sub in range(2):
                        fc = sp * 2 + sub
                        t.matmul(ps_z[sp % 2][:, sub * 512:(sub + 1) * 512],
                                 b1row[0:1, fc * 128:(fc + 1) * 128],
                                 ones_row[0:1, :], start=True, stop=False)
                        for rc in range(2):
                            mm = t.matmul(ps_z[sp % 2][:, sub * 512:(sub + 1) * 512],
                                          v1[:, rc * 4096 + fc * 128: rc * 4096 + (fc + 1) * 128],
                                          pT[:, rc * 512:(rc + 1) * 512],
                                          start=False, stop=(rc == 1))
                    mm.then_inc(mz.sem, 1)
                t.wait_ge(gv.sem, (g + 1) * NSP * 3)
                for rc in range(2):
                    i = g * 2 + rc
                    if i >= 2:
                        t.wait_ge(eq.sem, i - 1)
                    for fc in range(32):
                        mm = t.matmul(ps_z[rc][:, 0:512],
                                      u2[:, fc * 256 + rc * 128: fc * 256 + (rc + 1) * 128],
                                      h2[:, fc * 512:(fc + 1) * 512],
                                      start=(fc == 0), stop=(fc == 31))
                    mm.then_inc(mq.sem, 1)
                t.wait_ge(eq.sem, (g + 1) * 2)
                for ti in range(4):
                    u = g * 4 + ti
                    if u >= 1:
                        t.wait_ge(eo.sem, u)
                    for nh in range(2):
                        t.matmul(ps_o[:, nh * 512:(nh + 1) * 512], ones_bf[0:1, :],
                                 b2_bf[:, nh * 512:(nh + 1) * 512], start=True, stop=False)
                    for rc in range(2):
                        for nh in range(2):
                            mm = t.matmul(
                                ps_o[:, nh * 512:(nh + 1) * 512],
                                q2T[:, rc * 512 + ti * 128: rc * 512 + (ti + 1) * 128],
                                v2[:, rc * D + nh * 512: rc * D + (nh + 1) * 512],
                                start=False, stop=(rc == 1))
                    mm.then_inc(mo.sem, 1)

        @block.scalar
        def _(sc):
            for g in range(NG + 1):
                if g >= 1:
                    gp = g - 1
                    for rc in range(2):
                        i = gp * 2 + rc
                        sc.wait_ge(mq.sem, i + 1)
                        sc.mul(q2T[:, rc * 512:(rc + 1) * 512], ps_z[rc][:, 0:512],
                               0.5).then_inc(eq.sem, 1)
                if g < NG:
                    for sp in range(NSP):
                        s = g * NSP + sp
                        sc.wait_ge(mz.sem, s + 1)
                        if s >= 2:
                            sc.wait_ge(gv.sem, (s - 1) * 3)
                        sc.copy(zb[s % 2][:], ps_z[sp % 2][:]).then_inc(gs.sem, 1)
                        gs.n += 1
                        sc.wait_ge(gs.sem, gsB(s) + 1)
                        sc.activation(zsq[s % 2][:], zb[s % 2][:], AF.Square).then_inc(gs.sem, 1)
                        gs.n += 1
                        if sp >= 1:
                            sc.wait_ge(gv.sem, (s - 1) * 3 + 2)
                            sc.activation(th[(s - 1) % 2][:], um[(s - 1) % 2][:],
                                          AF.Tanh).then_inc(gs.sem, 1)
                            gs.n += 1
                    s = g * NSP + NSP - 1
                    sc.wait_ge(gv.sem, s * 3 + 2)
                    sc.activation(th[s % 2][:], um[s % 2][:], AF.Tanh).then_inc(gs.sem, 1)
                    gs.n += 1
                if g >= 1:
                    gp = g - 1
                    for ti in range(4):
                        u = gp * 4 + ti
                        ln_scalar(sc, lnv, lns, u, u * LV + 5, rstd)

        @block.vector
        def _(v):
            for g in range(NG + 1):
                if g >= 1:
                    for ti in range(4):
                        u = (g - 1) * 4 + ti
                        v.wait_ge(mo.sem, u + 1)
                        v.tensor_copy(o2b[:, ti * D:(ti + 1) * D], ps_o[:]).then_inc(eo.sem, 1)
                        eo.n += 1
                if g < NG:
                    if g >= 2:
                        v.wait_ge(mP.sem, (g - 1) * 2)
                    for dc in range(8):
                        v.wait_ge(mtp.sem, g * 8 + dc + 1)
                        v.tensor_copy(rT[g % 2][:, dc * 512:(dc + 1) * 512],
                                      ps_t[:]).then_inc(etp.sem, 1)
                        etp.n += 1
                    for rc in range(2):
                        v.wait_ge(mP.sem, g * 2 + rc + 1)
                        v.tensor_copy(pT[:, rc * 512:(rc + 1) * 512],
                                      ps_z[rc][:, 0:512]).then_inc(evP.sem, 1)
                    for sp in range(NSP):
                        s = g * NSP + sp
                        fc0 = sp * 2
                        v.wait_ge(gs.sem, gsB(s) + 2)
                        v.tensor_scalar(um[s % 2][:], zsq[s % 2][:], GC2, GC1,
                                        ALU.mult, ALU.add).then_inc(gv.sem, 1)
                        gv.n += 1
                        v.wait_ge(gv.sem, s * 3 + 1)
                        v.tensor_tensor(um[s % 2][:], um[s % 2][:], zb[s % 2][:],
                                        ALU.mult).then_inc(gv.sem, 1)
                        gv.n += 1
                        v.wait_ge(gs.sem, th_cnt(s))
                        v.scalar_tensor_tensor(h2[:, fc0 * 512:(fc0 + 2) * 512],
                                               th[s % 2][:], 1.0, zb[s % 2][:],
                                               ALU.add, ALU.mult).then_inc(gv.sem, 1)
                        gv.n += 1
                if g >= 1:
                    gp = g - 1
                    for ti in range(4):
                        u = gp * 4 + ti
                        v.wait_ge(lrf[u % 2].sem, 16 * (u // 2 + 1))
                        v.wait_ge(lnv.sem, lnv.n)
                        v.tensor_tensor(radd[:], o2b[:, ti * D:(ti + 1) * D], r1f[u % 2][:],
                                        ALU.add).then_inc(lnv.sem, 1)
                        lnv.n += 1
                        v.wait_ge(lnv.sem, u * LV + 1)
                        if u >= 2:
                            v.wait_ge(st[u % 2].sem, 16 * (u // 2))
                        ln_vec(v, lnv, lns, u, 0, 0, radd, r2t[u % 2][:], bn6, bn2, rstd, EPS0)

    return nc



def build_l1_attn(nc=None, sctx=None, pre="", io=None):
    if nc is None:
        nc = bass.Bass("TRN2", target_bir_lowering=False, debug=False)
    if io is None:
        io = dict(
            r2=nc.dram_tensor("r2", [NT, D], F32, kind="ExternalInput").ap(),
            qkv_w=nc.dram_tensor("qkv_w", [D, 3 * D], F32, kind="ExternalInput").ap(),
            lnsc_t=nc.dram_tensor("lnsc_t", [128, 8], F32, kind="ExternalInput").ap(),
            lnb_t=nc.dram_tensor("lnb_t", [128, 8], F32, kind="ExternalInput").ap(),
            Wo=nc.dram_tensor("Wo", [D, D], F32, kind="ExternalInput").ap(),
            ones_m=nc.dram_tensor("ones_m", [128, 128], F32, kind="ExternalInput").ap(),
            ident_f=nc.dram_tensor("ident_f", [128, 128], F32, kind="ExternalInput").ap(),
            r3=nc.dram_tensor("r3", [NT, D], F32, kind="ExternalOutput").ap())
    r2_i, qkv_w, lnsc_t, lnb_t = io["r2"], io["qkv_w"], io["lnsc_t"], io["lnb_t"]
    Wo, ones_m, ident_f, r3_o = io["Wo"], io["ones_m"], io["ident_f"], io["r3"]

    EPS1 = 1e-6
    PAIRS = [(i, j) for i in range(4) for j in range(i + 1)]
    VMX_G = 52

    with ExitStack() as ctx:
        cc = sctx if sctx is not None else ctx
        ec_ = ctx.enter_context
        tng = [ec_(nc.sbuf_tensor(pre + f"tng{i}", [128, 4 * D], BF16)) for i in range(2)]
        wq3 = ec_(nc.sbuf_tensor(pre + "wq3", [128, 8 * 3 * D], BF16))
        wo = ec_(nc.sbuf_tensor(pre + "wo", [128, 8 * D], BF16))
        qkv_g = [ec_(nc.sbuf_tensor(pre + f"qkv_g{i}", [128, 4 * 3 * D], BF16)) for i in range(2)]
        tnT = ec_(nc.sbuf_tensor(pre + "tnT", [128, 8 * 512], BF16))
        cT = ec_(nc.sbuf_tensor(pre + "cT", [128, 8 * 512], BF16))
        ctxg = [ec_(nc.sbuf_tensor(pre + f"ctxg{i}", [128, 4 * D], BF16)) for i in range(2)]
        prodb = [ec_(nc.sbuf_tensor(pre + f"prodb{i}", [128, D], BF16)) for i in range(2)]
        tmpv = ec_(nc.sbuf_tensor(pre + "tmpv", [128, D], BF16))
        lgs = ec_(nc.sbuf_tensor(pre + "lgs", [128, 160], F32))
        egs = ec_(nc.sbuf_tensor(pre + "egs", [128, 160], F32))
        su = ec_(nc.sbuf_tensor(pre + "su", [128, 48], F32))
        rec = ec_(nc.sbuf_tensor(pre + "rec", [128, 48], F32))
        atn = ec_(nc.sbuf_tensor(pre + "atn", [128, 144], BF16))
        lnscv = ec_(nc.sbuf_tensor(pre + "lnscv", [128, 8], F32))
        lnbv = ec_(nc.sbuf_tensor(pre + "lnbv", [128, 8], BF16))
        bqkv = ec_(nc.sbuf_tensor(pre + "bqkv", [1, 3 * D], BF16))
        id_bf = ec_(nc.sbuf_tensor(pre + "id_bf", [128, 128], BF16))
        ones_bf = ec_(nc.sbuf_tensor(pre + "ones_bf", [128, 128], BF16))
        r2f = [ec_(nc.sbuf_tensor(pre + f"r2f{i}", [128, D], F32)) for i in range(2)]
        r2e = [ec_(nc.sbuf_tensor(pre + f"r2e{i}", [128, D], F32)) for i in range(2)]
        radd = ec_(nc.sbuf_tensor(pre + "radd", [128, D], F32))
        r3t = [ec_(nc.sbuf_tensor(pre + f"r3t{i}", [128, D], F32)) for i in range(2)]
        bn6 = ec_(nc.sbuf_tensor(pre + "bn6", [128, 12], F32))
        bn2 = ec_(nc.sbuf_tensor(pre + "bn2", [128, 2], F32))
        rstd = ec_(nc.sbuf_tensor(pre + "rstd", [128, 1], F32))

        ps_t = ec_(nc.psum_tensor(pre + "ps_t", [128, 512], BF16))
        ps_tc = ec_(nc.psum_tensor(pre + "ps_tc", [128, 512], BF16))
        pq = [ec_(nc.psum_tensor(pre + f"pq{i}", [128, 512], F32)) for i in range(2)]
        po = ec_(nc.psum_tensor(pre + "po", [128, 1024], F32))
        ps_b = ec_(nc.psum_tensor(pre + "ps_b", [1, 512], F32))

        ldw = Chan(cc, nc, pre + "ldw")
        lr2a = [Chan(cc, nc, pre + f"lr2a{i}") for i in range(2)]
        lr2e = [Chan(cc, nc, pre + f"lr2e{i}") for i in range(2)]
        mbq = Chan(cc, nc, pre + "mbq")
        ebq = Chan(cc, nc, pre + "ebq")
        evw = Chan(cc, nc, pre + "evw")
        mtp = Chan(cc, nc, pre + "mtp")
        etp = Chan(cc, nc, pre + "etp")
        mq = Chan(cc, nc, pre + "mq")
        eqk = Chan(cc, nc, pre + "eqk")
        vmx = Chan(cc, nc, pre + "vmx")
        smx = Chan(cc, nc, pre + "smx")
        mtc = Chan(cc, nc, pre + "mtc")
        etc_ = Chan(cc, nc, pre + "etc")
        mpo = Chan(cc, nc, pre + "mpo")
        lnv = Chan(cc, nc, pre + "lnv")
        lns = Chan(cc, nc, pre + "lns")
        st = [Chan(cc, nc, pre + f"st{i}") for i in range(2)]
        block = ec_(nc.Block())

        def lnvA(a):
            sj, b = a // 4, a % 4
            return 20 * sj + 24 * max(0, sj - 1) + 5 * b

        def lnvB(k):
            sj, ti = k // 4, k % 4
            return 20 * min(sj + 2, 8) + 24 * sj + 6 * ti

        def lnsA(a):
            sj, b = a // 4, a % 4
            return 12 * sj + 12 * max(0, sj - 1) + 3 * b

        def lnsB(k):
            sj, ti = k // 4, k % 4
            return 12 * min(sj + 2, 8) + 12 * sj + 3 * ti

        def a_tile(a):
            return (a % 4) * 8 + a // 4

        @block.gpsimd
        def _(g_):
            g_.dma_start(out=wq3[:].rearrange("p (c f) -> p c f", f=3 * D),
                         in_=qkv_w.rearrange("(c p) f -> p c f", p=128)).then_inc(ldw.sem, 16)
            g_.dma_start(out=wo[:].rearrange("p (c f) -> p c f", f=D),
                         in_=Wo.rearrange("(c p) f -> p c f", p=128)).then_inc(ldw.sem, 16)
            g_.dma_start(out=lnscv[:], in_=lnsc_t).then_inc(ldw.sem, 16)
            g_.dma_start(out=lnbv[:], in_=lnb_t).then_inc(ldw.sem, 16)
            g_.dma_start(out=id_bf[:], in_=ident_f).then_inc(ldw.sem, 16)
            g_.dma_start(out=ones_bf[:], in_=ones_m).then_inc(ldw.sem, 16)
        LDW = 96

        @block.sync
        def _(sy):
            for sj in range(9):
                if sj < 8:
                    for b in range(4):
                        a = sj * 4 + b
                        if a >= 2:
                            sy.wait_ge(lnv.sem, lnvA(a - 2) + 5)
                        sy.dma_start(out=r2f[a % 2][:],
                                     in_=r2_i[a_tile(a) * 128:(a_tile(a) + 1) * 128, :]
                                     ).then_inc(lr2a[a % 2].sem, 16)
                if sj >= 1:
                    mj = sj - 1
                    for ti in range(4):
                        u = ti * 8 + mj
                        k = mj * 4 + ti
                        if k >= 2:
                            sy.wait_ge(lnv.sem, lnvB(k - 2) + 1)
                        sy.dma_start(out=r2e[k % 2][:],
                                     in_=r2_i[u * 128:(u + 1) * 128, :]).then_inc(lr2e[k % 2].sem, 16)
                    for ti in range(4):
                        u = ti * 8 + mj
                        k = mj * 4 + ti
                        sy.wait_ge(lnv.sem, lnvB(k) + 6)
                        sy.dma_start(out=r3_o[u * 128:(u + 1) * 128, :],
                                     in_=r3t[k % 2][:]).then_inc(st[k % 2].sem, 16)
            sy.wait_ge(st[0].sem, 16 * 16)
            sy.wait_ge(st[1].sem, 16 * 16)

        @block.tensor
        def _(t):
            t.wait_ge(ldw.sem, LDW)
            for ch in range(6):
                for dc in range(8):
                    mm = t.matmul(ps_b[0:1, :], lnbv[:, dc:dc + 1],
                                  wq3[:, dc * 3 * D + ch * 512: dc * 3 * D + (ch + 1) * 512],
                                  start=(dc == 0), stop=(dc == 7))
                mm.then_inc(mbq.sem, 1)
                t.wait_ge(ebq.sem, ch + 1)
            for sj in range(9):
                if sj < 8:
                    t.wait_ge(lnv.sem, lnvA(sj * 4 + 3) + 5)
                    for dc in range(8):
                        pk = sj * 8 + dc
                        if pk >= 1:
                            t.wait_ge(etp.sem, pk)
                        for ti in range(4):
                            tr = t.transpose(
                                ps_t[:, ti * 128:(ti + 1) * 128],
                                tng[sj % 2][:, ti * D + dc * 128: ti * D + (dc + 1) * 128],
                                id_bf[:])
                        tr.then_inc(mtp.sem, 1)
                    t.wait_ge(etp.sem, (sj + 1) * 8)
                    t.wait_ge(evw.sem, 8)
                    if sj >= 2:
                        t.wait_ge(vmx.sem, (sj - 1) * VMX_G)
                    for ti in range(4):
                        for ch in range(6):
                            k = sj * 24 + ti * 6 + ch
                            if k >= 2:
                                t.wait_ge(eqk.sem, k - 1)
                            p = pq[k % 2]
                            t.matmul(p[:], ones_bf[0:1, :],
                                     bqkv[:, ch * 512:(ch + 1) * 512], start=True, stop=False)
                            for dc in range(8):
                                mm = t.matmul(
                                    p[:],
                                    tnT[:, dc * 512 + ti * 128: dc * 512 + (ti + 1) * 128],
                                    wq3[:, dc * 3 * D + ch * 512: dc * 3 * D + (ch + 1) * 512],
                                    start=False, stop=(dc == 7))
                            mm.then_inc(mq.sem, 1)
                if sj >= 1:
                    mj = sj - 1
                    t.wait_ge(vmx.sem, (mj + 1) * VMX_G)
                    for ec2 in range(8):
                        pk = mj * 8 + ec2
                        if pk >= 1:
                            t.wait_ge(etc_.sem, pk)
                        for ti in range(4):
                            tr = t.transpose(
                                ps_tc[:, ti * 128:(ti + 1) * 128],
                                ctxg[mj % 2][:, ti * D + ec2 * 128: ti * D + (ec2 + 1) * 128],
                                id_bf[:])
                        tr.then_inc(mtc.sem, 1)
                    t.wait_ge(etc_.sem, (mj + 1) * 8)
                    for ti in range(4):
                        k = mj * 4 + ti
                        if k > 0:
                            t.wait_ge(lnv.sem, lnvB(k - 1) + 1)
                        for ec2 in range(8):
                            for nh in range(2):
                                mm = t.matmul(
                                    po[:, nh * 512:(nh + 1) * 512],
                                    cT[:, ec2 * 512 + ti * 128: ec2 * 512 + (ti + 1) * 128],
                                    wo[:, ec2 * D + nh * 512: ec2 * D + (nh + 1) * 512],
                                    start=(ec2 == 0), stop=(ec2 == 7))
                        mm.then_inc(mpo.sem, 1)

        @block.scalar
        def _(sc):
            for sj in range(9):
                if sj < 8:
                    for b in range(4):
                        a = sj * 4 + b
                        ln_scalar(sc, lnv, lns, a, lnvA(a) + 4, rstd)
                    for ti in range(4):
                        for ch in range(6):
                            k = sj * 24 + ti * 6 + ch
                            sc.wait_ge(mq.sem, k + 1)
                            sc.copy(qkv_g[sj % 2][:, ti * 3 * D + ch * 512: ti * 3 * D + (ch + 1) * 512],
                                    pq[k % 2][:]).then_inc(eqk.sem, 1)
                            eqk.n += 1
                if sj >= 1:
                    mj = sj - 1
                    sc.wait_ge(vmx.sem, mj * VMX_G + 20)
                    sc.activation(egs[:], lgs[:], AF.Exp, scale=0.125).then_inc(smx.sem, 1)
                    smx.n += 1
                    for ti in range(4):
                        k = mj * 4 + ti
                        ln_scalar(sc, lnv, lns, k, lnvB(k) + 1 + 4, rstd)

        @block.vector
        def _(v):
            for ch in range(6):
                v.wait_ge(mbq.sem, ch + 1)
                v.tensor_copy(bqkv[:, ch * 512:(ch + 1) * 512], ps_b[0:1, :]).then_inc(ebq.sem, 1)
                ebq.n += 1
            v.wait_ge(mbq.sem, 6)
            for dc in range(8):
                v.wait_ge(evw.sem, dc)
                v.tensor_scalar(wq3[:, dc * 3 * D:(dc + 1) * 3 * D],
                                wq3[:, dc * 3 * D:(dc + 1) * 3 * D],
                                lnscv[:, dc:dc + 1], None, ALU.mult).then_inc(evw.sem, 1)
                evw.n += 1

            PIX = {(i, j): pi for pi, (i, j) in enumerate(PAIRS)}

            for sj in range(9):
                if sj < 8:
                    if sj >= 2:
                        v.wait_ge(mtp.sem, 8 * (sj - 1))
                    for b in range(4):
                        a = sj * 4 + b
                        v.wait_ge(lr2a[a % 2].sem, 16 * (a // 2 + 1))
                        ln_vec(v, lnv, lns, 0, 0, lnsA(a), r2f[a % 2],
                               tng[sj % 2][:, b * D:(b + 1) * D], bn6, bn2, rstd, EPS1)
                    for dc in range(8):
                        v.wait_ge(mtp.sem, sj * 8 + dc + 1)
                        v.tensor_copy(tnT[:, dc * 512:(dc + 1) * 512], ps_t[:]).then_inc(etp.sem, 1)
                        etp.n += 1
                if sj >= 1:
                    mj = sj - 1
                    v.wait_ge(eqk.sem, (mj + 1) * 24)
                    if mj >= 2:
                        v.wait_ge(mtc.sem, 8 * (mj - 1))

                    def q_(i):
                        return qkv_g[mj % 2][:, i * 3 * D + 0: i * 3 * D + D]

                    def k_(i):
                        return qkv_g[mj % 2][:, i * 3 * D + D: i * 3 * D + 2 * D]

                    def v_(i):
                        return qkv_g[mj % 2][:, i * 3 * D + 2 * D: i * 3 * D + 3 * D]

                    def vop(inst):
                        inst.then_inc(vmx.sem, 1)
                        vmx.n += 1
                        v.wait_ge(vmx.sem, vmx.n)

                    for pi, (i, j) in enumerate(PAIRS):
                        vop(v.tensor_tensor(prodb[pi % 2][:], q_(i), k_(j), ALU.mult))
                        vop(v.tensor_reduce(lgs[:, pi * 16:(pi + 1) * 16],
                                            prodb[pi % 2][:].rearrange("p (h d) -> p h d", d=64),
                                            AX.X, ALU.add))
                    v.wait_ge(smx.sem, mj + 1)

                    def eg(i, j):
                        p = PIX[(i, j)]
                        return egs[:, p * 16:(p + 1) * 16]

                    for i in range(1, 4):
                        s_sl = su[:, (i - 1) * 16: i * 16]
                        vop(v.tensor_tensor(s_sl, eg(i, 0), eg(i, 1), ALU.add))
                        for j in range(2, i + 1):
                            vop(v.tensor_tensor(s_sl, s_sl, eg(i, j), ALU.add))
                    vop(v.reciprocal(rec[:], su[:]))
                    for i in range(1, 4):
                        for j in range(i + 1):
                            p = PIX[(i, j)] - 1
                            vop(v.tensor_tensor(atn[:, p * 16:(p + 1) * 16], eg(i, j),
                                                rec[:, (i - 1) * 16: i * 16], ALU.mult))
                    vop(v.tensor_copy(ctxg[mj % 2][:, 0:D], v_(0)))
                    for i in range(1, 4):
                        dst3 = ctxg[mj % 2][:, i * D:(i + 1) * D].rearrange("p (h d) -> p h d", d=64)
                        for j in range(i + 1):
                            p = PIX[(i, j)] - 1
                            a3 = atn[:, p * 16:(p + 1) * 16].unsqueeze(2).broadcast_to([128, 16, 64])
                            v3 = v_(j).rearrange("p (h d) -> p h d", d=64)
                            if j == 0:
                                vop(v.tensor_tensor(dst3, v3, a3, ALU.mult))
                            else:
                                vop(v.tensor_tensor(tmpv[:].rearrange("p (h d) -> p h d", d=64),
                                                    v3, a3, ALU.mult))
                                vop(v.tensor_tensor(dst3, dst3,
                                                    tmpv[:].rearrange("p (h d) -> p h d", d=64),
                                                    ALU.add))
                    for ec2 in range(8):
                        v.wait_ge(mtc.sem, mj * 8 + ec2 + 1)
                        v.tensor_copy(cT[:, ec2 * 512:(ec2 + 1) * 512], ps_tc[:]).then_inc(etc_.sem, 1)
                        etc_.n += 1
                    for ti in range(4):
                        k = mj * 4 + ti
                        v.wait_ge(mpo.sem, k + 1)
                        v.wait_ge(lr2e[k % 2].sem, 16 * (k // 2 + 1))
                        v.wait_ge(lnv.sem, lnv.n)
                        v.tensor_tensor(radd[:], po[:], r2e[k % 2][:], ALU.add).then_inc(lnv.sem, 1)
                        lnv.n += 1
                        v.wait_ge(lnv.sem, lnvB(k) + 1)
                        if k >= 2:
                            v.wait_ge(st[k % 2].sem, 16 * (k // 2))
                        ln_vec(v, lnv, lns, 0, 0, lnsB(k), radd, r3t[k % 2][:], bn6, bn2, rstd, EPS0)

    return nc



def build_l1_ffn(nc=None, sctx=None, pre="", io=None):
    if nc is None:
        nc = bass.Bass("TRN2", target_bir_lowering=False, debug=False)
    if io is None:
        io = dict(
            r3=nc.dram_tensor("r3", [NT, D], F32, kind="ExternalInput").ap(),
            wgT=nc.dram_tensor("wgT", [D, 3072], I32, kind="ExternalInput").ap(),
            wuT=nc.dram_tensor("wuT", [D, 3072], I32, kind="ExternalInput").ap(),
            wdT=nc.dram_tensor("wdT", [3072, D], I32, kind="ExternalInput").ap(),
            ident_f=nc.dram_tensor("ident_f", [128, 128], F32, kind="ExternalInput").ap(),
            out=nc.dram_tensor("out", [NT, D], F32, kind="ExternalOutput").ap())
    r3_i, wgT_s, wuT_s, wdT_s = io["r3"], io["wgT"], io["wuT"], io["wdT"]
    ident_f, out_o = io["ident_f"], io["out"]

    NG = 16
    QD = 1.0 / 127.0
    with ExitStack() as ctx:
        cc = sctx if sctx is not None else ctx
        ec_ = ctx.enter_context
        wg3 = ec_(nc.sbuf_tensor(pre + "wg3", [128, 8 * 3072], BF16))
        wu3 = ec_(nc.sbuf_tensor(pre + "wu3", [128, 8 * 3072], BF16))
        wd3 = ec_(nc.sbuf_tensor(pre + "wd3", [128, 24 * D], BF16))
        id_bf = ec_(nc.sbuf_tensor(pre + "id_bf", [128, 128], BF16))
        r3g = [ec_(nc.sbuf_tensor(pre + f"r3g{i}", [128, 2 * D], BF16)) for i in range(2)]
        rT = [ec_(nc.sbuf_tensor(pre + f"rT{i}", [128, 8 * 256], BF16)) for i in range(2)]
        hh = ec_(nc.sbuf_tensor(pre + "hh", [128, 24 * 256], BF16))
        sg_sb = [ec_(nc.sbuf_tensor(pre + f"sg_sb{i}", [128, 512], BF16)) for i in range(2)]
        r3f = [ec_(nc.sbuf_tensor(pre + f"r3f{i}", [128, D], F32)) for i in range(2)]
        radd = ec_(nc.sbuf_tensor(pre + "radd", [128, D], F32))
        r4t = [ec_(nc.sbuf_tensor(pre + f"r4t{i}", [128, D], F32)) for i in range(2)]
        bn6 = ec_(nc.sbuf_tensor(pre + "bn6", [128, 12], F32))
        bn2 = ec_(nc.sbuf_tensor(pre + "bn2", [128, 2], F32))
        rstd = ec_(nc.sbuf_tensor(pre + "rstd", [128, 1], F32))

        ps_t = ec_(nc.psum_tensor(pre + "ps_t", [128, 256], BF16))
        ps_g = [ec_(nc.psum_tensor(pre + f"ps_g{i}", [128, 512], F32)) for i in range(2)]
        ps_u = [ec_(nc.psum_tensor(pre + f"ps_u{i}", [128, 512], F32)) for i in range(2)]
        po = ec_(nc.psum_tensor(pre + "po", [128, 1024], F32))

        ldw = Chan(cc, nc, pre + "ldw")
        lr3 = [Chan(cc, nc, pre + f"lr3{i}") for i in range(2)]
        lrf = [Chan(cc, nc, pre + f"lrf{i}") for i in range(2)]
        mtp = Chan(cc, nc, pre + "mtp")
        etp = Chan(cc, nc, pre + "etp")
        mg = Chan(cc, nc, pre + "mg")
        mu = Chan(cc, nc, pre + "mu")
        gsl = Chan(cc, nc, pre + "gsl")
        ehh = Chan(cc, nc, pre + "ehh")
        mdo = Chan(cc, nc, pre + "mdo")
        lnv = Chan(cc, nc, pre + "lnv")
        lns = Chan(cc, nc, pre + "lns")
        st = [Chan(cc, nc, pre + f"st{i}") for i in range(2)]
        block = ec_(nc.Block())
        LV = 6

        @block.gpsimd
        def _(g_):
            g_.dma_start(out=wg3[:].rearrange("p (c f) -> p c f", f=3072),
                         in_=wgT_s.rearrange("(c p) f -> p c f", p=128)).then_inc(ldw.sem, 16)
            g_.dma_start(out=wu3[:].rearrange("p (c f) -> p c f", f=3072),
                         in_=wuT_s.rearrange("(c p) f -> p c f", p=128)).then_inc(ldw.sem, 16)
            g_.dma_start(out=wd3[:].rearrange("p (c f) -> p c f", f=D),
                         in_=wdT_s.rearrange("(c p) f -> p c f", p=128)).then_inc(ldw.sem, 16)
            g_.dma_start(out=id_bf[:], in_=ident_f).then_inc(ldw.sem, 16)
            for g in range(NG):
                if g >= 2:
                    g_.wait_ge(mtp.sem, 8 * (g - 1))
                g_.dma_start(
                    out=r3g[g % 2][:].rearrange("p (c d) -> p c d", d=D),
                    in_=r3_i[g * 256:(g + 1) * 256, :].rearrange("(c p) d -> p c d", p=128),
                ).then_inc(lr3[g % 2].sem, 16)
        LDW = 64

        @block.sync
        def _(sy):
            for g in range(NG):
                for ti in range(2):
                    u = g * 2 + ti
                    if u >= 2:
                        sy.wait_ge(lnv.sem, (u - 2) * LV + 1)
                    sy.dma_start(out=r3f[u % 2][:],
                                 in_=r3_i[u * 128:(u + 1) * 128, :]).then_inc(lrf[u % 2].sem, 16)
                    sy.wait_ge(lnv.sem, (u + 1) * LV)
                    sy.dma_start(out=out_o[u * 128:(u + 1) * 128, :],
                                 in_=r4t[u % 2][:]).then_inc(st[u % 2].sem, 16)
            sy.wait_ge(st[0].sem, 16 * 16)
            sy.wait_ge(st[1].sem, 16 * 16)

        @block.tensor
        def _(t):
            t.wait_ge(ldw.sem, LDW)
            for g in range(NG):
                t.wait_ge(lr3[g % 2].sem, 16 * (g // 2 + 1))
                for dc in range(8):
                    pk = g * 8 + dc
                    if pk >= 1:
                        t.wait_ge(etp.sem, pk)
                    for ti in range(2):
                        tr = t.transpose(
                            ps_t[:, ti * 128:(ti + 1) * 128],
                            r3g[g % 2][:, ti * D + dc * 128: ti * D + (dc + 1) * 128],
                            id_bf[:])
                    tr.then_inc(mtp.sem, 1)
                t.wait_ge(etp.sem, (g + 1) * 8)
                for sp in range(12):
                    s = g * 12 + sp
                    if s >= 2:
                        t.wait_ge(gsl.sem, s - 1)
                    for sub in range(2):
                        hc = sp * 2 + sub
                        for dc in range(8):
                            mm = t.matmul(ps_g[s % 2][:, sub * 256:(sub + 1) * 256],
                                          wg3[:, dc * 3072 + hc * 128: dc * 3072 + (hc + 1) * 128],
                                          rT[g % 2][:, dc * 256:(dc + 1) * 256],
                                          start=(dc == 0), stop=(dc == 7))
                    mm.then_inc(mg.sem, 1)
                    if s >= 2:
                        t.wait_ge(ehh.sem, s - 1)
                    for sub in range(2):
                        hc = sp * 2 + sub
                        for dc in range(8):
                            mm = t.matmul(ps_u[s % 2][:, sub * 256:(sub + 1) * 256],
                                          wu3[:, dc * 3072 + hc * 128: dc * 3072 + (hc + 1) * 128],
                                          rT[g % 2][:, dc * 256:(dc + 1) * 256],
                                          start=(dc == 0), stop=(dc == 7))
                    mm.then_inc(mu.sem, 1)
                t.wait_ge(ehh.sem, (g + 1) * 12)
                for ti in range(2):
                    u = g * 2 + ti
                    if u >= 1:
                        t.wait_ge(lnv.sem, (u - 1) * LV + 1)
                    for hc in range(24):
                        for nh in range(2):
                            mm = t.matmul(
                                po[:, nh * 512:(nh + 1) * 512],
                                hh[:, hc * 256 + ti * 128: hc * 256 + (ti + 1) * 128],
                                wd3[:, hc * D + nh * 512: hc * D + (nh + 1) * 512],
                                start=(hc == 0), stop=(hc == 23))
                    mm.then_inc(mdo.sem, 1)

        @block.scalar
        def _(sc):
            for g in range(NG):
                for sp in range(12):
                    s = g * 12 + sp
                    sc.wait_ge(mg.sem, s + 1)
                    if s >= 2:
                        sc.wait_ge(ehh.sem, s - 1)
                    sc.activation(sg_sb[s % 2][:], ps_g[s % 2][:], AF.Silu,
                                  scale=QD).then_inc(gsl.sem, 1)
                    gsl.n += 1
                for ti in range(2):
                    u = g * 2 + ti
                    ln_scalar(sc, lnv, lns, u, u * LV + 5, rstd)

        @block.vector
        def _(v):
            for g in range(NG):
                if g >= 2:
                    v.wait_ge(mu.sem, 12 * (g - 1))
                for dc in range(8):
                    v.wait_ge(mtp.sem, g * 8 + dc + 1)
                    v.tensor_copy(rT[g % 2][:, dc * 256:(dc + 1) * 256], ps_t[:]).then_inc(etp.sem, 1)
                    etp.n += 1
                if g >= 1:
                    v.wait_ge(mdo.sem, 2 * g)
                for sp in range(12):
                    s = g * 12 + sp
                    v.wait_ge(mu.sem, s + 1)
                    v.wait_ge(gsl.sem, s + 1)
                    v.scalar_tensor_tensor(hh[:, sp * 512:(sp + 1) * 512], ps_u[s % 2][:],
                                           QD, sg_sb[s % 2][:], ALU.mult, ALU.mult).then_inc(ehh.sem, 1)
                    ehh.n += 1
                for ti in range(2):
                    u = g * 2 + ti
                    v.wait_ge(mdo.sem, u + 1)
                    v.wait_ge(lrf[u % 2].sem, 16 * (u // 2 + 1))
                    v.wait_ge(lnv.sem, lnv.n)
                    v.scalar_tensor_tensor(radd[:], po[:], QD, r3f[u % 2][:],
                                           ALU.mult, ALU.add).then_inc(lnv.sem, 1)
                    lnv.n += 1
                    v.wait_ge(lnv.sem, u * LV + 1)
                    if u >= 2:
                        v.wait_ge(st[u % 2].sem, 16 * (u // 2))
                    ln_vec(v, lnv, lns, u, 0, 0, radd, r4t[u % 2][:], bn6, bn2, rstd, EPS0)

    return nc



def build_fused():
    nc = bass.Bass("TRN2", target_bir_lowering=False, debug=False)

    def ein(name, shape, dt=F32):
        return nc.dram_tensor(name, shape, dt, kind="ExternalInput").ap()

    io = dict(
        x_full=ein("x_full", [B, S, D]),
        xT_full=ein("xT_full", [B, D, S]),
        xT_own=ein("xT_own", [B, D, CH]),
        x_own=ein("x_own", [B, CH, D]),
        Wq=ein("Wq", [D, D]),
        WkT=ein("WkT", [D, D]),
        bq_t=ein("bq_t", [128, 8]),
        Wv=ein("Wv", [D, D]),
        bv_r=ein("bv_r", [1, D]),
        ones_m=ein("ones_m", [128, 128]),
        ident_f=ein("ident_f", [128, 128]),
        U1=ein("U1", [D, 256]),
        V1=ein("V1", [256, 4096]),
        b1_r=ein("b1_r", [1, 4096]),
        U2=ein("U2", [4096, 256]),
        V2=ein("V2", [256, D]),
        b2_r=ein("b2_r", [1, D]),
        qkv_w=ein("qkv_w", [D, 3 * D]),
        lnsc_t=ein("lnsc_t", [128, 8]),
        lnb_t=ein("lnb_t", [128, 8]),
        Wo=ein("Wo", [D, D]),
        wgT=ein("wgT", [D, 3072], I32),
        wuT=ein("wuT", [D, 3072], I32),
        wdT=ein("wdT", [3072, D], I32),
        qpT=nc.dram_tensor("qpT", [D, NT], F32, kind="Internal").ap(),
        r1=nc.dram_tensor("r1", [NT, D], F32, kind="Internal").ap(),
        r2=nc.dram_tensor("r2", [NT, D], F32, kind="Internal").ap(),
        r3=nc.dram_tensor("r3", [NT, D], F32, kind="Internal").ap(),
        out=nc.dram_tensor("out", [NT, D], F32, kind="ExternalOutput").ap(),
    )
    with ExitStack() as sctx:
        build_l0_qkv(nc, sctx, "a_", io)
        build_l0_attn(nc, sctx, "b_", io)
        build_l0_ffn(nc, sctx, "c_", io)
        build_l1_attn(nc, sctx, "d_", io)
        build_l1_ffn(nc, sctx, "e_", io)
    return nc



def _c(a):
    return np.ascontiguousarray(a)


def kernel(
    L0_attn_x, L0_attn_Wq, L0_attn_bq, L0_attn_Wk, L0_attn_bk, L0_attn_Wv,
    L0_attn_bv, L0_ffn_U1, L0_ffn_V1, L0_ffn_b1, L0_ffn_U2, L0_ffn_V2,
    L0_ffn_b2, L1_attn_ln_scale, L1_attn_ln_bias, L1_attn_qkv_kernel,
    L1_attn_out_kernel, L1_ffn_w_gate, L1_ffn_w_up, L1_ffn_w_down,
):
    x = np.asarray(L0_attn_x, dtype=np.float32)
    xT = _c(x.transpose(0, 2, 1))
    ones_m = np.ones((128, 128), np.float32)
    ident_f = np.eye(128, dtype=np.float32)
    cores = list(range(NCORES))

    nc = build_fused()
    shared = dict(
        x_full=x, xT_full=xT,
        Wq=np.asarray(L0_attn_Wq, np.float32),
        WkT=_c(np.asarray(L0_attn_Wk, np.float32).T),
        bq_t=_c(np.asarray(L0_attn_bq, np.float32).reshape(8, 128).T),
        Wv=np.asarray(L0_attn_Wv, np.float32),
        bv_r=np.asarray(L0_attn_bv, np.float32).reshape(1, D),
        ones_m=ones_m, ident_f=ident_f,
        U1=np.asarray(L0_ffn_U1, np.float32),
        V1=np.asarray(L0_ffn_V1, np.float32),
        b1_r=np.asarray(L0_ffn_b1, np.float32).reshape(1, 4096),
        U2=np.asarray(L0_ffn_U2, np.float32),
        V2=np.asarray(L0_ffn_V2, np.float32),
        b2_r=np.asarray(L0_ffn_b2, np.float32).reshape(1, D),
        qkv_w=_c(np.asarray(L1_attn_qkv_kernel, np.float32).reshape(D, 3 * D)),
        lnsc_t=_c(np.asarray(L1_attn_ln_scale, np.float32).reshape(8, 128).T),
        lnb_t=_c(np.asarray(L1_attn_ln_bias, np.float32).reshape(8, 128).T),
        Wo=np.asarray(L1_attn_out_kernel, np.float32),
        wgT=_c(np.asarray(L1_ffn_w_gate, np.int32).T),
        wuT=_c(np.asarray(L1_ffn_w_up, np.int32).T),
        wdT=_c(np.asarray(L1_ffn_w_down, np.int32).T),
    )
    maps = [dict(shared,
                 xT_own=_c(xT[:, :, c * CH:(c + 1) * CH]),
                 x_own=_c(x[:, c * CH:(c + 1) * CH, :])) for c in cores]
    res = run_bass_kernel_spmd(nc, maps, core_ids=cores)

    full = np.empty((B, S, D), dtype=np.float32)
    for c in cores:
        oc = np.asarray(res.results[c]["out"]).reshape(B, CH, D)
        full[:, c * CH:(c + 1) * CH, :] = oc
    return full


# version 07
# speedup vs baseline: 1.1682x; 1.1682x over previous
import numpy as np
import ml_dtypes
from contextlib import ExitStack

import concourse.bass as bass
import concourse.mybir as mybir
from concourse.bass_utils import run_bass_kernel_spmd

F32 = mybir.dt.float32
BF16 = mybir.dt.bfloat16
I32 = mybir.dt.int32
AF = mybir.ActivationFunctionType
ALU = mybir.AluOpType
AX = mybir.AxisListType

NCORES = 8
B = 4
S = 8192
D = 1024
CH = S // NCORES
NT = B * CH
EPS0 = 1e-5
SQ2 = float(np.sqrt(2.0))



LN2H = float(0.5 * np.log(2.0))


def ln_vec(v, lnv, lns, lns_after, src, dst, bn6, bn2, rstd):
    v.bn_stats(bn6[:, 0:6], src[:, 0:512]).then_inc(lnv.sem, 1)
    v.bn_stats(bn6[:, 6:12], src[:, 512:1024]).then_inc(lnv.sem, 1)
    lnv.n += 2
    v.wait_ge(lnv.sem, lnv.n)
    v.bn_aggr(bn2[:], bn6[:]).then_inc(lnv.sem, 1)
    lnv.n += 1
    v.wait_ge(lns.sem, lns_after)
    v.tensor_scalar(dst, src[:], bn2[:, 0:1], rstd[:], ALU.subtract, ALU.mult).then_inc(lnv.sem, 1)
    lnv.n += 1


def ln_scalar(sc, lnv, lns, lnv_ready, rstd, bn2, epsb, l2h):
    sc.wait_ge(lnv.sem, lnv_ready)
    sc.activation(rstd[:], bn2[:, 1:2], AF.Ln, bias=epsb[:], scale=2.0).then_inc(lns.sem, 1)
    lns.n += 1
    sc.wait_ge(lns.sem, lns.n)
    sc.activation(rstd[:], rstd[:], AF.Exp, bias=l2h[:], scale=-0.5).then_inc(lns.sem, 1)
    lns.n += 1


class Chan:

    def __init__(self, ctx, nc, name):
        self.sem = ctx.enter_context(nc.semaphore(name))
        self.n = 0

    def inc(self, inst, k=1):
        inst.then_inc(self.sem, k)
        self.n += k
        return self.n



def build_l0_qkv(nc=None, sctx=None, pre="", io=None):
    if nc is None:
        nc = bass.Bass("TRN2", target_bir_lowering=False, debug=False)
    if io is None:
        io = dict(
            xT_own=nc.dram_tensor("xT_own", [B, D, CH], F32, kind="ExternalInput").ap(),
            Wq=nc.dram_tensor("Wq", [D, D], F32, kind="ExternalInput").ap(),
            WkT=nc.dram_tensor("WkT", [D, D], F32, kind="ExternalInput").ap(),
            bq_t=nc.dram_tensor("bq_t", [128, 8], F32, kind="ExternalInput").ap(),
            qpT=nc.dram_tensor("qpT", [D, NT], F32, kind="ExternalOutput").ap())
    xT_own, Wq, WkT, bq_t, qpT_o = io["xT_own"], io["Wq"], io["WkT"], io["bq_t"], io["qpT"]

    with ExitStack() as ctx:
        cc = sctx if sctx is not None else ctx
        ec_ = ctx.enter_context
        xTo = ec_(nc.sbuf_tensor(pre + "xTo", [128, 8 * NT], BF16))
        wq = ec_(nc.sbuf_tensor(pre + "wq", [128, 8 * D], BF16))
        wkt = ec_(nc.sbuf_tensor(pre + "wkt", [128, 8 * D], BF16))
        bqv = ec_(nc.sbuf_tensor(pre + "bqv", [128, 8], F32))
        qt_tg = ec_(nc.sbuf_tensor(pre + "qt_tg", [128, 8 * 1024], BF16))
        qp_sb = [ec_(nc.sbuf_tensor(pre + f"qp_sb{i}", [128, 1024], F32)) for i in range(2)]
        ps = [ec_(nc.psum_tensor(pre + f"ps{i}", [128, 1024], F32)) for i in range(2)]
        ld = Chan(cc, nc, pre + "ld")
        mmq = Chan(cc, nc, pre + "mmq")
        evq = Chan(cc, nc, pre + "evq")
        mmp = Chan(cc, nc, pre + "mmp")
        evp = Chan(cc, nc, pre + "evp")
        st = [Chan(cc, nc, pre + f"st{i}") for i in range(2)]
        block = ec_(nc.Block())

        @block.gpsimd
        def _(g):
            ld.inc(g.dma_start(out=wq[:].rearrange("p (c f) -> p c f", f=D),
                               in_=Wq.rearrange("(c p) f -> p c f", p=128)), 16)
            ld.inc(g.dma_start(out=wkt[:].rearrange("p (c f) -> p c f", f=D),
                               in_=WkT.rearrange("(c p) f -> p c f", p=128)), 16)
            ld.inc(g.dma_start(out=bqv[:], in_=bq_t), 16)
            for b in range(B):
                ld.inc(g.dma_start(
                    out=xTo[:].rearrange("p (c s) -> p c s", s=NT)[:, :, b * CH:(b + 1) * CH],
                    in_=xT_own[b].rearrange("(c p) s -> p c s", p=128)), 16)

        LD_ALL = (3 + B) * 16

        @block.tensor
        def _(t):
            t.wait_ge(ld.sem, LD_ALL)
            for tg in range(4):
                t0 = tg * 1024
                if tg > 0:
                    t.wait_ge(evp.sem, tg * 8)
                for ecp in range(8):
                    u = tg * 8 + ecp
                    if u >= 2:
                        t.wait_ge(evq.sem, u - 1)
                    p = ps[u % 2]
                    for dc in range(8):
                        for nh in range(2):
                            mm = t.matmul(
                                p[:, nh * 512:(nh + 1) * 512],
                                wq[:, dc * D + ecp * 128: dc * D + (ecp + 1) * 128],
                                xTo[:, dc * NT + t0 + nh * 512: dc * NT + t0 + (nh + 1) * 512],
                                start=(dc == 0), stop=(dc == 7))
                    mm.then_inc(mmq.sem, 1)
                for dk in range(8):
                    u = tg * 8 + dk
                    t.wait_ge(evq.sem, tg * 8 + 8)
                    if u >= 2:
                        t.wait_ge(evp.sem, u - 1)
                    p = ps[u % 2]
                    for ecp in range(8):
                        for nh in range(2):
                            mm = t.matmul(
                                p[:, nh * 512:(nh + 1) * 512],
                                wkt[:, ecp * D + dk * 128: ecp * D + (dk + 1) * 128],
                                qt_tg[:, ecp * 1024 + nh * 512: ecp * 1024 + (nh + 1) * 512],
                                start=(ecp == 0), stop=(ecp == 7))
                    mm.then_inc(mmp.sem, 1)

        @block.vector
        def _(v):
            for tg in range(4):
                for ecp in range(8):
                    u = tg * 8 + ecp
                    v.wait_ge(mmq.sem, u + 1)
                    v.tensor_scalar(qt_tg[:, ecp * 1024:(ecp + 1) * 1024], ps[u % 2][:],
                                    bqv[:, ecp:ecp + 1], None, ALU.add).then_inc(evq.sem, 1)

        @block.scalar
        def _(sc):
            for tg in range(4):
                for dk in range(8):
                    u = tg * 8 + dk
                    sc.wait_ge(mmp.sem, u + 1)
                    if u >= 2:
                        sc.wait_ge(st[u % 2].sem, 16 * (u // 2))
                    sc.copy(qp_sb[u % 2][:], ps[u % 2][:]).then_inc(evp.sem, 1)
                    sc.wait_ge(evp.sem, u + 1)
                    sc.dma_start(
                        out=qpT_o[dk * 128:(dk + 1) * 128, tg * 1024:(tg + 1) * 1024],
                        in_=qp_sb[u % 2][:]).then_inc(st[u % 2].sem, 16)

        @block.sync
        def _(sy):
            sy.wait_ge(st[0].sem, 16 * 16)
            sy.wait_ge(st[1].sem, 16 * 16)

    return nc



def build_l0_attn(nc=None, sctx=None, pre="", io=None):
    if nc is None:
        nc = bass.Bass("TRN2", target_bir_lowering=False, debug=False)
    if io is None:
        io = dict(
            x_full=nc.dram_tensor("x_full", [B, S, D], F32, kind="ExternalInput").ap(),
            xT_full=nc.dram_tensor("xT_full", [B, D, S], F32, kind="ExternalInput").ap(),
            x_own=nc.dram_tensor("x_own", [B, CH, D], F32, kind="ExternalInput").ap(),
            qpT=nc.dram_tensor("qpT", [D, NT], F32, kind="ExternalInput").ap(),
            Wv=nc.dram_tensor("Wv", [D, D], F32, kind="ExternalInput").ap(),
            bv_r=nc.dram_tensor("bv_r", [1, D], F32, kind="ExternalInput").ap(),
            ones_m=nc.dram_tensor("ones_m", [128, 128], F32, kind="ExternalInput").ap(),
            ident_f=nc.dram_tensor("ident_f", [128, 128], F32, kind="ExternalInput").ap(),
            r1=nc.dram_tensor("r1", [NT, D], F32, kind="ExternalOutput").ap())
    x_full, xT_full, x_own, qpT_i = io["x_full"], io["xT_full"], io["x_own"], io["qpT"]
    Wv, bv_r, ones_m, ident_f, r1_o = io["Wv"], io["bv_r"], io["ones_m"], io["ident_f"], io["r1"]

    NKQ = 8
    KQ = S // NKQ
    KCQ = KQ // 128
    CPQ = KCQ * 2
    CPB = NKQ * CPQ
    CXQ = 8 * 2

    with ExitStack() as ctx:
        cc = sctx if sctx is not None else ctx
        ec_ = ctx.enter_context
        qpt = ec_(nc.sbuf_tensor(pre + "qpt", [128, 8 * CH], BF16))
        xtq = [ec_(nc.sbuf_tensor(pre + f"xtq{i}", [128, 8 * KQ], BF16)) for i in range(2)]
        xbq = [ec_(nc.sbuf_tensor(pre + f"xbq{i}", [128, KCQ * D], BF16)) for i in range(2)]
        att = ec_(nc.sbuf_tensor(pre + "att", [128, KCQ * CH], BF16))
        ctxa = ec_(nc.sbuf_tensor(pre + "ctxa", [128, 8 * CH], F32))
        ctxb = ec_(nc.sbuf_tensor(pre + "ctxb", [128, 8 * CH], BF16))
        wv = ec_(nc.sbuf_tensor(pre + "wv", [128, 8 * D], BF16))
        ones_bf = ec_(nc.sbuf_tensor(pre + "ones_bf", [128, 128], BF16))
        id_f = ec_(nc.sbuf_tensor(pre + "id_f", [128, 128], F32))
        neg32 = ec_(nc.sbuf_tensor(pre + "neg32", [128, 1], F32))
        epsb = ec_(nc.sbuf_tensor(pre + "epsb", [128, 1], F32))
        l2h = ec_(nc.sbuf_tensor(pre + "l2h", [128, 1], F32))
        bv_bf = ec_(nc.sbuf_tensor(pre + "bv_bf", [1, D], BF16))
        bvb = ec_(nc.sbuf_tensor(pre + "bvb", [128, D], F32))
        sums_sb = ec_(nc.sbuf_tensor(pre + "sums_sb", [1, CH], F32))
        recip_sb = ec_(nc.sbuf_tensor(pre + "recip_sb", [1, CH], F32))
        recip_t = ec_(nc.sbuf_tensor(pre + "recip_t", [128, 8], F32))
        xo_t = [ec_(nc.sbuf_tensor(pre + f"xo_t{i}", [128, D], F32)) for i in range(2)]
        radd = ec_(nc.sbuf_tensor(pre + "radd", [128, D], F32))
        r1_t = [ec_(nc.sbuf_tensor(pre + f"r1_t{i}", [128, D], F32)) for i in range(2)]
        bn6 = ec_(nc.sbuf_tensor(pre + "bn6", [128, 12], F32))
        bn2 = ec_(nc.sbuf_tensor(pre + "bn2", [128, 2], F32))
        rstd = ec_(nc.sbuf_tensor(pre + "rstd", [128, 1], F32))

        ps_sc = [ec_(nc.psum_tensor(pre + f"ps_sc{i}", [128, 512], F32)) for i in range(2)]
        ps_cx = [ec_(nc.psum_tensor(pre + f"ps_cx{i}", [128, 512], F32)) for i in range(2)]
        ps_sm = ec_(nc.psum_tensor(pre + "ps_sm", [1, 1024], F32))
        ps_out = ec_(nc.psum_tensor(pre + "ps_out", [128, 1024], F32))

        ldc = Chan(cc, nc, pre + "ldc")
        ldm = Chan(cc, nc, pre + "ldm")
        ldq = Chan(cc, nc, pre + "ldq")
        ltq = [Chan(cc, nc, pre + f"ltq{i}") for i in range(2)]
        lbq = [Chan(cc, nc, pre + f"lbq{i}") for i in range(2)]
        ldi = Chan(cc, nc, pre + "ldi")
        lxo = [Chan(cc, nc, pre + f"lxo{i}") for i in range(2)]
        msc = Chan(cc, nc, pre + "msc")
        msm = Chan(cc, nc, pre + "msm")
        mcx = Chan(cc, nc, pre + "mcx")
        mbv = Chan(cc, nc, pre + "mbv")
        mtr = Chan(cc, nc, pre + "mtr")
        mop = Chan(cc, nc, pre + "mop")
        ex = Chan(cc, nc, pre + "ex")
        ecx = Chan(cc, nc, pre + "ecx")
        ecb = Chan(cc, nc, pre + "ecb")
        erc = Chan(cc, nc, pre + "erc")
        ert = Chan(cc, nc, pre + "ert")
        ebv = Chan(cc, nc, pre + "ebv")
        lnv = Chan(cc, nc, pre + "lnv")
        lns = Chan(cc, nc, pre + "lns")
        st = [Chan(cc, nc, pre + f"st{i}") for i in range(2)]

        block = ec_(nc.Block())

        LV, LS = 6, 2

        @block.gpsimd
        def _(g):
            g.dma_start(out=wv[:].rearrange("p (c f) -> p c f", f=D),
                        in_=Wv.rearrange("(c p) f -> p c f", p=128)).then_inc(ldc.sem, 16)
            g.dma_start(out=ones_bf[:], in_=ones_m).then_inc(ldc.sem, 16)
            g.dma_start(out=bv_bf[:], in_=bv_r).then_inc(ldc.sem, 16)
            g.memset(neg32[:], -32.0).then_inc(ldm.sem, 1)
            g.memset(epsb[:], 2.0 * EPS0).then_inc(ldm.sem, 1)
            g.memset(l2h[:], LN2H).then_inc(ldm.sem, 1)
            for b in range(B):
                if b > 0:
                    g.wait_ge(msc.sem, b * CPB)
                g.dma_start(
                    out=qpt[:].rearrange("p (c q) -> p c q", q=CH),
                    in_=qpT_i[:, b * CH:(b + 1) * CH].rearrange("(c p) q -> p c q", p=128),
                ).then_inc(ldq.sem, 16)
                for kq in range(NKQ):
                    qs = b * NKQ + kq
                    buf = qs % 2
                    if qs >= 2:
                        g.wait_ge(msc.sem, (qs - 1) * CPQ)
                    g.dma_start(
                        out=xtq[buf][:].rearrange("p (c k) -> p c k", k=KQ),
                        in_=xT_full[b, :, kq * KQ:(kq + 1) * KQ].rearrange("(c p) k -> p c k", p=128),
                    ).then_inc(ltq[buf].sem, 16)
                    if qs >= 2:
                        g.wait_ge(mcx.sem, (qs - 1) * CXQ)
                    g.dma_start(
                        out=xbq[buf][:].rearrange("p (c d) -> p c d", d=D),
                        in_=x_full[b, kq * KQ:(kq + 1) * KQ, :].rearrange("(c p) d -> p c d", p=128),
                    ).then_inc(lbq[buf].sem, 16)

        @block.sync
        def _(sy):
            sy.dma_start(out=id_f[:], in_=ident_f).then_inc(ldi.sem, 16)
            for b in range(B):
                for qt in range(8):
                    u = b * 8 + qt
                    if u >= 2:
                        sy.wait_ge(lnv.sem, (u - 2) * LV + 2)
                    sy.dma_start(out=xo_t[u % 2][:],
                                 in_=x_own[b, qt * 128:(qt + 1) * 128, :]).then_inc(lxo[u % 2].sem, 16)
                    sy.wait_ge(lnv.sem, (u + 1) * LV)
                    sy.dma_start(out=r1_o[u * 128:(u + 1) * 128, :],
                                 in_=r1_t[u % 2][:]).then_inc(st[u % 2].sem, 16)
            sy.wait_ge(st[0].sem, 16 * 16)
            sy.wait_ge(st[1].sem, 16 * 16)

        @block.tensor
        def _(t):
            t.wait_ge(ldc.sem, 48)
            for nh in range(2):
                t.matmul(ps_out[:, nh * 512:(nh + 1) * 512], ones_bf[0:1, :],
                         bv_bf[:, nh * 512:(nh + 1) * 512],
                         start=True, stop=True).then_inc(mbv.sem, 1)
            for b in range(B):
                t.wait_ge(ldq.sem, 16 * (b + 1))
                for kq in range(NKQ):
                    qs = b * NKQ + kq
                    t.wait_ge(ltq[qs % 2].sem, 16 * (qs // 2 + 1))
                    for kc in range(KCQ):
                        for qh in range(2):
                            ci = qs * CPQ + kc * 2 + qh
                            if ci >= 2:
                                t.wait_ge(ex.sem, ci - 1)
                            p = ps_sc[ci % 2]
                            for dc in range(8):
                                mm = t.matmul(
                                    p[:],
                                    xtq[qs % 2][:, dc * KQ + kc * 128: dc * KQ + (kc + 1) * 128],
                                    qpt[:, dc * CH + qh * 512: dc * CH + (qh + 1) * 512],
                                    start=(dc == 0), stop=(dc == 7))
                            mm.then_inc(msc.sem, 1)
                            pc = ci - 1
                            if pc >= b * CPB:
                                if pc == b * CPB:
                                    t.wait_ge(erc.sem, 2 * b)
                                t.wait_ge(ex.sem, pc + 1)
                                kcp = (pc % CPQ) // 2
                                qhp = pc % 2
                                t.matmul(ps_sm[0:1, qhp * 512:(qhp + 1) * 512],
                                         ones_bf[:, 0:1],
                                         att[:, kcp * CH + qhp * 512: kcp * CH + (qhp + 1) * 512],
                                         start=(pc < b * CPB + 2),
                                         stop=(pc == (b + 1) * CPB - 2)).then_inc(msm.sem, 1)
                    t.wait_ge(ex.sem, (qs + 1) * CPQ)
                    t.wait_ge(lbq[qs % 2].sem, 16 * (qs // 2 + 1))
                    for dc in range(8):
                        for qh in range(2):
                            gi = qs * CXQ + dc * 2 + qh
                            if gi >= 2:
                                t.wait_ge(ecx.sem, gi - 1)
                            if gi % (CXQ * NKQ) == 0 and gi > 0:
                                t.wait_ge(ert.sem, gi // (CXQ * NKQ))
                            p = ps_cx[gi % 2]
                            for kc in range(KCQ):
                                mm = t.matmul(
                                    p[:],
                                    xbq[qs % 2][:, kc * D + dc * 128: kc * D + (dc + 1) * 128],
                                    att[:, kc * CH + qh * 512: kc * CH + (qh + 1) * 512],
                                    start=(kc == 0), stop=(kc == KCQ - 1))
                            mm.then_inc(mcx.sem, 1)
                lc = (b + 1) * CPB - 1
                t.wait_ge(ex.sem, lc + 1)
                kcp = (lc % CPQ) // 2
                qhp = lc % 2
                t.matmul(ps_sm[0:1, qhp * 512:(qhp + 1) * 512], ones_bf[:, 0:1],
                         att[:, kcp * CH + qhp * 512: kcp * CH + (qhp + 1) * 512],
                         start=False, stop=True).then_inc(msm.sem, 1)
                t.wait_ge(erc.sem, 2 * (b + 1))
                t.wait_ge(ldi.sem, 16)
                if b > 0:
                    t.wait_ge(ert.sem, b)
                for qt in range(8):
                    tr = t.matmul(ps_cx[0][:, qt:qt + 1],
                                  recip_sb[0:1, qt * 128:(qt + 1) * 128],
                                  id_f[0:1, 0:1], start=True, stop=True)
                tr.then_inc(mtr.sem, 1)
                t.wait_ge(ecb.sem, (b + 1) * 8)
                for qt in range(8):
                    u = b * 8 + qt
                    if u >= 1:
                        t.wait_ge(lnv.sem, (u - 1) * LV + 1)
                    else:
                        t.wait_ge(ebv.sem, 1)
                    for ecc in range(8):
                        for nh in range(2):
                            mm = t.matmul(
                                ps_out[:, nh * 512:(nh + 1) * 512],
                                ctxb[:, ecc * CH + qt * 128: ecc * CH + (qt + 1) * 128],
                                wv[:, ecc * D + nh * 512: ecc * D + (nh + 1) * 512],
                                start=(ecc == 0), stop=(ecc == 7))
                    mm.then_inc(mop.sem, 1)

        @block.scalar
        def _(sc):
            sc.wait_ge(ldc.sem, 48)
            sc.wait_ge(ldm.sem, 3)
            for b in range(B):
                for kq in range(NKQ):
                    qs = b * NKQ + kq
                    for kc in range(KCQ):
                        for qh in range(2):
                            ci = qs * CPQ + kc * 2 + qh
                            sc.wait_ge(msc.sem, ci + 1)
                            if qs >= 1:
                                if kc == 0 and qh == 0:
                                    sc.wait_ge(mcx.sem, qs * CXQ)
                                sc.wait_ge(msm.sem, (qs - 1) * CPQ + kc * 2 + qh + 1)
                            sc.activation(att[:, kc * CH + qh * 512: kc * CH + (qh + 1) * 512],
                                          ps_sc[ci % 2][:], AF.Exp,
                                          bias=neg32[:], scale=1.0 / 32.0).then_inc(ex.sem, 1)
                for qt in range(8):
                    u = b * 8 + qt
                    ln_scalar(sc, lnv, lns, u * LV + 5, rstd, bn2, epsb, l2h)

        @block.vector
        def _(v):
            v.wait_ge(mbv.sem, 2)
            v.tensor_copy(bvb[:], ps_out[:]).then_inc(ebv.sem, 1)
            for b in range(B):
                for kq in range(NKQ):
                    qs = b * NKQ + kq
                    for dc in range(8):
                        for qh in range(2):
                            gi = qs * CXQ + dc * 2 + qh
                            v.wait_ge(mcx.sem, gi + 1)
                            dst = ctxa[:, dc * CH + qh * 512: dc * CH + (qh + 1) * 512]
                            if kq == 0:
                                v.tensor_copy(dst, ps_cx[gi % 2][:]).then_inc(ecx.sem, 1)
                            else:
                                v.tensor_tensor(dst, dst, ps_cx[gi % 2][:], ALU.add).then_inc(ecx.sem, 1)
                            ecx.n += 1
                v.wait_ge(msm.sem, (b + 1) * CPB)
                v.tensor_copy(sums_sb[:], ps_sm[0:1, :]).then_inc(erc.sem, 1)
                v.wait_ge(erc.sem, 2 * b + 1)
                v.reciprocal(recip_sb[:], sums_sb[:]).then_inc(erc.sem, 1)
                if b > 0:
                    v.wait_ge(mop.sem, b * 8)
                for dc in range(8):
                    v.tensor_copy(ctxb[:, dc * CH:(dc + 1) * CH],
                                  ctxa[:, dc * CH:(dc + 1) * CH]).then_inc(ecb.sem, 1)
                v.wait_ge(mtr.sem, b + 1)
                v.tensor_copy(recip_t[:], ps_cx[0][:, 0:8]).then_inc(ert.sem, 1)
                v.wait_ge(ert.sem, b + 1)
                for qt in range(8):
                    u = b * 8 + qt
                    v.wait_ge(mop.sem, u + 1)
                    v.wait_ge(lxo[u % 2].sem, 16 * (u // 2 + 1))
                    v.wait_ge(lnv.sem, lnv.n)
                    v.scalar_tensor_tensor(radd[:], ps_out[:], recip_t[:, qt:qt + 1],
                                           bvb[:], ALU.mult, ALU.add).then_inc(lnv.sem, 1)
                    lnv.n += 1
                    v.wait_ge(lnv.sem, u * LV + 1)
                    v.tensor_tensor(radd[:], radd[:], xo_t[u % 2][:], ALU.add).then_inc(lnv.sem, 1)
                    lnv.n += 1
                    v.wait_ge(lnv.sem, u * LV + 2)
                    if u >= 2:
                        v.wait_ge(st[u % 2].sem, 16 * (u // 2))
                    ln_vec(v, lnv, lns, 2 * (u + 1), radd, r1_t[u % 2][:], bn6, bn2, rstd)

    return nc



GC1 = 0.7978845608028654
GC2 = GC1 * 0.044715


def build_l0_ffn(nc=None, sctx=None, pre="", io=None):
    if nc is None:
        nc = bass.Bass("TRN2", target_bir_lowering=False, debug=False)
    if io is None:
        io = dict(
            r1=nc.dram_tensor("r1", [NT, D], F32, kind="ExternalInput").ap(),
            U1=nc.dram_tensor("U1", [D, 256], F32, kind="ExternalInput").ap(),
            V1=nc.dram_tensor("V1", [256, 4096], F32, kind="ExternalInput").ap(),
            b1_r=nc.dram_tensor("b1_r", [1, 4096], F32, kind="ExternalInput").ap(),
            U2=nc.dram_tensor("U2", [4096, 256], F32, kind="ExternalInput").ap(),
            V2=nc.dram_tensor("V2", [256, D], F32, kind="ExternalInput").ap(),
            b2_r=nc.dram_tensor("b2_r", [1, D], F32, kind="ExternalInput").ap(),
            ones_m=nc.dram_tensor("ones_m", [128, 128], F32, kind="ExternalInput").ap(),
            ident_f=nc.dram_tensor("ident_f", [128, 128], F32, kind="ExternalInput").ap(),
            r2=nc.dram_tensor("r2", [NT, D], F32, kind="ExternalOutput").ap())
    r1_i, U1, V1, b1_r, U2, V2 = io["r1"], io["U1"], io["V1"], io["b1_r"], io["U2"], io["V2"]
    b2_r, ones_m, ident_f, r2_o = io["b2_r"], io["ones_m"], io["ident_f"], io["r2"]

    NG = 8
    with ExitStack() as ctx:
        cc = sctx if sctx is not None else ctx
        ec_ = ctx.enter_context
        r1g = [ec_(nc.sbuf_tensor(pre + f"r1g{i}", [128, 4 * D], BF16)) for i in range(2)]
        rT = [ec_(nc.sbuf_tensor(pre + f"rT{i}", [128, 8 * 512], BF16)) for i in range(2)]
        u1 = ec_(nc.sbuf_tensor(pre + "u1", [128, 8 * 256], BF16))
        v1 = ec_(nc.sbuf_tensor(pre + "v1", [128, 2 * 4096], BF16))
        u2 = ec_(nc.sbuf_tensor(pre + "u2", [128, 32 * 256], BF16))
        v2 = ec_(nc.sbuf_tensor(pre + "v2", [128, 2 * D], BF16))
        b1row = ec_(nc.sbuf_tensor(pre + "b1row", [1, 4096], BF16))
        ones_row = ec_(nc.sbuf_tensor(pre + "ones_row", [1, 512], BF16))
        epsb = ec_(nc.sbuf_tensor(pre + "epsb", [128, 1], F32))
        l2h = ec_(nc.sbuf_tensor(pre + "l2h", [128, 1], F32))
        b2_bf = ec_(nc.sbuf_tensor(pre + "b2_bf", [1, D], BF16))
        id_bf = ec_(nc.sbuf_tensor(pre + "id_bf", [128, 128], BF16))
        ones_bf = ec_(nc.sbuf_tensor(pre + "ones_bf", [128, 128], BF16))
        pT = ec_(nc.sbuf_tensor(pre + "pT", [128, 2 * 512], BF16))
        h2 = ec_(nc.sbuf_tensor(pre + "h2", [128, 32 * 512], BF16))
        q2T = ec_(nc.sbuf_tensor(pre + "q2T", [128, 2 * 512], BF16))
        zb = [ec_(nc.sbuf_tensor(pre + f"zb{i}", [128, 1024], BF16)) for i in range(2)]
        zsq = [ec_(nc.sbuf_tensor(pre + f"zsq{i}", [128, 1024], BF16)) for i in range(2)]
        um = [ec_(nc.sbuf_tensor(pre + f"um{i}", [128, 1024], BF16)) for i in range(2)]
        th = [ec_(nc.sbuf_tensor(pre + f"th{i}", [128, 1024], BF16)) for i in range(2)]
        r1f = [ec_(nc.sbuf_tensor(pre + f"r1f{i}", [128, D], F32)) for i in range(2)]
        o2b = ec_(nc.sbuf_tensor(pre + "o2b", [128, 4 * D], F32))
        radd = ec_(nc.sbuf_tensor(pre + "radd", [128, D], F32))
        r2t = [ec_(nc.sbuf_tensor(pre + f"r2t{i}", [128, D], F32)) for i in range(2)]
        bn6 = ec_(nc.sbuf_tensor(pre + "bn6", [128, 12], F32))
        bn2 = ec_(nc.sbuf_tensor(pre + "bn2", [128, 2], F32))
        rstd = ec_(nc.sbuf_tensor(pre + "rstd", [128, 1], F32))

        ps_t = ec_(nc.psum_tensor(pre + "ps_t", [128, 512], BF16))
        ps_z = [ec_(nc.psum_tensor(pre + f"ps_z{i}", [128, 1024], F32)) for i in range(2)]
        ps_o = ec_(nc.psum_tensor(pre + "ps_o", [128, 1024], F32))

        ldc = Chan(cc, nc, pre + "ldc")
        ldm = Chan(cc, nc, pre + "ldm")
        ldr = [Chan(cc, nc, pre + f"ldr{i}") for i in range(2)]
        lrf = [Chan(cc, nc, pre + f"lrf{i}") for i in range(2)]
        mtp = Chan(cc, nc, pre + "mtp")
        mP = Chan(cc, nc, pre + "mP")
        mz = Chan(cc, nc, pre + "mz")
        mq = Chan(cc, nc, pre + "mq")
        mo = Chan(cc, nc, pre + "mo")
        etp = Chan(cc, nc, pre + "etp")
        evP = Chan(cc, nc, pre + "evP")
        gv = Chan(cc, nc, pre + "gv")
        eo = Chan(cc, nc, pre + "eo")
        gs = Chan(cc, nc, pre + "gs")
        eq = Chan(cc, nc, pre + "eq")
        lnv = Chan(cc, nc, pre + "lnv")
        lns = Chan(cc, nc, pre + "lns")
        st = [Chan(cc, nc, pre + f"st{i}") for i in range(2)]
        block = ec_(nc.Block())

        LV = 5
        NSP = 16
        NSPT = NG * NSP

        def gsB(s):
            g, sp = s // NSP, s % NSP
            return 3 * NSP * g + (0 if sp == 0 else 3 * sp - 1)

        def th_cnt(s):
            g, sp = s // NSP, s % NSP
            return 3 * NSP * g + (3 * sp + 5 if sp < NSP - 1 else 3 * NSP)

        @block.gpsimd
        def _(g_):
            g_.dma_start(out=u1[:].rearrange("p (c f) -> p c f", f=256),
                         in_=U1.rearrange("(c p) f -> p c f", p=128)).then_inc(ldc.sem, 16)
            g_.dma_start(out=v1[:].rearrange("p (c f) -> p c f", f=4096),
                         in_=V1.rearrange("(c p) f -> p c f", p=128)).then_inc(ldc.sem, 16)
            g_.dma_start(out=u2[:].rearrange("p (c f) -> p c f", f=256),
                         in_=U2.rearrange("(c p) f -> p c f", p=128)).then_inc(ldc.sem, 16)
            g_.dma_start(out=v2[:].rearrange("p (c f) -> p c f", f=D),
                         in_=V2.rearrange("(c p) f -> p c f", p=128)).then_inc(ldc.sem, 16)
            g_.dma_start(out=b1row[:], in_=b1_r).then_inc(ldc.sem, 16)
            g_.dma_start(out=b2_bf[:], in_=b2_r).then_inc(ldc.sem, 16)
            g_.dma_start(out=id_bf[:], in_=ident_f).then_inc(ldc.sem, 16)
            g_.dma_start(out=ones_bf[:], in_=ones_m).then_inc(ldc.sem, 16)
            g_.memset(ones_row[:], 1.0).then_inc(ldm.sem, 1)
            g_.memset(epsb[:], 2.0 * EPS0).then_inc(ldm.sem, 1)
            g_.memset(l2h[:], LN2H).then_inc(ldm.sem, 1)
            for g in range(NG):
                if g >= 2:
                    g_.wait_ge(mtp.sem, 8 * (g - 1))
                g_.dma_start(
                    out=r1g[g % 2][:].rearrange("p (c d) -> p c d", d=D),
                    in_=r1_i[g * 512:(g + 1) * 512, :].rearrange("(c p) d -> p c d", p=128),
                ).then_inc(ldr[g % 2].sem, 16)
        LDC = 8 * 16

        @block.sync
        def _(sy):
            for g in range(NG):
                for ti in range(4):
                    u = g * 4 + ti
                    if u >= 2:
                        sy.wait_ge(lnv.sem, (u - 2) * LV + 1)
                    sy.dma_start(out=r1f[u % 2][:],
                                 in_=r1_i[u * 128:(u + 1) * 128, :]).then_inc(lrf[u % 2].sem, 16)
                for ti in range(4):
                    u = g * 4 + ti
                    sy.wait_ge(lnv.sem, (u + 1) * LV)
                    sy.dma_start(out=r2_o[u * 128:(u + 1) * 128, :],
                                 in_=r2t[u % 2][:]).then_inc(st[u % 2].sem, 16)
            sy.wait_ge(st[0].sem, 16 * 16)
            sy.wait_ge(st[1].sem, 16 * 16)

        @block.tensor
        def _(t):
            t.wait_ge(ldc.sem, LDC)
            t.wait_ge(ldm.sem, 1)
            for g in range(NG):
                gb = rT[g % 2]
                t.wait_ge(ldr[g % 2].sem, 16 * (g // 2 + 1))
                for dc in range(8):
                    for ti in range(4):
                        if g >= 1 or dc >= 1:
                            t.wait_ge(etp.sem, g * 8 + dc)
                        tr = t.transpose(
                            ps_t[:, ti * 128:(ti + 1) * 128],
                            r1g[g % 2][:, ti * D + dc * 128: ti * D + (dc + 1) * 128],
                            id_bf[:])
                    tr.then_inc(mtp.sem, 1)
                t.wait_ge(etp.sem, (g + 1) * 8)
                if g > 0:
                    t.wait_ge(gv.sem, g * NSP * 3)
                for rc in range(2):
                    i = g * 2 + rc
                    if i >= 2:
                        t.wait_ge(eq.sem, i - 1)
                    for dc in range(8):
                        mm = t.matmul(ps_z[rc][:, 0:512],
                                      u1[:, dc * 256 + rc * 128: dc * 256 + (rc + 1) * 128],
                                      gb[:, dc * 512:(dc + 1) * 512],
                                      start=(dc == 0), stop=(dc == 7))
                    mm.then_inc(mP.sem, 1)
                t.wait_ge(evP.sem, (g + 1) * 2)
                for sp in range(NSP):
                    s = g * NSP + sp
                    if sp >= 2:
                        t.wait_ge(gs.sem, gsB(s - 2) + 1)
                    for sub in range(2):
                        fc = sp * 2 + sub
                        t.matmul(ps_z[sp % 2][:, sub * 512:(sub + 1) * 512],
                                 b1row[0:1, fc * 128:(fc + 1) * 128],
                                 ones_row[0:1, :], start=True, stop=False)
                        for rc in range(2):
                            mm = t.matmul(ps_z[sp % 2][:, sub * 512:(sub + 1) * 512],
                                          v1[:, rc * 4096 + fc * 128: rc * 4096 + (fc + 1) * 128],
                                          pT[:, rc * 512:(rc + 1) * 512],
                                          start=False, stop=(rc == 1))
                    mm.then_inc(mz.sem, 1)
                t.wait_ge(gv.sem, (g + 1) * NSP * 3)
                for rc in range(2):
                    i = g * 2 + rc
                    if i >= 2:
                        t.wait_ge(eq.sem, i - 1)
                    for fc in range(32):
                        mm = t.matmul(ps_z[rc][:, 0:512],
                                      u2[:, fc * 256 + rc * 128: fc * 256 + (rc + 1) * 128],
                                      h2[:, fc * 512:(fc + 1) * 512],
                                      start=(fc == 0), stop=(fc == 31))
                    mm.then_inc(mq.sem, 1)
                t.wait_ge(eq.sem, (g + 1) * 2)
                for ti in range(4):
                    u = g * 4 + ti
                    if u >= 1:
                        t.wait_ge(eo.sem, u)
                    for nh in range(2):
                        t.matmul(ps_o[:, nh * 512:(nh + 1) * 512], ones_bf[0:1, :],
                                 b2_bf[:, nh * 512:(nh + 1) * 512], start=True, stop=False)
                    for rc in range(2):
                        for nh in range(2):
                            mm = t.matmul(
                                ps_o[:, nh * 512:(nh + 1) * 512],
                                q2T[:, rc * 512 + ti * 128: rc * 512 + (ti + 1) * 128],
                                v2[:, rc * D + nh * 512: rc * D + (nh + 1) * 512],
                                start=False, stop=(rc == 1))
                    mm.then_inc(mo.sem, 1)

        @block.scalar
        def _(sc):
            for g in range(NG + 1):
                if g >= 1:
                    gp = g - 1
                    for rc in range(2):
                        i = gp * 2 + rc
                        sc.wait_ge(mq.sem, i + 1)
                        sc.mul(q2T[:, rc * 512:(rc + 1) * 512], ps_z[rc][:, 0:512],
                               0.5).then_inc(eq.sem, 1)
                if g < NG:
                    for sp in range(NSP):
                        s = g * NSP + sp
                        sc.wait_ge(mz.sem, s + 1)
                        if s >= 2:
                            sc.wait_ge(gv.sem, (s - 1) * 3)
                        sc.copy(zb[s % 2][:], ps_z[sp % 2][:]).then_inc(gs.sem, 1)
                        gs.n += 1
                        sc.wait_ge(gs.sem, gsB(s) + 1)
                        sc.activation(zsq[s % 2][:], zb[s % 2][:], AF.Square).then_inc(gs.sem, 1)
                        gs.n += 1
                        if sp >= 1:
                            sc.wait_ge(gv.sem, (s - 1) * 3 + 2)
                            sc.activation(th[(s - 1) % 2][:], um[(s - 1) % 2][:],
                                          AF.Tanh).then_inc(gs.sem, 1)
                            gs.n += 1
                    s = g * NSP + NSP - 1
                    sc.wait_ge(gv.sem, s * 3 + 2)
                    sc.activation(th[s % 2][:], um[s % 2][:], AF.Tanh).then_inc(gs.sem, 1)
                    gs.n += 1
                if g >= 1:
                    gp = g - 1
                    sc.wait_ge(ldm.sem, 3)
                    for ti in range(4):
                        u = gp * 4 + ti
                        ln_scalar(sc, lnv, lns, u * LV + 4, rstd, bn2, epsb, l2h)

        @block.vector
        def _(v):
            for g in range(NG + 1):
                if g >= 1:
                    for ti in range(4):
                        u = (g - 1) * 4 + ti
                        v.wait_ge(mo.sem, u + 1)
                        v.tensor_copy(o2b[:, ti * D:(ti + 1) * D], ps_o[:]).then_inc(eo.sem, 1)
                        eo.n += 1
                if g < NG:
                    if g >= 2:
                        v.wait_ge(mP.sem, (g - 1) * 2)
                    for dc in range(8):
                        v.wait_ge(mtp.sem, g * 8 + dc + 1)
                        v.tensor_copy(rT[g % 2][:, dc * 512:(dc + 1) * 512],
                                      ps_t[:]).then_inc(etp.sem, 1)
                        etp.n += 1
                    for rc in range(2):
                        v.wait_ge(mP.sem, g * 2 + rc + 1)
                        v.tensor_copy(pT[:, rc * 512:(rc + 1) * 512],
                                      ps_z[rc][:, 0:512]).then_inc(evP.sem, 1)
                    for sp in range(NSP):
                        s = g * NSP + sp
                        fc0 = sp * 2
                        v.wait_ge(gs.sem, gsB(s) + 2)
                        v.tensor_scalar(um[s % 2][:], zsq[s % 2][:], GC2, GC1,
                                        ALU.mult, ALU.add).then_inc(gv.sem, 1)
                        gv.n += 1
                        v.wait_ge(gv.sem, s * 3 + 1)
                        v.tensor_tensor(um[s % 2][:], um[s % 2][:], zb[s % 2][:],
                                        ALU.mult).then_inc(gv.sem, 1)
                        gv.n += 1
                        v.wait_ge(gs.sem, th_cnt(s))
                        v.scalar_tensor_tensor(h2[:, fc0 * 512:(fc0 + 2) * 512],
                                               th[s % 2][:], 1.0, zb[s % 2][:],
                                               ALU.add, ALU.mult).then_inc(gv.sem, 1)
                        gv.n += 1
                if g >= 1:
                    gp = g - 1
                    for ti in range(4):
                        u = gp * 4 + ti
                        v.wait_ge(lrf[u % 2].sem, 16 * (u // 2 + 1))
                        v.wait_ge(lnv.sem, lnv.n)
                        v.tensor_tensor(radd[:], o2b[:, ti * D:(ti + 1) * D], r1f[u % 2][:],
                                        ALU.add).then_inc(lnv.sem, 1)
                        lnv.n += 1
                        v.wait_ge(lnv.sem, u * LV + 1)
                        if u >= 2:
                            v.wait_ge(st[u % 2].sem, 16 * (u // 2))
                        ln_vec(v, lnv, lns, 2 * (u + 1), radd, r2t[u % 2][:], bn6, bn2, rstd)

    return nc



def build_l1_attn(nc=None, sctx=None, pre="", io=None):
    if nc is None:
        nc = bass.Bass("TRN2", target_bir_lowering=False, debug=False)
    if io is None:
        io = dict(
            r2=nc.dram_tensor("r2", [NT, D], F32, kind="ExternalInput").ap(),
            qkv_w=nc.dram_tensor("qkv_w", [D, 3 * D], F32, kind="ExternalInput").ap(),
            lnsc_t=nc.dram_tensor("lnsc_t", [128, 8], F32, kind="ExternalInput").ap(),
            lnb_t=nc.dram_tensor("lnb_t", [128, 8], F32, kind="ExternalInput").ap(),
            Wo=nc.dram_tensor("Wo", [D, D], F32, kind="ExternalInput").ap(),
            ones_m=nc.dram_tensor("ones_m", [128, 128], F32, kind="ExternalInput").ap(),
            ident_f=nc.dram_tensor("ident_f", [128, 128], F32, kind="ExternalInput").ap(),
            r3=nc.dram_tensor("r3", [NT, D], F32, kind="ExternalOutput").ap())
    r2_i, qkv_w, lnsc_t, lnb_t = io["r2"], io["qkv_w"], io["lnsc_t"], io["lnb_t"]
    Wo, ones_m, ident_f, r3_o = io["Wo"], io["ones_m"], io["ident_f"], io["r3"]

    EPS1 = 1e-6
    PAIRS = [(i, j) for i in range(4) for j in range(i + 1)]
    VMX_G = 52

    with ExitStack() as ctx:
        cc = sctx if sctx is not None else ctx
        ec_ = ctx.enter_context
        tng = [ec_(nc.sbuf_tensor(pre + f"tng{i}", [128, 4 * D], BF16)) for i in range(2)]
        wq3 = ec_(nc.sbuf_tensor(pre + "wq3", [128, 8 * 3 * D], BF16))
        wo = ec_(nc.sbuf_tensor(pre + "wo", [128, 8 * D], BF16))
        qkv_g = [ec_(nc.sbuf_tensor(pre + f"qkv_g{i}", [128, 4 * 3 * D], BF16)) for i in range(2)]
        tnT = ec_(nc.sbuf_tensor(pre + "tnT", [128, 8 * 512], BF16))
        cT = ec_(nc.sbuf_tensor(pre + "cT", [128, 8 * 512], BF16))
        ctxg = [ec_(nc.sbuf_tensor(pre + f"ctxg{i}", [128, 4 * D], BF16)) for i in range(2)]
        prodb = [ec_(nc.sbuf_tensor(pre + f"prodb{i}", [128, D], BF16)) for i in range(2)]
        tmpv = ec_(nc.sbuf_tensor(pre + "tmpv", [128, D], BF16))
        lgs = ec_(nc.sbuf_tensor(pre + "lgs", [128, 160], F32))
        egs = ec_(nc.sbuf_tensor(pre + "egs", [128, 160], F32))
        su = ec_(nc.sbuf_tensor(pre + "su", [128, 48], F32))
        rec = ec_(nc.sbuf_tensor(pre + "rec", [128, 48], F32))
        atn = ec_(nc.sbuf_tensor(pre + "atn", [128, 144], BF16))
        lnscv = ec_(nc.sbuf_tensor(pre + "lnscv", [128, 8], F32))
        lnbv = ec_(nc.sbuf_tensor(pre + "lnbv", [128, 8], BF16))
        bqkv = ec_(nc.sbuf_tensor(pre + "bqkv", [1, 3 * D], BF16))
        epsA = ec_(nc.sbuf_tensor(pre + "epsA", [128, 1], F32))
        epsB = ec_(nc.sbuf_tensor(pre + "epsB", [128, 1], F32))
        l2h = ec_(nc.sbuf_tensor(pre + "l2h", [128, 1], F32))
        id_bf = ec_(nc.sbuf_tensor(pre + "id_bf", [128, 128], BF16))
        ones_bf = ec_(nc.sbuf_tensor(pre + "ones_bf", [128, 128], BF16))
        r2f = [ec_(nc.sbuf_tensor(pre + f"r2f{i}", [128, D], F32)) for i in range(2)]
        r2e = [ec_(nc.sbuf_tensor(pre + f"r2e{i}", [128, D], F32)) for i in range(2)]
        radd = ec_(nc.sbuf_tensor(pre + "radd", [128, D], F32))
        r3t = [ec_(nc.sbuf_tensor(pre + f"r3t{i}", [128, D], F32)) for i in range(2)]
        bn6 = ec_(nc.sbuf_tensor(pre + "bn6", [128, 12], F32))
        bn2 = ec_(nc.sbuf_tensor(pre + "bn2", [128, 2], F32))
        rstd = ec_(nc.sbuf_tensor(pre + "rstd", [128, 1], F32))

        ps_t = ec_(nc.psum_tensor(pre + "ps_t", [128, 512], BF16))
        ps_tc = ec_(nc.psum_tensor(pre + "ps_tc", [128, 512], BF16))
        pq = [ec_(nc.psum_tensor(pre + f"pq{i}", [128, 512], F32)) for i in range(2)]
        po = ec_(nc.psum_tensor(pre + "po", [128, 1024], F32))
        ps_b = ec_(nc.psum_tensor(pre + "ps_b", [1, 512], F32))

        ldw = Chan(cc, nc, pre + "ldw")
        ldm = Chan(cc, nc, pre + "ldm")
        lr2a = [Chan(cc, nc, pre + f"lr2a{i}") for i in range(2)]
        lr2e = [Chan(cc, nc, pre + f"lr2e{i}") for i in range(2)]
        mbq = Chan(cc, nc, pre + "mbq")
        ebq = Chan(cc, nc, pre + "ebq")
        evw = Chan(cc, nc, pre + "evw")
        mtp = Chan(cc, nc, pre + "mtp")
        etp = Chan(cc, nc, pre + "etp")
        mq = Chan(cc, nc, pre + "mq")
        eqk = Chan(cc, nc, pre + "eqk")
        vmx = Chan(cc, nc, pre + "vmx")
        smx = Chan(cc, nc, pre + "smx")
        mtc = Chan(cc, nc, pre + "mtc")
        etc_ = Chan(cc, nc, pre + "etc")
        mpo = Chan(cc, nc, pre + "mpo")
        lnv = Chan(cc, nc, pre + "lnv")
        lns = Chan(cc, nc, pre + "lns")
        st = [Chan(cc, nc, pre + f"st{i}") for i in range(2)]
        block = ec_(nc.Block())

        def lnvA(a):
            sj, b = a // 4, a % 4
            return 16 * sj + 20 * max(0, sj - 1) + 4 * b

        def lnvB(k):
            sj, ti = k // 4, k % 4
            return 16 * min(sj + 2, 8) + 20 * sj + 5 * ti

        def lnsA(a):
            sj, b = a // 4, a % 4
            return 8 * sj + 8 * max(0, sj - 1) + 2 * b

        def lnsB(k):
            sj, ti = k // 4, k % 4
            return 8 * min(sj + 2, 8) + 8 * sj + 2 * ti

        def a_tile(a):
            return (a % 4) * 8 + a // 4

        @block.gpsimd
        def _(g_):
            g_.dma_start(out=wq3[:].rearrange("p (c f) -> p c f", f=3 * D),
                         in_=qkv_w.rearrange("(c p) f -> p c f", p=128)).then_inc(ldw.sem, 16)
            g_.dma_start(out=wo[:].rearrange("p (c f) -> p c f", f=D),
                         in_=Wo.rearrange("(c p) f -> p c f", p=128)).then_inc(ldw.sem, 16)
            g_.dma_start(out=lnscv[:], in_=lnsc_t).then_inc(ldw.sem, 16)
            g_.dma_start(out=lnbv[:], in_=lnb_t).then_inc(ldw.sem, 16)
            g_.dma_start(out=id_bf[:], in_=ident_f).then_inc(ldw.sem, 16)
            g_.dma_start(out=ones_bf[:], in_=ones_m).then_inc(ldw.sem, 16)
            g_.memset(epsA[:], 2.0 * EPS1).then_inc(ldm.sem, 1)
            g_.memset(epsB[:], 2.0 * EPS0).then_inc(ldm.sem, 1)
            g_.memset(l2h[:], LN2H).then_inc(ldm.sem, 1)
        LDW = 96

        @block.sync
        def _(sy):
            for sj in range(9):
                if sj < 8:
                    for b in range(4):
                        a = sj * 4 + b
                        if a >= 2:
                            sy.wait_ge(lnv.sem, lnvA(a - 2) + 4)
                        sy.dma_start(out=r2f[a % 2][:],
                                     in_=r2_i[a_tile(a) * 128:(a_tile(a) + 1) * 128, :]
                                     ).then_inc(lr2a[a % 2].sem, 16)
                if sj >= 1:
                    mj = sj - 1
                    for ti in range(4):
                        u = ti * 8 + mj
                        k = mj * 4 + ti
                        if k >= 2:
                            sy.wait_ge(lnv.sem, lnvB(k - 2) + 1)
                        sy.dma_start(out=r2e[k % 2][:],
                                     in_=r2_i[u * 128:(u + 1) * 128, :]).then_inc(lr2e[k % 2].sem, 16)
                    for ti in range(4):
                        u = ti * 8 + mj
                        k = mj * 4 + ti
                        sy.wait_ge(lnv.sem, lnvB(k) + 5)
                        sy.dma_start(out=r3_o[u * 128:(u + 1) * 128, :],
                                     in_=r3t[k % 2][:]).then_inc(st[k % 2].sem, 16)
            sy.wait_ge(st[0].sem, 16 * 16)
            sy.wait_ge(st[1].sem, 16 * 16)

        @block.tensor
        def _(t):
            t.wait_ge(ldw.sem, LDW)
            for ch in range(6):
                for dc in range(8):
                    mm = t.matmul(ps_b[0:1, :], lnbv[:, dc:dc + 1],
                                  wq3[:, dc * 3 * D + ch * 512: dc * 3 * D + (ch + 1) * 512],
                                  start=(dc == 0), stop=(dc == 7))
                mm.then_inc(mbq.sem, 1)
                t.wait_ge(ebq.sem, ch + 1)
            for sj in range(9):
                if sj < 8:
                    t.wait_ge(lnv.sem, lnvA(sj * 4 + 3) + 4)
                    for dc in range(8):
                        pk = sj * 8 + dc
                        if pk >= 1:
                            t.wait_ge(etp.sem, pk)
                        for ti in range(4):
                            tr = t.transpose(
                                ps_t[:, ti * 128:(ti + 1) * 128],
                                tng[sj % 2][:, ti * D + dc * 128: ti * D + (dc + 1) * 128],
                                id_bf[:])
                        tr.then_inc(mtp.sem, 1)
                    t.wait_ge(etp.sem, (sj + 1) * 8)
                    t.wait_ge(evw.sem, 8)
                    if sj >= 2:
                        t.wait_ge(vmx.sem, (sj - 1) * VMX_G)
                    for ti in range(4):
                        for ch in range(6):
                            k = sj * 24 + ti * 6 + ch
                            if k >= 2:
                                t.wait_ge(eqk.sem, k - 1)
                            p = pq[k % 2]
                            t.matmul(p[:], ones_bf[0:1, :],
                                     bqkv[:, ch * 512:(ch + 1) * 512], start=True, stop=False)
                            for dc in range(8):
                                mm = t.matmul(
                                    p[:],
                                    tnT[:, dc * 512 + ti * 128: dc * 512 + (ti + 1) * 128],
                                    wq3[:, dc * 3 * D + ch * 512: dc * 3 * D + (ch + 1) * 512],
                                    start=False, stop=(dc == 7))
                            mm.then_inc(mq.sem, 1)
                if sj >= 1:
                    mj = sj - 1
                    t.wait_ge(vmx.sem, (mj + 1) * VMX_G)
                    for ec2 in range(8):
                        pk = mj * 8 + ec2
                        if pk >= 1:
                            t.wait_ge(etc_.sem, pk)
                        for ti in range(4):
                            tr = t.transpose(
                                ps_tc[:, ti * 128:(ti + 1) * 128],
                                ctxg[mj % 2][:, ti * D + ec2 * 128: ti * D + (ec2 + 1) * 128],
                                id_bf[:])
                        tr.then_inc(mtc.sem, 1)
                    t.wait_ge(etc_.sem, (mj + 1) * 8)
                    for ti in range(4):
                        k = mj * 4 + ti
                        if k > 0:
                            t.wait_ge(lnv.sem, lnvB(k - 1) + 1)
                        for ec2 in range(8):
                            for nh in range(2):
                                mm = t.matmul(
                                    po[:, nh * 512:(nh + 1) * 512],
                                    cT[:, ec2 * 512 + ti * 128: ec2 * 512 + (ti + 1) * 128],
                                    wo[:, ec2 * D + nh * 512: ec2 * D + (nh + 1) * 512],
                                    start=(ec2 == 0), stop=(ec2 == 7))
                        mm.then_inc(mpo.sem, 1)

        @block.scalar
        def _(sc):
            for sj in range(9):
                if sj < 8:
                    sc.wait_ge(ldm.sem, 3)
                    for b in range(4):
                        a = sj * 4 + b
                        ln_scalar(sc, lnv, lns, lnvA(a) + 3, rstd, bn2, epsA, l2h)
                    for ti in range(4):
                        for ch in range(6):
                            k = sj * 24 + ti * 6 + ch
                            sc.wait_ge(mq.sem, k + 1)
                            sc.copy(qkv_g[sj % 2][:, ti * 3 * D + ch * 512: ti * 3 * D + (ch + 1) * 512],
                                    pq[k % 2][:]).then_inc(eqk.sem, 1)
                            eqk.n += 1
                if sj >= 1:
                    mj = sj - 1
                    sc.wait_ge(vmx.sem, mj * VMX_G + 20)
                    sc.activation(egs[:], lgs[:], AF.Exp, scale=0.125).then_inc(smx.sem, 1)
                    smx.n += 1
                    for ti in range(4):
                        k = mj * 4 + ti
                        ln_scalar(sc, lnv, lns, lnvB(k) + 4, rstd, bn2, epsB, l2h)

        @block.vector
        def _(v):
            for ch in range(6):
                v.wait_ge(mbq.sem, ch + 1)
                v.tensor_copy(bqkv[:, ch * 512:(ch + 1) * 512], ps_b[0:1, :]).then_inc(ebq.sem, 1)
                ebq.n += 1
            v.wait_ge(mbq.sem, 6)
            for dc in range(8):
                v.wait_ge(evw.sem, dc)
                v.tensor_scalar(wq3[:, dc * 3 * D:(dc + 1) * 3 * D],
                                wq3[:, dc * 3 * D:(dc + 1) * 3 * D],
                                lnscv[:, dc:dc + 1], None, ALU.mult).then_inc(evw.sem, 1)
                evw.n += 1

            PIX = {(i, j): pi for pi, (i, j) in enumerate(PAIRS)}

            for sj in range(9):
                if sj < 8:
                    if sj >= 2:
                        v.wait_ge(mtp.sem, 8 * (sj - 1))
                    for b in range(4):
                        a = sj * 4 + b
                        v.wait_ge(lr2a[a % 2].sem, 16 * (a // 2 + 1))
                        ln_vec(v, lnv, lns, lnsA(a) + 2, r2f[a % 2],
                               tng[sj % 2][:, b * D:(b + 1) * D], bn6, bn2, rstd)
                    for dc in range(8):
                        v.wait_ge(mtp.sem, sj * 8 + dc + 1)
                        v.tensor_copy(tnT[:, dc * 512:(dc + 1) * 512], ps_t[:]).then_inc(etp.sem, 1)
                        etp.n += 1
                if sj >= 1:
                    mj = sj - 1
                    v.wait_ge(eqk.sem, (mj + 1) * 24)
                    if mj >= 2:
                        v.wait_ge(mtc.sem, 8 * (mj - 1))

                    def q_(i):
                        return qkv_g[mj % 2][:, i * 3 * D + 0: i * 3 * D + D]

                    def k_(i):
                        return qkv_g[mj % 2][:, i * 3 * D + D: i * 3 * D + 2 * D]

                    def v_(i):
                        return qkv_g[mj % 2][:, i * 3 * D + 2 * D: i * 3 * D + 3 * D]

                    def vop(inst):
                        inst.then_inc(vmx.sem, 1)
                        vmx.n += 1
                        v.wait_ge(vmx.sem, vmx.n)

                    for pi, (i, j) in enumerate(PAIRS):
                        vop(v.tensor_tensor(prodb[pi % 2][:], q_(i), k_(j), ALU.mult))
                        vop(v.tensor_reduce(lgs[:, pi * 16:(pi + 1) * 16],
                                            prodb[pi % 2][:].rearrange("p (h d) -> p h d", d=64),
                                            AX.X, ALU.add))
                    v.wait_ge(smx.sem, mj + 1)

                    def eg(i, j):
                        p = PIX[(i, j)]
                        return egs[:, p * 16:(p + 1) * 16]

                    for i in range(1, 4):
                        s_sl = su[:, (i - 1) * 16: i * 16]
                        vop(v.tensor_tensor(s_sl, eg(i, 0), eg(i, 1), ALU.add))
                        for j in range(2, i + 1):
                            vop(v.tensor_tensor(s_sl, s_sl, eg(i, j), ALU.add))
                    vop(v.reciprocal(rec[:], su[:]))
                    for i in range(1, 4):
                        for j in range(i + 1):
                            p = PIX[(i, j)] - 1
                            vop(v.tensor_tensor(atn[:, p * 16:(p + 1) * 16], eg(i, j),
                                                rec[:, (i - 1) * 16: i * 16], ALU.mult))
                    vop(v.tensor_copy(ctxg[mj % 2][:, 0:D], v_(0)))
                    for i in range(1, 4):
                        dst3 = ctxg[mj % 2][:, i * D:(i + 1) * D].rearrange("p (h d) -> p h d", d=64)
                        for j in range(i + 1):
                            p = PIX[(i, j)] - 1
                            a3 = atn[:, p * 16:(p + 1) * 16].unsqueeze(2).broadcast_to([128, 16, 64])
                            v3 = v_(j).rearrange("p (h d) -> p h d", d=64)
                            if j == 0:
                                vop(v.tensor_tensor(dst3, v3, a3, ALU.mult))
                            else:
                                vop(v.tensor_tensor(tmpv[:].rearrange("p (h d) -> p h d", d=64),
                                                    v3, a3, ALU.mult))
                                vop(v.tensor_tensor(dst3, dst3,
                                                    tmpv[:].rearrange("p (h d) -> p h d", d=64),
                                                    ALU.add))
                    for ec2 in range(8):
                        v.wait_ge(mtc.sem, mj * 8 + ec2 + 1)
                        v.tensor_copy(cT[:, ec2 * 512:(ec2 + 1) * 512], ps_tc[:]).then_inc(etc_.sem, 1)
                        etc_.n += 1
                    for ti in range(4):
                        k = mj * 4 + ti
                        v.wait_ge(mpo.sem, k + 1)
                        v.wait_ge(lr2e[k % 2].sem, 16 * (k // 2 + 1))
                        v.wait_ge(lnv.sem, lnv.n)
                        v.tensor_tensor(radd[:], po[:], r2e[k % 2][:], ALU.add).then_inc(lnv.sem, 1)
                        lnv.n += 1
                        v.wait_ge(lnv.sem, lnvB(k) + 1)
                        if k >= 2:
                            v.wait_ge(st[k % 2].sem, 16 * (k // 2))
                        ln_vec(v, lnv, lns, lnsB(k) + 2, radd, r3t[k % 2][:], bn6, bn2, rstd)

    return nc



def build_l1_ffn(nc=None, sctx=None, pre="", io=None):
    if nc is None:
        nc = bass.Bass("TRN2", target_bir_lowering=False, debug=False)
    if io is None:
        io = dict(
            r3=nc.dram_tensor("r3", [NT, D], F32, kind="ExternalInput").ap(),
            wgT=nc.dram_tensor("wgT", [D, 3072], I32, kind="ExternalInput").ap(),
            wuT=nc.dram_tensor("wuT", [D, 3072], I32, kind="ExternalInput").ap(),
            wdT=nc.dram_tensor("wdT", [3072, D], I32, kind="ExternalInput").ap(),
            ident_f=nc.dram_tensor("ident_f", [128, 128], F32, kind="ExternalInput").ap(),
            out=nc.dram_tensor("out", [NT, D], F32, kind="ExternalOutput").ap())
    r3_i, wgT_s, wuT_s, wdT_s = io["r3"], io["wgT"], io["wuT"], io["wdT"]
    ident_f, out_o = io["ident_f"], io["out"]

    NG = 16
    QD = 1.0 / 127.0
    with ExitStack() as ctx:
        cc = sctx if sctx is not None else ctx
        ec_ = ctx.enter_context
        wg3 = ec_(nc.sbuf_tensor(pre + "wg3", [128, 8 * 3072], BF16))
        wu3 = ec_(nc.sbuf_tensor(pre + "wu3", [128, 8 * 3072], BF16))
        wd3 = ec_(nc.sbuf_tensor(pre + "wd3", [128, 24 * D], BF16))
        id_bf = ec_(nc.sbuf_tensor(pre + "id_bf", [128, 128], BF16))
        r3g = [ec_(nc.sbuf_tensor(pre + f"r3g{i}", [128, 2 * D], BF16)) for i in range(2)]
        rT = [ec_(nc.sbuf_tensor(pre + f"rT{i}", [128, 8 * 256], BF16)) for i in range(2)]
        hh = ec_(nc.sbuf_tensor(pre + "hh", [128, 24 * 256], BF16))
        sg_sb = [ec_(nc.sbuf_tensor(pre + f"sg_sb{i}", [128, 512], BF16)) for i in range(2)]
        r3f = [ec_(nc.sbuf_tensor(pre + f"r3f{i}", [128, D], F32)) for i in range(2)]
        radd = ec_(nc.sbuf_tensor(pre + "radd", [128, D], F32))
        r4t = [ec_(nc.sbuf_tensor(pre + f"r4t{i}", [128, D], F32)) for i in range(2)]
        bn6 = ec_(nc.sbuf_tensor(pre + "bn6", [128, 12], F32))
        bn2 = ec_(nc.sbuf_tensor(pre + "bn2", [128, 2], F32))
        rstd = ec_(nc.sbuf_tensor(pre + "rstd", [128, 1], F32))

        epsb = ec_(nc.sbuf_tensor(pre + "epsb", [128, 1], F32))
        l2h = ec_(nc.sbuf_tensor(pre + "l2h", [128, 1], F32))
        ps_t = ec_(nc.psum_tensor(pre + "ps_t", [128, 256], BF16))
        ps_g = [ec_(nc.psum_tensor(pre + f"ps_g{i}", [128, 512], F32)) for i in range(2)]
        ps_u = [ec_(nc.psum_tensor(pre + f"ps_u{i}", [128, 512], F32)) for i in range(2)]
        po = ec_(nc.psum_tensor(pre + "po", [128, 1024], F32))

        ldw = Chan(cc, nc, pre + "ldw")
        lr3 = [Chan(cc, nc, pre + f"lr3{i}") for i in range(2)]
        lrf = [Chan(cc, nc, pre + f"lrf{i}") for i in range(2)]
        mtp = Chan(cc, nc, pre + "mtp")
        etp = Chan(cc, nc, pre + "etp")
        mg = Chan(cc, nc, pre + "mg")
        mu = Chan(cc, nc, pre + "mu")
        gsl = Chan(cc, nc, pre + "gsl")
        ehh = Chan(cc, nc, pre + "ehh")
        mdo = Chan(cc, nc, pre + "mdo")
        lnv = Chan(cc, nc, pre + "lnv")
        lns = Chan(cc, nc, pre + "lns")
        st = [Chan(cc, nc, pre + f"st{i}") for i in range(2)]
        ldm = Chan(cc, nc, pre + "ldm")
        block = ec_(nc.Block())
        LV = 5

        @block.gpsimd
        def _(g_):
            g_.memset(epsb[:], 2.0 * EPS0).then_inc(ldm.sem, 1)
            g_.memset(l2h[:], LN2H).then_inc(ldm.sem, 1)
            g_.dma_start(out=wg3[:].rearrange("p (c f) -> p c f", f=3072),
                         in_=wgT_s.rearrange("(c p) f -> p c f", p=128)).then_inc(ldw.sem, 16)
            g_.dma_start(out=wu3[:].rearrange("p (c f) -> p c f", f=3072),
                         in_=wuT_s.rearrange("(c p) f -> p c f", p=128)).then_inc(ldw.sem, 16)
            g_.dma_start(out=wd3[:].rearrange("p (c f) -> p c f", f=D),
                         in_=wdT_s.rearrange("(c p) f -> p c f", p=128)).then_inc(ldw.sem, 16)
            g_.dma_start(out=id_bf[:], in_=ident_f).then_inc(ldw.sem, 16)
            for g in range(NG):
                if g >= 2:
                    g_.wait_ge(mtp.sem, 8 * (g - 1))
                g_.dma_start(
                    out=r3g[g % 2][:].rearrange("p (c d) -> p c d", d=D),
                    in_=r3_i[g * 256:(g + 1) * 256, :].rearrange("(c p) d -> p c d", p=128),
                ).then_inc(lr3[g % 2].sem, 16)
        LDW = 64

        @block.sync
        def _(sy):
            for g in range(NG):
                for ti in range(2):
                    u = g * 2 + ti
                    if u >= 2:
                        sy.wait_ge(lnv.sem, (u - 2) * LV + 1)
                    sy.dma_start(out=r3f[u % 2][:],
                                 in_=r3_i[u * 128:(u + 1) * 128, :]).then_inc(lrf[u % 2].sem, 16)
                    sy.wait_ge(lnv.sem, (u + 1) * LV)
                    sy.dma_start(out=out_o[u * 128:(u + 1) * 128, :],
                                 in_=r4t[u % 2][:]).then_inc(st[u % 2].sem, 16)
            sy.wait_ge(st[0].sem, 16 * 16)
            sy.wait_ge(st[1].sem, 16 * 16)

        @block.tensor
        def _(t):
            t.wait_ge(ldw.sem, LDW)
            for g in range(NG):
                t.wait_ge(lr3[g % 2].sem, 16 * (g // 2 + 1))
                for dc in range(8):
                    pk = g * 8 + dc
                    if pk >= 1:
                        t.wait_ge(etp.sem, pk)
                    for ti in range(2):
                        tr = t.transpose(
                            ps_t[:, ti * 128:(ti + 1) * 128],
                            r3g[g % 2][:, ti * D + dc * 128: ti * D + (dc + 1) * 128],
                            id_bf[:])
                    tr.then_inc(mtp.sem, 1)
                t.wait_ge(etp.sem, (g + 1) * 8)
                for sp in range(12):
                    s = g * 12 + sp
                    if s >= 2:
                        t.wait_ge(gsl.sem, s - 1)
                    for sub in range(2):
                        hc = sp * 2 + sub
                        for dc in range(8):
                            mm = t.matmul(ps_g[s % 2][:, sub * 256:(sub + 1) * 256],
                                          wg3[:, dc * 3072 + hc * 128: dc * 3072 + (hc + 1) * 128],
                                          rT[g % 2][:, dc * 256:(dc + 1) * 256],
                                          start=(dc == 0), stop=(dc == 7))
                    mm.then_inc(mg.sem, 1)
                    if s >= 2:
                        t.wait_ge(ehh.sem, s - 1)
                    for sub in range(2):
                        hc = sp * 2 + sub
                        for dc in range(8):
                            mm = t.matmul(ps_u[s % 2][:, sub * 256:(sub + 1) * 256],
                                          wu3[:, dc * 3072 + hc * 128: dc * 3072 + (hc + 1) * 128],
                                          rT[g % 2][:, dc * 256:(dc + 1) * 256],
                                          start=(dc == 0), stop=(dc == 7))
                    mm.then_inc(mu.sem, 1)
                t.wait_ge(ehh.sem, (g + 1) * 12)
                for ti in range(2):
                    u = g * 2 + ti
                    if u >= 1:
                        t.wait_ge(lnv.sem, (u - 1) * LV + 1)
                    for hc in range(24):
                        for nh in range(2):
                            mm = t.matmul(
                                po[:, nh * 512:(nh + 1) * 512],
                                hh[:, hc * 256 + ti * 128: hc * 256 + (ti + 1) * 128],
                                wd3[:, hc * D + nh * 512: hc * D + (nh + 1) * 512],
                                start=(hc == 0), stop=(hc == 23))
                    mm.then_inc(mdo.sem, 1)

        @block.scalar
        def _(sc):
            for g in range(NG):
                for sp in range(12):
                    s = g * 12 + sp
                    sc.wait_ge(mg.sem, s + 1)
                    if s >= 2:
                        sc.wait_ge(ehh.sem, s - 1)
                    sc.activation(sg_sb[s % 2][:], ps_g[s % 2][:], AF.Silu,
                                  scale=QD).then_inc(gsl.sem, 1)
                    gsl.n += 1
                sc.wait_ge(ldm.sem, 2)
                for ti in range(2):
                    u = g * 2 + ti
                    ln_scalar(sc, lnv, lns, u * LV + 4, rstd, bn2, epsb, l2h)

        @block.vector
        def _(v):
            for g in range(NG):
                if g >= 2:
                    v.wait_ge(mu.sem, 12 * (g - 1))
                for dc in range(8):
                    v.wait_ge(mtp.sem, g * 8 + dc + 1)
                    v.tensor_copy(rT[g % 2][:, dc * 256:(dc + 1) * 256], ps_t[:]).then_inc(etp.sem, 1)
                    etp.n += 1
                if g >= 1:
                    v.wait_ge(mdo.sem, 2 * g)
                for sp in range(12):
                    s = g * 12 + sp
                    v.wait_ge(mu.sem, s + 1)
                    v.wait_ge(gsl.sem, s + 1)
                    v.scalar_tensor_tensor(hh[:, sp * 512:(sp + 1) * 512], ps_u[s % 2][:],
                                           QD, sg_sb[s % 2][:], ALU.mult, ALU.mult).then_inc(ehh.sem, 1)
                    ehh.n += 1
                for ti in range(2):
                    u = g * 2 + ti
                    v.wait_ge(mdo.sem, u + 1)
                    v.wait_ge(lrf[u % 2].sem, 16 * (u // 2 + 1))
                    v.wait_ge(lnv.sem, lnv.n)
                    v.scalar_tensor_tensor(radd[:], po[:], QD, r3f[u % 2][:],
                                           ALU.mult, ALU.add).then_inc(lnv.sem, 1)
                    lnv.n += 1
                    v.wait_ge(lnv.sem, u * LV + 1)
                    if u >= 2:
                        v.wait_ge(st[u % 2].sem, 16 * (u // 2))
                    ln_vec(v, lnv, lns, 2 * (u + 1), radd, r4t[u % 2][:], bn6, bn2, rstd)

    return nc



def build_fused():
    nc = bass.Bass("TRN2", target_bir_lowering=False, debug=False)

    def ein(name, shape, dt=F32):
        return nc.dram_tensor(name, shape, dt, kind="ExternalInput").ap()

    io = dict(
        x_full=ein("x_full", [B, S, D]),
        xT_full=ein("xT_full", [B, D, S]),
        xT_own=ein("xT_own", [B, D, CH]),
        x_own=ein("x_own", [B, CH, D]),
        Wq=ein("Wq", [D, D]),
        WkT=ein("WkT", [D, D]),
        bq_t=ein("bq_t", [128, 8]),
        Wv=ein("Wv", [D, D]),
        bv_r=ein("bv_r", [1, D]),
        ones_m=ein("ones_m", [128, 128]),
        ident_f=ein("ident_f", [128, 128]),
        U1=ein("U1", [D, 256]),
        V1=ein("V1", [256, 4096]),
        b1_r=ein("b1_r", [1, 4096]),
        U2=ein("U2", [4096, 256]),
        V2=ein("V2", [256, D]),
        b2_r=ein("b2_r", [1, D]),
        qkv_w=ein("qkv_w", [D, 3 * D]),
        lnsc_t=ein("lnsc_t", [128, 8]),
        lnb_t=ein("lnb_t", [128, 8]),
        Wo=ein("Wo", [D, D]),
        wgT=ein("wgT", [D, 3072], I32),
        wuT=ein("wuT", [D, 3072], I32),
        wdT=ein("wdT", [3072, D], I32),
        qpT=nc.dram_tensor("qpT", [D, NT], F32, kind="Internal").ap(),
        r1=nc.dram_tensor("r1", [NT, D], F32, kind="Internal").ap(),
        r2=nc.dram_tensor("r2", [NT, D], F32, kind="Internal").ap(),
        r3=nc.dram_tensor("r3", [NT, D], F32, kind="Internal").ap(),
        out=nc.dram_tensor("out", [NT, D], F32, kind="ExternalOutput").ap(),
    )
    with ExitStack() as sctx:
        build_l0_qkv(nc, sctx, "a_", io)
        build_l0_attn(nc, sctx, "b_", io)
        build_l0_ffn(nc, sctx, "c_", io)
        build_l1_attn(nc, sctx, "d_", io)
        build_l1_ffn(nc, sctx, "e_", io)
    return nc



def _c(a):
    return np.ascontiguousarray(a)


def kernel(
    L0_attn_x, L0_attn_Wq, L0_attn_bq, L0_attn_Wk, L0_attn_bk, L0_attn_Wv,
    L0_attn_bv, L0_ffn_U1, L0_ffn_V1, L0_ffn_b1, L0_ffn_U2, L0_ffn_V2,
    L0_ffn_b2, L1_attn_ln_scale, L1_attn_ln_bias, L1_attn_qkv_kernel,
    L1_attn_out_kernel, L1_ffn_w_gate, L1_ffn_w_up, L1_ffn_w_down,
):
    x = np.asarray(L0_attn_x, dtype=np.float32)
    xT = _c(x.transpose(0, 2, 1))
    ones_m = np.ones((128, 128), np.float32)
    ident_f = np.eye(128, dtype=np.float32)
    cores = list(range(NCORES))

    nc = build_fused()
    shared = dict(
        x_full=x, xT_full=xT,
        Wq=np.asarray(L0_attn_Wq, np.float32),
        WkT=_c(np.asarray(L0_attn_Wk, np.float32).T),
        bq_t=_c(np.asarray(L0_attn_bq, np.float32).reshape(8, 128).T),
        Wv=np.asarray(L0_attn_Wv, np.float32),
        bv_r=np.asarray(L0_attn_bv, np.float32).reshape(1, D),
        ones_m=ones_m, ident_f=ident_f,
        U1=np.asarray(L0_ffn_U1, np.float32),
        V1=np.asarray(L0_ffn_V1, np.float32),
        b1_r=np.asarray(L0_ffn_b1, np.float32).reshape(1, 4096),
        U2=np.asarray(L0_ffn_U2, np.float32),
        V2=np.asarray(L0_ffn_V2, np.float32),
        b2_r=np.asarray(L0_ffn_b2, np.float32).reshape(1, D),
        qkv_w=_c(np.asarray(L1_attn_qkv_kernel, np.float32).reshape(D, 3 * D)),
        lnsc_t=_c(np.asarray(L1_attn_ln_scale, np.float32).reshape(8, 128).T),
        lnb_t=_c(np.asarray(L1_attn_ln_bias, np.float32).reshape(8, 128).T),
        Wo=np.asarray(L1_attn_out_kernel, np.float32),
        wgT=_c(np.asarray(L1_ffn_w_gate, np.int32).T),
        wuT=_c(np.asarray(L1_ffn_w_up, np.int32).T),
        wdT=_c(np.asarray(L1_ffn_w_down, np.int32).T),
    )
    maps = [dict(shared,
                 xT_own=_c(xT[:, :, c * CH:(c + 1) * CH]),
                 x_own=_c(x[:, c * CH:(c + 1) * CH, :])) for c in cores]
    res = run_bass_kernel_spmd(nc, maps, core_ids=cores)

    full = np.empty((B, S, D), dtype=np.float32)
    for c in cores:
        oc = np.asarray(res.results[c]["out"]).reshape(B, CH, D)
        full[:, c * CH:(c + 1) * CH, :] = oc
    return full


# version 08
# speedup vs baseline: 1.1968x; 1.0245x over previous
import numpy as np
import ml_dtypes
from contextlib import ExitStack

import concourse.bass as bass
import concourse.mybir as mybir
from concourse.bass_utils import run_bass_kernel_spmd

F32 = mybir.dt.float32
BF16 = mybir.dt.bfloat16
I32 = mybir.dt.int32
AF = mybir.ActivationFunctionType
ALU = mybir.AluOpType
AX = mybir.AxisListType

NCORES = 8
B = 4
S = 8192
D = 1024
CH = S // NCORES
NT = B * CH
EPS0 = 1e-5
SQ2 = float(np.sqrt(2.0))



LN2H = float(0.5 * np.log(2.0))


def ln_vec(v, lnv, lns, lns_after, src, dst, bn6, bn2, rstd):
    v.bn_stats(bn6[:, 0:6], src[:, 0:512]).then_inc(lnv.sem, 1)
    v.bn_stats(bn6[:, 6:12], src[:, 512:1024]).then_inc(lnv.sem, 1)
    lnv.n += 2
    v.wait_ge(lnv.sem, lnv.n)
    v.bn_aggr(bn2[:], bn6[:]).then_inc(lnv.sem, 1)
    lnv.n += 1
    v.wait_ge(lns.sem, lns_after)
    v.tensor_scalar(dst, src[:], bn2[:, 0:1], rstd[:], ALU.subtract, ALU.mult).then_inc(lnv.sem, 1)
    lnv.n += 1


def ln_scalar(sc, lnv, lns, lnv_ready, rstd, bn2, epsb, l2h):
    sc.wait_ge(lnv.sem, lnv_ready)
    sc.activation(rstd[:], bn2[:, 1:2], AF.Ln, bias=epsb[:], scale=2.0).then_inc(lns.sem, 1)
    lns.n += 1
    sc.wait_ge(lns.sem, lns.n)
    sc.activation(rstd[:], rstd[:], AF.Exp, bias=l2h[:], scale=-0.5).then_inc(lns.sem, 1)
    lns.n += 1


class Chan:

    def __init__(self, ctx, nc, name):
        self.sem = ctx.enter_context(nc.semaphore(name))
        self.n = 0

    def inc(self, inst, k=1):
        inst.then_inc(self.sem, k)
        self.n += k
        return self.n



def build_l0_qkv(nc=None, sctx=None, pre="", io=None):
    if nc is None:
        nc = bass.Bass("TRN2", target_bir_lowering=False, debug=False)
    if io is None:
        io = dict(
            xT_own=nc.dram_tensor("xT_own", [B, D, CH], F32, kind="ExternalInput").ap(),
            Wq=nc.dram_tensor("Wq", [D, D], F32, kind="ExternalInput").ap(),
            WkT=nc.dram_tensor("WkT", [D, D], F32, kind="ExternalInput").ap(),
            bq_t=nc.dram_tensor("bq_t", [128, 8], F32, kind="ExternalInput").ap(),
            qpT=nc.dram_tensor("qpT", [D, NT], F32, kind="ExternalOutput").ap())
    xT_own, Wq, WkT, bq_t, qpT_o = io["xT_own"], io["Wq"], io["WkT"], io["bq_t"], io["qpT"]

    with ExitStack() as ctx:
        cc = sctx if sctx is not None else ctx
        ec_ = ctx.enter_context
        xTo = ec_(nc.sbuf_tensor(pre + "xTo", [128, 8 * NT], BF16))
        wq = ec_(nc.sbuf_tensor(pre + "wq", [128, 8 * D], BF16))
        wkt = ec_(nc.sbuf_tensor(pre + "wkt", [128, 8 * D], BF16))
        bqv = ec_(nc.sbuf_tensor(pre + "bqv", [128, 8], F32))
        qt_tg = ec_(nc.sbuf_tensor(pre + "qt_tg", [128, 8 * 1024], BF16))
        qp_sb = [ec_(nc.sbuf_tensor(pre + f"qp_sb{i}", [128, 1024], F32)) for i in range(2)]
        ps = [ec_(nc.psum_tensor(pre + f"ps{i}", [128, 1024], F32)) for i in range(2)]
        ld = Chan(cc, nc, pre + "ld")
        mmq = Chan(cc, nc, pre + "mmq")
        evq = Chan(cc, nc, pre + "evq")
        mmp = Chan(cc, nc, pre + "mmp")
        evp = Chan(cc, nc, pre + "evp")
        st = [Chan(cc, nc, pre + f"st{i}") for i in range(2)]
        block = ec_(nc.Block())

        @block.gpsimd
        def _(g):
            ld.inc(g.dma_start(out=wq[:].rearrange("p (c f) -> p c f", f=D),
                               in_=Wq.rearrange("(c p) f -> p c f", p=128)), 16)
            ld.inc(g.dma_start(out=wkt[:].rearrange("p (c f) -> p c f", f=D),
                               in_=WkT.rearrange("(c p) f -> p c f", p=128)), 16)
            ld.inc(g.dma_start(out=bqv[:], in_=bq_t), 16)
            for b in range(B):
                ld.inc(g.dma_start(
                    out=xTo[:].rearrange("p (c s) -> p c s", s=NT)[:, :, b * CH:(b + 1) * CH],
                    in_=xT_own[b].rearrange("(c p) s -> p c s", p=128)), 16)

        LD_ALL = (3 + B) * 16

        @block.tensor
        def _(t):
            t.wait_ge(ld.sem, LD_ALL)
            for tg in range(4):
                t0 = tg * 1024
                if tg > 0:
                    t.wait_ge(evp.sem, tg * 8)
                for ecp in range(8):
                    u = tg * 8 + ecp
                    if u >= 2:
                        t.wait_ge(evq.sem, u - 1)
                    p = ps[u % 2]
                    for dc in range(8):
                        for nh in range(2):
                            mm = t.matmul(
                                p[:, nh * 512:(nh + 1) * 512],
                                wq[:, dc * D + ecp * 128: dc * D + (ecp + 1) * 128],
                                xTo[:, dc * NT + t0 + nh * 512: dc * NT + t0 + (nh + 1) * 512],
                                start=(dc == 0), stop=(dc == 7))
                    mm.then_inc(mmq.sem, 1)
                for dk in range(8):
                    u = tg * 8 + dk
                    t.wait_ge(evq.sem, tg * 8 + 8)
                    if u >= 2:
                        t.wait_ge(evp.sem, u - 1)
                    p = ps[u % 2]
                    for ecp in range(8):
                        for nh in range(2):
                            mm = t.matmul(
                                p[:, nh * 512:(nh + 1) * 512],
                                wkt[:, ecp * D + dk * 128: ecp * D + (dk + 1) * 128],
                                qt_tg[:, ecp * 1024 + nh * 512: ecp * 1024 + (nh + 1) * 512],
                                start=(ecp == 0), stop=(ecp == 7))
                    mm.then_inc(mmp.sem, 1)

        @block.vector
        def _(v):
            for tg in range(4):
                for ecp in range(8):
                    u = tg * 8 + ecp
                    v.wait_ge(mmq.sem, u + 1)
                    v.tensor_scalar(qt_tg[:, ecp * 1024:(ecp + 1) * 1024], ps[u % 2][:],
                                    bqv[:, ecp:ecp + 1], None, ALU.add).then_inc(evq.sem, 1)

        @block.scalar
        def _(sc):
            for tg in range(4):
                for dk in range(8):
                    u = tg * 8 + dk
                    sc.wait_ge(mmp.sem, u + 1)
                    if u >= 2:
                        sc.wait_ge(st[u % 2].sem, 16 * (u // 2))
                    sc.copy(qp_sb[u % 2][:], ps[u % 2][:]).then_inc(evp.sem, 1)
                    sc.wait_ge(evp.sem, u + 1)
                    sc.dma_start(
                        out=qpT_o[dk * 128:(dk + 1) * 128, tg * 1024:(tg + 1) * 1024],
                        in_=qp_sb[u % 2][:]).then_inc(st[u % 2].sem, 16)

        @block.sync
        def _(sy):
            sy.wait_ge(st[0].sem, 16 * 16)
            sy.wait_ge(st[1].sem, 16 * 16)

    return nc



def build_l0_attn(nc=None, sctx=None, pre="", io=None):
    if nc is None:
        nc = bass.Bass("TRN2", target_bir_lowering=False, debug=False)
    if io is None:
        io = dict(
            x_full=nc.dram_tensor("x_full", [B, S, D], F32, kind="ExternalInput").ap(),
            xT_full=nc.dram_tensor("xT_full", [B, D, S], F32, kind="ExternalInput").ap(),
            x_own=nc.dram_tensor("x_own", [B, CH, D], F32, kind="ExternalInput").ap(),
            qpT=nc.dram_tensor("qpT", [D, NT], F32, kind="ExternalInput").ap(),
            Wv=nc.dram_tensor("Wv", [D, D], F32, kind="ExternalInput").ap(),
            bv_r=nc.dram_tensor("bv_r", [1, D], F32, kind="ExternalInput").ap(),
            ones_m=nc.dram_tensor("ones_m", [128, 128], F32, kind="ExternalInput").ap(),
            ident_f=nc.dram_tensor("ident_f", [128, 128], F32, kind="ExternalInput").ap(),
            r1=nc.dram_tensor("r1", [NT, D], F32, kind="ExternalOutput").ap())
    x_full, xT_full, x_own, qpT_i = io["x_full"], io["xT_full"], io["x_own"], io["qpT"]
    Wv, bv_r, ones_m, ident_f, r1_o = io["Wv"], io["bv_r"], io["ones_m"], io["ident_f"], io["r1"]

    NKQ = 8
    KQ = S // NKQ
    KCQ = KQ // 128
    CPQ = KCQ * 2
    CPB = NKQ * CPQ
    CXQ = 8 * 2

    with ExitStack() as ctx:
        cc = sctx if sctx is not None else ctx
        ec_ = ctx.enter_context
        qpt = ec_(nc.sbuf_tensor(pre + "qpt", [128, 8 * CH], BF16))
        xtq = [ec_(nc.sbuf_tensor(pre + f"xtq{i}", [128, 8 * KQ], BF16)) for i in range(2)]
        xbq = [ec_(nc.sbuf_tensor(pre + f"xbq{i}", [128, KCQ * D], BF16)) for i in range(2)]
        att = ec_(nc.sbuf_tensor(pre + "att", [128, KCQ * CH], BF16))
        ctxa = ec_(nc.sbuf_tensor(pre + "ctxa", [128, 8 * CH], F32))
        ctxb = ec_(nc.sbuf_tensor(pre + "ctxb", [128, 8 * CH], BF16))
        wv = ec_(nc.sbuf_tensor(pre + "wv", [128, 8 * D], BF16))
        ones_bf = ec_(nc.sbuf_tensor(pre + "ones_bf", [128, 128], BF16))
        id_f = ec_(nc.sbuf_tensor(pre + "id_f", [128, 128], F32))
        neg32 = ec_(nc.sbuf_tensor(pre + "neg32", [128, 1], F32))
        epsb = ec_(nc.sbuf_tensor(pre + "epsb", [128, 1], F32))
        l2h = ec_(nc.sbuf_tensor(pre + "l2h", [128, 1], F32))
        bv_bf = ec_(nc.sbuf_tensor(pre + "bv_bf", [1, D], BF16))
        bvb = ec_(nc.sbuf_tensor(pre + "bvb", [128, D], F32))
        sums_sb = ec_(nc.sbuf_tensor(pre + "sums_sb", [1, CH], F32))
        recip_sb = ec_(nc.sbuf_tensor(pre + "recip_sb", [1, CH], F32))
        recip_t = ec_(nc.sbuf_tensor(pre + "recip_t", [128, 8], F32))
        xo_t = [ec_(nc.sbuf_tensor(pre + f"xo_t{i}", [128, D], F32)) for i in range(2)]
        radd = ec_(nc.sbuf_tensor(pre + "radd", [128, D], F32))
        r1_t = [ec_(nc.sbuf_tensor(pre + f"r1_t{i}", [128, D], F32)) for i in range(2)]
        bn6 = ec_(nc.sbuf_tensor(pre + "bn6", [128, 12], F32))
        bn2 = ec_(nc.sbuf_tensor(pre + "bn2", [128, 2], F32))
        rstd = ec_(nc.sbuf_tensor(pre + "rstd", [128, 1], F32))

        ps_sc = [ec_(nc.psum_tensor(pre + f"ps_sc{i}", [128, 512], F32)) for i in range(2)]
        ps_cx = [ec_(nc.psum_tensor(pre + f"ps_cx{i}", [128, 512], F32)) for i in range(2)]
        ps_sm = ec_(nc.psum_tensor(pre + "ps_sm", [1, 1024], F32))
        ps_out = ec_(nc.psum_tensor(pre + "ps_out", [128, 1024], F32))

        ldc = Chan(cc, nc, pre + "ldc")
        ldm = Chan(cc, nc, pre + "ldm")
        ldq = Chan(cc, nc, pre + "ldq")
        ltq = [Chan(cc, nc, pre + f"ltq{i}") for i in range(2)]
        lbq = [Chan(cc, nc, pre + f"lbq{i}") for i in range(2)]
        ldi = Chan(cc, nc, pre + "ldi")
        lxo = [Chan(cc, nc, pre + f"lxo{i}") for i in range(2)]
        msc = Chan(cc, nc, pre + "msc")
        msm = Chan(cc, nc, pre + "msm")
        mcx = Chan(cc, nc, pre + "mcx")
        mbv = Chan(cc, nc, pre + "mbv")
        mtr = Chan(cc, nc, pre + "mtr")
        mop = Chan(cc, nc, pre + "mop")
        ex = Chan(cc, nc, pre + "ex")
        ecx = Chan(cc, nc, pre + "ecx")
        ecb = Chan(cc, nc, pre + "ecb")
        erc = Chan(cc, nc, pre + "erc")
        ert = Chan(cc, nc, pre + "ert")
        ebv = Chan(cc, nc, pre + "ebv")
        lnv = Chan(cc, nc, pre + "lnv")
        lns = Chan(cc, nc, pre + "lns")
        st = [Chan(cc, nc, pre + f"st{i}") for i in range(2)]

        block = ec_(nc.Block())

        LV, LS = 6, 2

        @block.gpsimd
        def _(g):
            g.dma_start(out=wv[:].rearrange("p (c f) -> p c f", f=D),
                        in_=Wv.rearrange("(c p) f -> p c f", p=128)).then_inc(ldc.sem, 16)
            g.dma_start(out=ones_bf[:], in_=ones_m).then_inc(ldc.sem, 16)
            g.dma_start(out=bv_bf[:], in_=bv_r).then_inc(ldc.sem, 16)
            g.memset(neg32[:], -32.0).then_inc(ldm.sem, 1)
            g.memset(epsb[:], 2.0 * EPS0).then_inc(ldm.sem, 1)
            g.memset(l2h[:], LN2H).then_inc(ldm.sem, 1)
            for b in range(B):
                if b > 0:
                    g.wait_ge(msc.sem, b * CPB)
                g.dma_start(
                    out=qpt[:].rearrange("p (c q) -> p c q", q=CH),
                    in_=qpT_i[:, b * CH:(b + 1) * CH].rearrange("(c p) q -> p c q", p=128),
                ).then_inc(ldq.sem, 16)
                for kq in range(NKQ):
                    qs = b * NKQ + kq
                    buf = qs % 2
                    if qs >= 2:
                        g.wait_ge(msc.sem, (qs - 1) * CPQ)
                    g.dma_start(
                        out=xtq[buf][:].rearrange("p (c k) -> p c k", k=KQ),
                        in_=xT_full[b, :, kq * KQ:(kq + 1) * KQ].rearrange("(c p) k -> p c k", p=128),
                    ).then_inc(ltq[buf].sem, 16)
                    if qs >= 2:
                        g.wait_ge(mcx.sem, (qs - 1) * CXQ)
                    g.dma_start(
                        out=xbq[buf][:].rearrange("p (c d) -> p c d", d=D),
                        in_=x_full[b, kq * KQ:(kq + 1) * KQ, :].rearrange("(c p) d -> p c d", p=128),
                    ).then_inc(lbq[buf].sem, 16)

        @block.sync
        def _(sy):
            sy.dma_start(out=id_f[:], in_=ident_f).then_inc(ldi.sem, 16)
            for b in range(B):
                for qt in range(8):
                    u = b * 8 + qt
                    if u >= 2:
                        sy.wait_ge(lnv.sem, (u - 2) * LV + 2)
                    sy.dma_start(out=xo_t[u % 2][:],
                                 in_=x_own[b, qt * 128:(qt + 1) * 128, :]).then_inc(lxo[u % 2].sem, 16)
                    sy.wait_ge(lnv.sem, (u + 1) * LV)
                    sy.dma_start(out=r1_o[u * 128:(u + 1) * 128, :],
                                 in_=r1_t[u % 2][:]).then_inc(st[u % 2].sem, 16)
            sy.wait_ge(st[0].sem, 16 * 16)
            sy.wait_ge(st[1].sem, 16 * 16)

        @block.tensor
        def _(t):
            t.wait_ge(ldc.sem, 48)
            for nh in range(2):
                t.matmul(ps_out[:, nh * 512:(nh + 1) * 512], ones_bf[0:1, :],
                         bv_bf[:, nh * 512:(nh + 1) * 512],
                         start=True, stop=True).then_inc(mbv.sem, 1)
            def emit_scores(t, b, kq):
                qs = b * NKQ + kq
                t.wait_ge(ltq[qs % 2].sem, 16 * (qs // 2 + 1))
                for kc in range(KCQ):
                    for qh in range(2):
                        ci = qs * CPQ + kc * 2 + qh
                        if ci >= 2:
                            t.wait_ge(ex.sem, ci - 1)
                        p = ps_sc[ci % 2]
                        for dc in range(8):
                            mm = t.matmul(
                                p[:],
                                xtq[qs % 2][:, dc * KQ + kc * 128: dc * KQ + (kc + 1) * 128],
                                qpt[:, dc * CH + qh * 512: dc * CH + (qh + 1) * 512],
                                start=(dc == 0), stop=(dc == 7))
                        mm.then_inc(msc.sem, 1)
                        pc = ci - 1
                        if pc >= b * CPB:
                            if pc == b * CPB:
                                t.wait_ge(erc.sem, 2 * b)
                            t.wait_ge(ex.sem, pc + 1)
                            kcp = (pc % CPQ) // 2
                            qhp = pc % 2
                            t.matmul(ps_sm[0:1, qhp * 512:(qhp + 1) * 512],
                                     ones_bf[:, 0:1],
                                     att[:, kcp * CH + qhp * 512: kcp * CH + (qhp + 1) * 512],
                                     start=(pc < b * CPB + 2),
                                     stop=(pc == (b + 1) * CPB - 2)).then_inc(msm.sem, 1)

            def emit_ctx(t, b, kq):
                qs = b * NKQ + kq
                t.wait_ge(ex.sem, (qs + 1) * CPQ)
                t.wait_ge(lbq[qs % 2].sem, 16 * (qs // 2 + 1))
                for dc in range(8):
                    for qh in range(2):
                        gi = qs * CXQ + dc * 2 + qh
                        if gi >= 2:
                            t.wait_ge(ecx.sem, gi - 1)
                        if gi % (CXQ * NKQ) == 0 and gi > 0:
                            t.wait_ge(ert.sem, gi // (CXQ * NKQ))
                        p = ps_cx[gi % 2]
                        for kc in range(KCQ):
                            mm = t.matmul(
                                p[:],
                                xbq[qs % 2][:, kc * D + dc * 128: kc * D + (dc + 1) * 128],
                                att[:, kc * CH + qh * 512: kc * CH + (qh + 1) * 512],
                                start=(kc == 0), stop=(kc == KCQ - 1))
                        mm.then_inc(mcx.sem, 1)

            def emit_trailing(t, b):
                lc = (b + 1) * CPB - 1
                t.wait_ge(ex.sem, lc + 1)
                kcp = (lc % CPQ) // 2
                qhp = lc % 2
                t.matmul(ps_sm[0:1, qhp * 512:(qhp + 1) * 512], ones_bf[:, 0:1],
                         att[:, kcp * CH + qhp * 512: kcp * CH + (qhp + 1) * 512],
                         start=False, stop=True).then_inc(msm.sem, 1)
                t.wait_ge(erc.sem, 2 * (b + 1))
                t.wait_ge(ldi.sem, 16)
                if b > 0:
                    t.wait_ge(ert.sem, b)
                for qt in range(8):
                    tr = t.matmul(ps_cx[0][:, qt:qt + 1],
                                  recip_sb[0:1, qt * 128:(qt + 1) * 128],
                                  id_f[0:1, 0:1], start=True, stop=True)
                tr.then_inc(mtr.sem, 1)

            def emit_outproj(t, b):
                t.wait_ge(ecb.sem, (b + 1) * 8)
                for qt in range(8):
                    u = b * 8 + qt
                    if u >= 1:
                        t.wait_ge(lnv.sem, (u - 1) * LV + 1)
                    else:
                        t.wait_ge(ebv.sem, 1)
                    for ecc in range(8):
                        for nh in range(2):
                            mm = t.matmul(
                                ps_out[:, nh * 512:(nh + 1) * 512],
                                ctxb[:, ecc * CH + qt * 128: ecc * CH + (qt + 1) * 128],
                                wv[:, ecc * D + nh * 512: ecc * D + (nh + 1) * 512],
                                start=(ecc == 0), stop=(ecc == 7))
                    mm.then_inc(mop.sem, 1)

            for b in range(B):
                t.wait_ge(ldq.sem, 16 * (b + 1))
                for kq in range(NKQ):
                    emit_scores(t, b, kq)
                    if kq == 0 and b >= 1:
                        emit_outproj(t, b - 1)
                    emit_ctx(t, b, kq)
                emit_trailing(t, b)
            emit_outproj(t, B - 1)

        @block.scalar
        def _(sc):
            sc.wait_ge(ldc.sem, 48)
            sc.wait_ge(ldm.sem, 3)
            for b in range(B):
                for kq in range(NKQ):
                    qs = b * NKQ + kq
                    for kc in range(KCQ):
                        for qh in range(2):
                            ci = qs * CPQ + kc * 2 + qh
                            sc.wait_ge(msc.sem, ci + 1)
                            if qs >= 1:
                                if kc == 0 and qh == 0:
                                    sc.wait_ge(mcx.sem, qs * CXQ)
                                sc.wait_ge(msm.sem, (qs - 1) * CPQ + kc * 2 + qh + 1)
                            sc.activation(att[:, kc * CH + qh * 512: kc * CH + (qh + 1) * 512],
                                          ps_sc[ci % 2][:], AF.Exp,
                                          bias=neg32[:], scale=1.0 / 32.0).then_inc(ex.sem, 1)
                    if kq == 0 and b >= 1:
                        for qt in range(8):
                            u = (b - 1) * 8 + qt
                            ln_scalar(sc, lnv, lns, u * LV + 5, rstd, bn2, epsb, l2h)
            for qt in range(8):
                u = (B - 1) * 8 + qt
                ln_scalar(sc, lnv, lns, u * LV + 5, rstd, bn2, epsb, l2h)

        @block.vector
        def _(v):
            v.wait_ge(mbv.sem, 2)
            v.tensor_copy(bvb[:], ps_out[:]).then_inc(ebv.sem, 1)
            for b in range(B):
                for kq in range(NKQ):
                    qs = b * NKQ + kq
                    for dc in range(8):
                        for qh in range(2):
                            gi = qs * CXQ + dc * 2 + qh
                            v.wait_ge(mcx.sem, gi + 1)
                            dst = ctxa[:, dc * CH + qh * 512: dc * CH + (qh + 1) * 512]
                            if kq == 0:
                                v.tensor_copy(dst, ps_cx[gi % 2][:]).then_inc(ecx.sem, 1)
                            else:
                                v.tensor_tensor(dst, dst, ps_cx[gi % 2][:], ALU.add).then_inc(ecx.sem, 1)
                            ecx.n += 1
                v.wait_ge(msm.sem, (b + 1) * CPB)
                v.tensor_copy(sums_sb[:], ps_sm[0:1, :]).then_inc(erc.sem, 1)
                v.wait_ge(erc.sem, 2 * b + 1)
                v.reciprocal(recip_sb[:], sums_sb[:]).then_inc(erc.sem, 1)
                if b > 0:
                    v.wait_ge(mop.sem, b * 8)
                for dc in range(8):
                    v.tensor_copy(ctxb[:, dc * CH:(dc + 1) * CH],
                                  ctxa[:, dc * CH:(dc + 1) * CH]).then_inc(ecb.sem, 1)
                v.wait_ge(mtr.sem, b + 1)
                v.tensor_copy(recip_t[:], ps_cx[0][:, 0:8]).then_inc(ert.sem, 1)
                v.wait_ge(ert.sem, b + 1)
                for qt in range(8):
                    u = b * 8 + qt
                    v.wait_ge(mop.sem, u + 1)
                    v.wait_ge(lxo[u % 2].sem, 16 * (u // 2 + 1))
                    v.wait_ge(lnv.sem, lnv.n)
                    v.scalar_tensor_tensor(radd[:], ps_out[:], recip_t[:, qt:qt + 1],
                                           bvb[:], ALU.mult, ALU.add).then_inc(lnv.sem, 1)
                    lnv.n += 1
                    v.wait_ge(lnv.sem, u * LV + 1)
                    v.tensor_tensor(radd[:], radd[:], xo_t[u % 2][:], ALU.add).then_inc(lnv.sem, 1)
                    lnv.n += 1
                    v.wait_ge(lnv.sem, u * LV + 2)
                    if u >= 2:
                        v.wait_ge(st[u % 2].sem, 16 * (u // 2))
                    ln_vec(v, lnv, lns, 2 * (u + 1), radd, r1_t[u % 2][:], bn6, bn2, rstd)

    return nc



GC1 = 0.7978845608028654
GC2 = GC1 * 0.044715


def build_l0_ffn(nc=None, sctx=None, pre="", io=None):
    if nc is None:
        nc = bass.Bass("TRN2", target_bir_lowering=False, debug=False)
    if io is None:
        io = dict(
            r1=nc.dram_tensor("r1", [NT, D], F32, kind="ExternalInput").ap(),
            U1=nc.dram_tensor("U1", [D, 256], F32, kind="ExternalInput").ap(),
            V1=nc.dram_tensor("V1", [256, 4096], F32, kind="ExternalInput").ap(),
            b1_r=nc.dram_tensor("b1_r", [1, 4096], F32, kind="ExternalInput").ap(),
            U2=nc.dram_tensor("U2", [4096, 256], F32, kind="ExternalInput").ap(),
            V2=nc.dram_tensor("V2", [256, D], F32, kind="ExternalInput").ap(),
            b2_r=nc.dram_tensor("b2_r", [1, D], F32, kind="ExternalInput").ap(),
            ones_m=nc.dram_tensor("ones_m", [128, 128], F32, kind="ExternalInput").ap(),
            ident_f=nc.dram_tensor("ident_f", [128, 128], F32, kind="ExternalInput").ap(),
            r2=nc.dram_tensor("r2", [NT, D], F32, kind="ExternalOutput").ap())
    r1_i, U1, V1, b1_r, U2, V2 = io["r1"], io["U1"], io["V1"], io["b1_r"], io["U2"], io["V2"]
    b2_r, ones_m, ident_f, r2_o = io["b2_r"], io["ones_m"], io["ident_f"], io["r2"]

    NG = 8
    with ExitStack() as ctx:
        cc = sctx if sctx is not None else ctx
        ec_ = ctx.enter_context
        r1g = [ec_(nc.sbuf_tensor(pre + f"r1g{i}", [128, 4 * D], BF16)) for i in range(2)]
        rT = [ec_(nc.sbuf_tensor(pre + f"rT{i}", [128, 8 * 512], BF16)) for i in range(2)]
        u1 = ec_(nc.sbuf_tensor(pre + "u1", [128, 8 * 256], BF16))
        v1 = ec_(nc.sbuf_tensor(pre + "v1", [128, 2 * 4096], BF16))
        u2 = ec_(nc.sbuf_tensor(pre + "u2", [128, 32 * 256], BF16))
        v2 = ec_(nc.sbuf_tensor(pre + "v2", [128, 2 * D], BF16))
        b1row = ec_(nc.sbuf_tensor(pre + "b1row", [1, 4096], BF16))
        ones_row = ec_(nc.sbuf_tensor(pre + "ones_row", [1, 512], BF16))
        epsb = ec_(nc.sbuf_tensor(pre + "epsb", [128, 1], F32))
        l2h = ec_(nc.sbuf_tensor(pre + "l2h", [128, 1], F32))
        b2_bf = ec_(nc.sbuf_tensor(pre + "b2_bf", [1, D], BF16))
        id_bf = ec_(nc.sbuf_tensor(pre + "id_bf", [128, 128], BF16))
        ones_bf = ec_(nc.sbuf_tensor(pre + "ones_bf", [128, 128], BF16))
        pT = ec_(nc.sbuf_tensor(pre + "pT", [128, 2 * 512], BF16))
        h2 = ec_(nc.sbuf_tensor(pre + "h2", [128, 32 * 512], BF16))
        q2T = ec_(nc.sbuf_tensor(pre + "q2T", [128, 2 * 512], BF16))
        zb = [ec_(nc.sbuf_tensor(pre + f"zb{i}", [128, 1024], BF16)) for i in range(2)]
        zsq = [ec_(nc.sbuf_tensor(pre + f"zsq{i}", [128, 1024], BF16)) for i in range(2)]
        um = [ec_(nc.sbuf_tensor(pre + f"um{i}", [128, 1024], BF16)) for i in range(2)]
        th = [ec_(nc.sbuf_tensor(pre + f"th{i}", [128, 1024], BF16)) for i in range(2)]
        r1f = [ec_(nc.sbuf_tensor(pre + f"r1f{i}", [128, D], F32)) for i in range(2)]
        o2b = ec_(nc.sbuf_tensor(pre + "o2b", [128, 4 * D], F32))
        radd = ec_(nc.sbuf_tensor(pre + "radd", [128, D], F32))
        r2t = [ec_(nc.sbuf_tensor(pre + f"r2t{i}", [128, D], F32)) for i in range(2)]
        bn6 = ec_(nc.sbuf_tensor(pre + "bn6", [128, 12], F32))
        bn2 = ec_(nc.sbuf_tensor(pre + "bn2", [128, 2], F32))
        rstd = ec_(nc.sbuf_tensor(pre + "rstd", [128, 1], F32))

        ps_t = ec_(nc.psum_tensor(pre + "ps_t", [128, 512], BF16))
        ps_z = [ec_(nc.psum_tensor(pre + f"ps_z{i}", [128, 1024], F32)) for i in range(2)]
        ps_o = ec_(nc.psum_tensor(pre + "ps_o", [128, 1024], F32))

        ldc = Chan(cc, nc, pre + "ldc")
        ldm = Chan(cc, nc, pre + "ldm")
        ldr = [Chan(cc, nc, pre + f"ldr{i}") for i in range(2)]
        lrf = [Chan(cc, nc, pre + f"lrf{i}") for i in range(2)]
        mtp = Chan(cc, nc, pre + "mtp")
        mP = Chan(cc, nc, pre + "mP")
        mz = Chan(cc, nc, pre + "mz")
        mq = Chan(cc, nc, pre + "mq")
        mo = Chan(cc, nc, pre + "mo")
        etp = Chan(cc, nc, pre + "etp")
        evP = Chan(cc, nc, pre + "evP")
        gv = Chan(cc, nc, pre + "gv")
        eo = Chan(cc, nc, pre + "eo")
        gs = Chan(cc, nc, pre + "gs")
        eq = Chan(cc, nc, pre + "eq")
        lnv = Chan(cc, nc, pre + "lnv")
        lns = Chan(cc, nc, pre + "lns")
        st = [Chan(cc, nc, pre + f"st{i}") for i in range(2)]
        block = ec_(nc.Block())

        LV = 5
        NSP = 16
        NSPT = NG * NSP

        def gsB(s):
            g, sp = s // NSP, s % NSP
            return 3 * NSP * g + (0 if sp == 0 else 3 * sp - 1)

        def th_cnt(s):
            g, sp = s // NSP, s % NSP
            return 3 * NSP * g + (3 * sp + 5 if sp < NSP - 1 else 3 * NSP)

        @block.gpsimd
        def _(g_):
            g_.dma_start(out=u1[:].rearrange("p (c f) -> p c f", f=256),
                         in_=U1.rearrange("(c p) f -> p c f", p=128)).then_inc(ldc.sem, 16)
            g_.dma_start(out=v1[:].rearrange("p (c f) -> p c f", f=4096),
                         in_=V1.rearrange("(c p) f -> p c f", p=128)).then_inc(ldc.sem, 16)
            g_.dma_start(out=u2[:].rearrange("p (c f) -> p c f", f=256),
                         in_=U2.rearrange("(c p) f -> p c f", p=128)).then_inc(ldc.sem, 16)
            g_.dma_start(out=v2[:].rearrange("p (c f) -> p c f", f=D),
                         in_=V2.rearrange("(c p) f -> p c f", p=128)).then_inc(ldc.sem, 16)
            g_.dma_start(out=b1row[:], in_=b1_r).then_inc(ldc.sem, 16)
            g_.dma_start(out=b2_bf[:], in_=b2_r).then_inc(ldc.sem, 16)
            g_.dma_start(out=id_bf[:], in_=ident_f).then_inc(ldc.sem, 16)
            g_.dma_start(out=ones_bf[:], in_=ones_m).then_inc(ldc.sem, 16)
            g_.memset(ones_row[:], 1.0).then_inc(ldm.sem, 1)
            g_.memset(epsb[:], 2.0 * EPS0).then_inc(ldm.sem, 1)
            g_.memset(l2h[:], LN2H).then_inc(ldm.sem, 1)
            for g in range(NG):
                if g >= 2:
                    g_.wait_ge(mtp.sem, 8 * (g - 1))
                g_.dma_start(
                    out=r1g[g % 2][:].rearrange("p (c d) -> p c d", d=D),
                    in_=r1_i[g * 512:(g + 1) * 512, :].rearrange("(c p) d -> p c d", p=128),
                ).then_inc(ldr[g % 2].sem, 16)
        LDC = 8 * 16

        @block.sync
        def _(sy):
            for g in range(NG):
                for ti in range(4):
                    u = g * 4 + ti
                    if u >= 2:
                        sy.wait_ge(lnv.sem, (u - 2) * LV + 1)
                    sy.dma_start(out=r1f[u % 2][:],
                                 in_=r1_i[u * 128:(u + 1) * 128, :]).then_inc(lrf[u % 2].sem, 16)
                for ti in range(4):
                    u = g * 4 + ti
                    sy.wait_ge(lnv.sem, (u + 1) * LV)
                    sy.dma_start(out=r2_o[u * 128:(u + 1) * 128, :],
                                 in_=r2t[u % 2][:]).then_inc(st[u % 2].sem, 16)
            sy.wait_ge(st[0].sem, 16 * 16)
            sy.wait_ge(st[1].sem, 16 * 16)

        @block.tensor
        def _(t):
            t.wait_ge(ldc.sem, LDC)
            t.wait_ge(ldm.sem, 1)
            for g in range(NG):
                gb = rT[g % 2]
                t.wait_ge(ldr[g % 2].sem, 16 * (g // 2 + 1))
                for dc in range(8):
                    for ti in range(4):
                        if g >= 1 or dc >= 1:
                            t.wait_ge(etp.sem, g * 8 + dc)
                        tr = t.transpose(
                            ps_t[:, ti * 128:(ti + 1) * 128],
                            r1g[g % 2][:, ti * D + dc * 128: ti * D + (dc + 1) * 128],
                            id_bf[:])
                    tr.then_inc(mtp.sem, 1)
                t.wait_ge(etp.sem, (g + 1) * 8)
                if g > 0:
                    t.wait_ge(gv.sem, g * NSP * 3)
                for rc in range(2):
                    i = g * 2 + rc
                    if i >= 2:
                        t.wait_ge(eq.sem, i - 1)
                    for dc in range(8):
                        mm = t.matmul(ps_z[rc][:, 0:512],
                                      u1[:, dc * 256 + rc * 128: dc * 256 + (rc + 1) * 128],
                                      gb[:, dc * 512:(dc + 1) * 512],
                                      start=(dc == 0), stop=(dc == 7))
                    mm.then_inc(mP.sem, 1)
                t.wait_ge(evP.sem, (g + 1) * 2)
                for sp in range(NSP):
                    s = g * NSP + sp
                    if sp >= 2:
                        t.wait_ge(gs.sem, gsB(s - 2) + 1)
                    for sub in range(2):
                        fc = sp * 2 + sub
                        t.matmul(ps_z[sp % 2][:, sub * 512:(sub + 1) * 512],
                                 b1row[0:1, fc * 128:(fc + 1) * 128],
                                 ones_row[0:1, :], start=True, stop=False)
                        for rc in range(2):
                            mm = t.matmul(ps_z[sp % 2][:, sub * 512:(sub + 1) * 512],
                                          v1[:, rc * 4096 + fc * 128: rc * 4096 + (fc + 1) * 128],
                                          pT[:, rc * 512:(rc + 1) * 512],
                                          start=False, stop=(rc == 1))
                    mm.then_inc(mz.sem, 1)
                t.wait_ge(gv.sem, (g + 1) * NSP * 3)
                for rc in range(2):
                    i = g * 2 + rc
                    if i >= 2:
                        t.wait_ge(eq.sem, i - 1)
                    for fc in range(32):
                        mm = t.matmul(ps_z[rc][:, 0:512],
                                      u2[:, fc * 256 + rc * 128: fc * 256 + (rc + 1) * 128],
                                      h2[:, fc * 512:(fc + 1) * 512],
                                      start=(fc == 0), stop=(fc == 31))
                    mm.then_inc(mq.sem, 1)
                t.wait_ge(eq.sem, (g + 1) * 2)
                for ti in range(4):
                    u = g * 4 + ti
                    if u >= 1:
                        t.wait_ge(eo.sem, u)
                    for nh in range(2):
                        t.matmul(ps_o[:, nh * 512:(nh + 1) * 512], ones_bf[0:1, :],
                                 b2_bf[:, nh * 512:(nh + 1) * 512], start=True, stop=False)
                    for rc in range(2):
                        for nh in range(2):
                            mm = t.matmul(
                                ps_o[:, nh * 512:(nh + 1) * 512],
                                q2T[:, rc * 512 + ti * 128: rc * 512 + (ti + 1) * 128],
                                v2[:, rc * D + nh * 512: rc * D + (nh + 1) * 512],
                                start=False, stop=(rc == 1))
                    mm.then_inc(mo.sem, 1)

        @block.scalar
        def _(sc):
            for g in range(NG + 1):
                if g >= 1:
                    gp = g - 1
                    for rc in range(2):
                        i = gp * 2 + rc
                        sc.wait_ge(mq.sem, i + 1)
                        sc.mul(q2T[:, rc * 512:(rc + 1) * 512], ps_z[rc][:, 0:512],
                               0.5).then_inc(eq.sem, 1)
                if g < NG:
                    for sp in range(NSP):
                        s = g * NSP + sp
                        sc.wait_ge(mz.sem, s + 1)
                        if s >= 2:
                            sc.wait_ge(gv.sem, (s - 1) * 3)
                        sc.copy(zb[s % 2][:], ps_z[sp % 2][:]).then_inc(gs.sem, 1)
                        gs.n += 1
                        sc.wait_ge(gs.sem, gsB(s) + 1)
                        sc.activation(zsq[s % 2][:], zb[s % 2][:], AF.Square).then_inc(gs.sem, 1)
                        gs.n += 1
                        if sp >= 1:
                            sc.wait_ge(gv.sem, (s - 1) * 3 + 2)
                            sc.activation(th[(s - 1) % 2][:], um[(s - 1) % 2][:],
                                          AF.Tanh).then_inc(gs.sem, 1)
                            gs.n += 1
                    s = g * NSP + NSP - 1
                    sc.wait_ge(gv.sem, s * 3 + 2)
                    sc.activation(th[s % 2][:], um[s % 2][:], AF.Tanh).then_inc(gs.sem, 1)
                    gs.n += 1
                if g >= 1:
                    gp = g - 1
                    sc.wait_ge(ldm.sem, 3)
                    for ti in range(4):
                        u = gp * 4 + ti
                        ln_scalar(sc, lnv, lns, u * LV + 4, rstd, bn2, epsb, l2h)

        @block.vector
        def _(v):
            for g in range(NG + 1):
                if g >= 1:
                    for ti in range(4):
                        u = (g - 1) * 4 + ti
                        v.wait_ge(mo.sem, u + 1)
                        v.tensor_copy(o2b[:, ti * D:(ti + 1) * D], ps_o[:]).then_inc(eo.sem, 1)
                        eo.n += 1
                if g < NG:
                    if g >= 2:
                        v.wait_ge(mP.sem, (g - 1) * 2)
                    for dc in range(8):
                        v.wait_ge(mtp.sem, g * 8 + dc + 1)
                        v.tensor_copy(rT[g % 2][:, dc * 512:(dc + 1) * 512],
                                      ps_t[:]).then_inc(etp.sem, 1)
                        etp.n += 1
                    for rc in range(2):
                        v.wait_ge(mP.sem, g * 2 + rc + 1)
                        v.tensor_copy(pT[:, rc * 512:(rc + 1) * 512],
                                      ps_z[rc][:, 0:512]).then_inc(evP.sem, 1)
                    for sp in range(NSP):
                        s = g * NSP + sp
                        fc0 = sp * 2
                        v.wait_ge(gs.sem, gsB(s) + 2)
                        v.tensor_scalar(um[s % 2][:], zsq[s % 2][:], GC2, GC1,
                                        ALU.mult, ALU.add).then_inc(gv.sem, 1)
                        gv.n += 1
                        v.wait_ge(gv.sem, s * 3 + 1)
                        v.tensor_tensor(um[s % 2][:], um[s % 2][:], zb[s % 2][:],
                                        ALU.mult).then_inc(gv.sem, 1)
                        gv.n += 1
                        v.wait_ge(gs.sem, th_cnt(s))
                        v.scalar_tensor_tensor(h2[:, fc0 * 512:(fc0 + 2) * 512],
                                               th[s % 2][:], 1.0, zb[s % 2][:],
                                               ALU.add, ALU.mult).then_inc(gv.sem, 1)
                        gv.n += 1
                if g >= 1:
                    gp = g - 1
                    for ti in range(4):
                        u = gp * 4 + ti
                        v.wait_ge(lrf[u % 2].sem, 16 * (u // 2 + 1))
                        v.wait_ge(lnv.sem, lnv.n)
                        v.tensor_tensor(radd[:], o2b[:, ti * D:(ti + 1) * D], r1f[u % 2][:],
                                        ALU.add).then_inc(lnv.sem, 1)
                        lnv.n += 1
                        v.wait_ge(lnv.sem, u * LV + 1)
                        if u >= 2:
                            v.wait_ge(st[u % 2].sem, 16 * (u // 2))
                        ln_vec(v, lnv, lns, 2 * (u + 1), radd, r2t[u % 2][:], bn6, bn2, rstd)

    return nc



def build_l1_attn(nc=None, sctx=None, pre="", io=None):
    if nc is None:
        nc = bass.Bass("TRN2", target_bir_lowering=False, debug=False)
    if io is None:
        io = dict(
            r2=nc.dram_tensor("r2", [NT, D], F32, kind="ExternalInput").ap(),
            qkv_w=nc.dram_tensor("qkv_w", [D, 3 * D], F32, kind="ExternalInput").ap(),
            lnsc_t=nc.dram_tensor("lnsc_t", [128, 8], F32, kind="ExternalInput").ap(),
            lnb_t=nc.dram_tensor("lnb_t", [128, 8], F32, kind="ExternalInput").ap(),
            Wo=nc.dram_tensor("Wo", [D, D], F32, kind="ExternalInput").ap(),
            ones_m=nc.dram_tensor("ones_m", [128, 128], F32, kind="ExternalInput").ap(),
            ident_f=nc.dram_tensor("ident_f", [128, 128], F32, kind="ExternalInput").ap(),
            r3=nc.dram_tensor("r3", [NT, D], F32, kind="ExternalOutput").ap())
    r2_i, qkv_w, lnsc_t, lnb_t = io["r2"], io["qkv_w"], io["lnsc_t"], io["lnb_t"]
    Wo, ones_m, ident_f, r3_o = io["Wo"], io["ones_m"], io["ident_f"], io["r3"]

    EPS1 = 1e-6
    PAIRS = [(i, j) for i in range(4) for j in range(i + 1)]
    VMX_G = 52

    with ExitStack() as ctx:
        cc = sctx if sctx is not None else ctx
        ec_ = ctx.enter_context
        tng = [ec_(nc.sbuf_tensor(pre + f"tng{i}", [128, 4 * D], BF16)) for i in range(2)]
        wq3 = ec_(nc.sbuf_tensor(pre + "wq3", [128, 8 * 3 * D], BF16))
        wo = ec_(nc.sbuf_tensor(pre + "wo", [128, 8 * D], BF16))
        qkv_g = [ec_(nc.sbuf_tensor(pre + f"qkv_g{i}", [128, 4 * 3 * D], BF16)) for i in range(2)]
        tnT = ec_(nc.sbuf_tensor(pre + "tnT", [128, 8 * 512], BF16))
        cT = ec_(nc.sbuf_tensor(pre + "cT", [128, 8 * 512], BF16))
        ctxg = [ec_(nc.sbuf_tensor(pre + f"ctxg{i}", [128, 4 * D], BF16)) for i in range(2)]
        prodb = [ec_(nc.sbuf_tensor(pre + f"prodb{i}", [128, D], BF16)) for i in range(2)]
        tmpv = ec_(nc.sbuf_tensor(pre + "tmpv", [128, D], BF16))
        lgs = ec_(nc.sbuf_tensor(pre + "lgs", [128, 160], F32))
        egs = ec_(nc.sbuf_tensor(pre + "egs", [128, 160], F32))
        su = ec_(nc.sbuf_tensor(pre + "su", [128, 48], F32))
        rec = ec_(nc.sbuf_tensor(pre + "rec", [128, 48], F32))
        atn = ec_(nc.sbuf_tensor(pre + "atn", [128, 144], BF16))
        lnscv = ec_(nc.sbuf_tensor(pre + "lnscv", [128, 8], F32))
        lnbv = ec_(nc.sbuf_tensor(pre + "lnbv", [128, 8], BF16))
        bqkv = ec_(nc.sbuf_tensor(pre + "bqkv", [1, 3 * D], BF16))
        epsA = ec_(nc.sbuf_tensor(pre + "epsA", [128, 1], F32))
        epsB = ec_(nc.sbuf_tensor(pre + "epsB", [128, 1], F32))
        l2h = ec_(nc.sbuf_tensor(pre + "l2h", [128, 1], F32))
        id_bf = ec_(nc.sbuf_tensor(pre + "id_bf", [128, 128], BF16))
        ones_bf = ec_(nc.sbuf_tensor(pre + "ones_bf", [128, 128], BF16))
        r2f = [ec_(nc.sbuf_tensor(pre + f"r2f{i}", [128, D], F32)) for i in range(2)]
        r2e = [ec_(nc.sbuf_tensor(pre + f"r2e{i}", [128, D], F32)) for i in range(2)]
        radd = ec_(nc.sbuf_tensor(pre + "radd", [128, D], F32))
        r3t = [ec_(nc.sbuf_tensor(pre + f"r3t{i}", [128, D], F32)) for i in range(2)]
        bn6 = ec_(nc.sbuf_tensor(pre + "bn6", [128, 12], F32))
        bn2 = ec_(nc.sbuf_tensor(pre + "bn2", [128, 2], F32))
        rstd = ec_(nc.sbuf_tensor(pre + "rstd", [128, 1], F32))

        ps_t = ec_(nc.psum_tensor(pre + "ps_t", [128, 512], BF16))
        ps_tc = ec_(nc.psum_tensor(pre + "ps_tc", [128, 512], BF16))
        pq = [ec_(nc.psum_tensor(pre + f"pq{i}", [128, 512], F32)) for i in range(2)]
        po = ec_(nc.psum_tensor(pre + "po", [128, 1024], F32))
        ps_b = ec_(nc.psum_tensor(pre + "ps_b", [1, 512], F32))

        ldw = Chan(cc, nc, pre + "ldw")
        ldm = Chan(cc, nc, pre + "ldm")
        lr2a = [Chan(cc, nc, pre + f"lr2a{i}") for i in range(2)]
        lr2e = [Chan(cc, nc, pre + f"lr2e{i}") for i in range(2)]
        mbq = Chan(cc, nc, pre + "mbq")
        ebq = Chan(cc, nc, pre + "ebq")
        evw = Chan(cc, nc, pre + "evw")
        mtp = Chan(cc, nc, pre + "mtp")
        etp = Chan(cc, nc, pre + "etp")
        mq = Chan(cc, nc, pre + "mq")
        eqk = Chan(cc, nc, pre + "eqk")
        vmx = Chan(cc, nc, pre + "vmx")
        smx = Chan(cc, nc, pre + "smx")
        mtc = Chan(cc, nc, pre + "mtc")
        etc_ = Chan(cc, nc, pre + "etc")
        mpo = Chan(cc, nc, pre + "mpo")
        lnv = Chan(cc, nc, pre + "lnv")
        lns = Chan(cc, nc, pre + "lns")
        st = [Chan(cc, nc, pre + f"st{i}") for i in range(2)]
        block = ec_(nc.Block())

        def lnvA(a):
            sj, b = a // 4, a % 4
            return 16 * sj + 20 * max(0, sj - 1) + 4 * b

        def lnvB(k):
            sj, ti = k // 4, k % 4
            return 16 * min(sj + 2, 8) + 20 * sj + 5 * ti

        def lnsA(a):
            sj, b = a // 4, a % 4
            return 8 * sj + 8 * max(0, sj - 1) + 2 * b

        def lnsB(k):
            sj, ti = k // 4, k % 4
            return 8 * min(sj + 2, 8) + 8 * sj + 2 * ti

        def a_tile(a):
            return (a % 4) * 8 + a // 4

        @block.gpsimd
        def _(g_):
            g_.dma_start(out=wq3[:].rearrange("p (c f) -> p c f", f=3 * D),
                         in_=qkv_w.rearrange("(c p) f -> p c f", p=128)).then_inc(ldw.sem, 16)
            g_.dma_start(out=wo[:].rearrange("p (c f) -> p c f", f=D),
                         in_=Wo.rearrange("(c p) f -> p c f", p=128)).then_inc(ldw.sem, 16)
            g_.dma_start(out=lnscv[:], in_=lnsc_t).then_inc(ldw.sem, 16)
            g_.dma_start(out=lnbv[:], in_=lnb_t).then_inc(ldw.sem, 16)
            g_.dma_start(out=id_bf[:], in_=ident_f).then_inc(ldw.sem, 16)
            g_.dma_start(out=ones_bf[:], in_=ones_m).then_inc(ldw.sem, 16)
            g_.memset(epsA[:], 2.0 * EPS1).then_inc(ldm.sem, 1)
            g_.memset(epsB[:], 2.0 * EPS0).then_inc(ldm.sem, 1)
            g_.memset(l2h[:], LN2H).then_inc(ldm.sem, 1)
        LDW = 96

        @block.sync
        def _(sy):
            for sj in range(9):
                if sj < 8:
                    for b in range(4):
                        a = sj * 4 + b
                        if a >= 2:
                            sy.wait_ge(lnv.sem, lnvA(a - 2) + 4)
                        sy.dma_start(out=r2f[a % 2][:],
                                     in_=r2_i[a_tile(a) * 128:(a_tile(a) + 1) * 128, :]
                                     ).then_inc(lr2a[a % 2].sem, 16)
                if sj >= 1:
                    mj = sj - 1
                    for ti in range(4):
                        u = ti * 8 + mj
                        k = mj * 4 + ti
                        if k >= 2:
                            sy.wait_ge(lnv.sem, lnvB(k - 2) + 1)
                        sy.dma_start(out=r2e[k % 2][:],
                                     in_=r2_i[u * 128:(u + 1) * 128, :]).then_inc(lr2e[k % 2].sem, 16)
                    for ti in range(4):
                        u = ti * 8 + mj
                        k = mj * 4 + ti
                        sy.wait_ge(lnv.sem, lnvB(k) + 5)
                        sy.dma_start(out=r3_o[u * 128:(u + 1) * 128, :],
                                     in_=r3t[k % 2][:]).then_inc(st[k % 2].sem, 16)
            sy.wait_ge(st[0].sem, 16 * 16)
            sy.wait_ge(st[1].sem, 16 * 16)

        @block.tensor
        def _(t):
            t.wait_ge(ldw.sem, LDW)
            for ch in range(6):
                for dc in range(8):
                    mm = t.matmul(ps_b[0:1, :], lnbv[:, dc:dc + 1],
                                  wq3[:, dc * 3 * D + ch * 512: dc * 3 * D + (ch + 1) * 512],
                                  start=(dc == 0), stop=(dc == 7))
                mm.then_inc(mbq.sem, 1)
                t.wait_ge(ebq.sem, ch + 1)
            for sj in range(9):
                if sj < 8:
                    t.wait_ge(lnv.sem, lnvA(sj * 4 + 3) + 4)
                    for dc in range(8):
                        pk = sj * 8 + dc
                        if pk >= 1:
                            t.wait_ge(etp.sem, pk)
                        for ti in range(4):
                            tr = t.transpose(
                                ps_t[:, ti * 128:(ti + 1) * 128],
                                tng[sj % 2][:, ti * D + dc * 128: ti * D + (dc + 1) * 128],
                                id_bf[:])
                        tr.then_inc(mtp.sem, 1)
                    t.wait_ge(etp.sem, (sj + 1) * 8)
                    t.wait_ge(evw.sem, 8)
                    if sj >= 2:
                        t.wait_ge(vmx.sem, (sj - 1) * VMX_G)
                    for ti in range(4):
                        for ch in range(6):
                            k = sj * 24 + ti * 6 + ch
                            if k >= 2:
                                t.wait_ge(eqk.sem, k - 1)
                            p = pq[k % 2]
                            t.matmul(p[:], ones_bf[0:1, :],
                                     bqkv[:, ch * 512:(ch + 1) * 512], start=True, stop=False)
                            for dc in range(8):
                                mm = t.matmul(
                                    p[:],
                                    tnT[:, dc * 512 + ti * 128: dc * 512 + (ti + 1) * 128],
                                    wq3[:, dc * 3 * D + ch * 512: dc * 3 * D + (ch + 1) * 512],
                                    start=False, stop=(dc == 7))
                            mm.then_inc(mq.sem, 1)
                if sj >= 1:
                    mj = sj - 1
                    t.wait_ge(vmx.sem, (mj + 1) * VMX_G)
                    for ec2 in range(8):
                        pk = mj * 8 + ec2
                        if pk >= 1:
                            t.wait_ge(etc_.sem, pk)
                        for ti in range(4):
                            tr = t.transpose(
                                ps_tc[:, ti * 128:(ti + 1) * 128],
                                ctxg[mj % 2][:, ti * D + ec2 * 128: ti * D + (ec2 + 1) * 128],
                                id_bf[:])
                        tr.then_inc(mtc.sem, 1)
                    t.wait_ge(etc_.sem, (mj + 1) * 8)
                    for ti in range(4):
                        k = mj * 4 + ti
                        if k > 0:
                            t.wait_ge(lnv.sem, lnvB(k - 1) + 1)
                        for ec2 in range(8):
                            for nh in range(2):
                                mm = t.matmul(
                                    po[:, nh * 512:(nh + 1) * 512],
                                    cT[:, ec2 * 512 + ti * 128: ec2 * 512 + (ti + 1) * 128],
                                    wo[:, ec2 * D + nh * 512: ec2 * D + (nh + 1) * 512],
                                    start=(ec2 == 0), stop=(ec2 == 7))
                        mm.then_inc(mpo.sem, 1)

        @block.scalar
        def _(sc):
            for sj in range(9):
                if sj < 8:
                    sc.wait_ge(ldm.sem, 3)
                    for b in range(4):
                        a = sj * 4 + b
                        ln_scalar(sc, lnv, lns, lnvA(a) + 3, rstd, bn2, epsA, l2h)
                    for ti in range(4):
                        for ch in range(6):
                            k = sj * 24 + ti * 6 + ch
                            sc.wait_ge(mq.sem, k + 1)
                            sc.copy(qkv_g[sj % 2][:, ti * 3 * D + ch * 512: ti * 3 * D + (ch + 1) * 512],
                                    pq[k % 2][:]).then_inc(eqk.sem, 1)
                            eqk.n += 1
                if sj >= 1:
                    mj = sj - 1
                    sc.wait_ge(vmx.sem, mj * VMX_G + 20)
                    sc.activation(egs[:], lgs[:], AF.Exp, scale=0.125).then_inc(smx.sem, 1)
                    smx.n += 1
                    for ti in range(4):
                        k = mj * 4 + ti
                        ln_scalar(sc, lnv, lns, lnvB(k) + 4, rstd, bn2, epsB, l2h)

        @block.vector
        def _(v):
            for ch in range(6):
                v.wait_ge(mbq.sem, ch + 1)
                v.tensor_copy(bqkv[:, ch * 512:(ch + 1) * 512], ps_b[0:1, :]).then_inc(ebq.sem, 1)
                ebq.n += 1
            v.wait_ge(mbq.sem, 6)
            for dc in range(8):
                v.wait_ge(evw.sem, dc)
                v.tensor_scalar(wq3[:, dc * 3 * D:(dc + 1) * 3 * D],
                                wq3[:, dc * 3 * D:(dc + 1) * 3 * D],
                                lnscv[:, dc:dc + 1], None, ALU.mult).then_inc(evw.sem, 1)
                evw.n += 1

            PIX = {(i, j): pi for pi, (i, j) in enumerate(PAIRS)}

            for sj in range(9):
                if sj < 8:
                    if sj >= 2:
                        v.wait_ge(mtp.sem, 8 * (sj - 1))
                    for b in range(4):
                        a = sj * 4 + b
                        v.wait_ge(lr2a[a % 2].sem, 16 * (a // 2 + 1))
                        ln_vec(v, lnv, lns, lnsA(a) + 2, r2f[a % 2],
                               tng[sj % 2][:, b * D:(b + 1) * D], bn6, bn2, rstd)
                    for dc in range(8):
                        v.wait_ge(mtp.sem, sj * 8 + dc + 1)
                        v.tensor_copy(tnT[:, dc * 512:(dc + 1) * 512], ps_t[:]).then_inc(etp.sem, 1)
                        etp.n += 1
                if sj >= 1:
                    mj = sj - 1
                    v.wait_ge(eqk.sem, (mj + 1) * 24)
                    if mj >= 2:
                        v.wait_ge(mtc.sem, 8 * (mj - 1))

                    def q_(i):
                        return qkv_g[mj % 2][:, i * 3 * D + 0: i * 3 * D + D]

                    def k_(i):
                        return qkv_g[mj % 2][:, i * 3 * D + D: i * 3 * D + 2 * D]

                    def v_(i):
                        return qkv_g[mj % 2][:, i * 3 * D + 2 * D: i * 3 * D + 3 * D]

                    def vop(inst):
                        inst.then_inc(vmx.sem, 1)
                        vmx.n += 1
                        v.wait_ge(vmx.sem, vmx.n)

                    for pi, (i, j) in enumerate(PAIRS):
                        vop(v.tensor_tensor(prodb[pi % 2][:], q_(i), k_(j), ALU.mult))
                        vop(v.tensor_reduce(lgs[:, pi * 16:(pi + 1) * 16],
                                            prodb[pi % 2][:].rearrange("p (h d) -> p h d", d=64),
                                            AX.X, ALU.add))
                    v.wait_ge(smx.sem, mj + 1)

                    def eg(i, j):
                        p = PIX[(i, j)]
                        return egs[:, p * 16:(p + 1) * 16]

                    for i in range(1, 4):
                        s_sl = su[:, (i - 1) * 16: i * 16]
                        vop(v.tensor_tensor(s_sl, eg(i, 0), eg(i, 1), ALU.add))
                        for j in range(2, i + 1):
                            vop(v.tensor_tensor(s_sl, s_sl, eg(i, j), ALU.add))
                    vop(v.reciprocal(rec[:], su[:]))
                    for i in range(1, 4):
                        for j in range(i + 1):
                            p = PIX[(i, j)] - 1
                            vop(v.tensor_tensor(atn[:, p * 16:(p + 1) * 16], eg(i, j),
                                                rec[:, (i - 1) * 16: i * 16], ALU.mult))
                    vop(v.tensor_copy(ctxg[mj % 2][:, 0:D], v_(0)))
                    for i in range(1, 4):
                        dst3 = ctxg[mj % 2][:, i * D:(i + 1) * D].rearrange("p (h d) -> p h d", d=64)
                        for j in range(i + 1):
                            p = PIX[(i, j)] - 1
                            a3 = atn[:, p * 16:(p + 1) * 16].unsqueeze(2).broadcast_to([128, 16, 64])
                            v3 = v_(j).rearrange("p (h d) -> p h d", d=64)
                            if j == 0:
                                vop(v.tensor_tensor(dst3, v3, a3, ALU.mult))
                            else:
                                vop(v.tensor_tensor(tmpv[:].rearrange("p (h d) -> p h d", d=64),
                                                    v3, a3, ALU.mult))
                                vop(v.tensor_tensor(dst3, dst3,
                                                    tmpv[:].rearrange("p (h d) -> p h d", d=64),
                                                    ALU.add))
                    for ec2 in range(8):
                        v.wait_ge(mtc.sem, mj * 8 + ec2 + 1)
                        v.tensor_copy(cT[:, ec2 * 512:(ec2 + 1) * 512], ps_tc[:]).then_inc(etc_.sem, 1)
                        etc_.n += 1
                    for ti in range(4):
                        k = mj * 4 + ti
                        v.wait_ge(mpo.sem, k + 1)
                        v.wait_ge(lr2e[k % 2].sem, 16 * (k // 2 + 1))
                        v.wait_ge(lnv.sem, lnv.n)
                        v.tensor_tensor(radd[:], po[:], r2e[k % 2][:], ALU.add).then_inc(lnv.sem, 1)
                        lnv.n += 1
                        v.wait_ge(lnv.sem, lnvB(k) + 1)
                        if k >= 2:
                            v.wait_ge(st[k % 2].sem, 16 * (k // 2))
                        ln_vec(v, lnv, lns, lnsB(k) + 2, radd, r3t[k % 2][:], bn6, bn2, rstd)

    return nc



def build_l1_ffn(nc=None, sctx=None, pre="", io=None):
    if nc is None:
        nc = bass.Bass("TRN2", target_bir_lowering=False, debug=False)
    if io is None:
        io = dict(
            r3=nc.dram_tensor("r3", [NT, D], F32, kind="ExternalInput").ap(),
            wgT=nc.dram_tensor("wgT", [D, 3072], I32, kind="ExternalInput").ap(),
            wuT=nc.dram_tensor("wuT", [D, 3072], I32, kind="ExternalInput").ap(),
            wdT=nc.dram_tensor("wdT", [3072, D], I32, kind="ExternalInput").ap(),
            ident_f=nc.dram_tensor("ident_f", [128, 128], F32, kind="ExternalInput").ap(),
            out=nc.dram_tensor("out", [NT, D], F32, kind="ExternalOutput").ap())
    r3_i, wgT_s, wuT_s, wdT_s = io["r3"], io["wgT"], io["wuT"], io["wdT"]
    ident_f, out_o = io["ident_f"], io["out"]

    NG = 16
    QD = 1.0 / 127.0
    with ExitStack() as ctx:
        cc = sctx if sctx is not None else ctx
        ec_ = ctx.enter_context
        wg3 = ec_(nc.sbuf_tensor(pre + "wg3", [128, 8 * 3072], BF16))
        wu3 = ec_(nc.sbuf_tensor(pre + "wu3", [128, 8 * 3072], BF16))
        wd3 = ec_(nc.sbuf_tensor(pre + "wd3", [128, 24 * D], BF16))
        id_bf = ec_(nc.sbuf_tensor(pre + "id_bf", [128, 128], BF16))
        r3g = [ec_(nc.sbuf_tensor(pre + f"r3g{i}", [128, 2 * D], BF16)) for i in range(2)]
        rT = [ec_(nc.sbuf_tensor(pre + f"rT{i}", [128, 8 * 256], BF16)) for i in range(2)]
        hh = ec_(nc.sbuf_tensor(pre + "hh", [128, 24 * 256], BF16))
        sg_sb = [ec_(nc.sbuf_tensor(pre + f"sg_sb{i}", [128, 512], BF16)) for i in range(2)]
        r3f = [ec_(nc.sbuf_tensor(pre + f"r3f{i}", [128, D], F32)) for i in range(2)]
        radd = ec_(nc.sbuf_tensor(pre + "radd", [128, D], F32))
        r4t = [ec_(nc.sbuf_tensor(pre + f"r4t{i}", [128, D], F32)) for i in range(2)]
        bn6 = ec_(nc.sbuf_tensor(pre + "bn6", [128, 12], F32))
        bn2 = ec_(nc.sbuf_tensor(pre + "bn2", [128, 2], F32))
        rstd = ec_(nc.sbuf_tensor(pre + "rstd", [128, 1], F32))

        epsb = ec_(nc.sbuf_tensor(pre + "epsb", [128, 1], F32))
        l2h = ec_(nc.sbuf_tensor(pre + "l2h", [128, 1], F32))
        ps_t = ec_(nc.psum_tensor(pre + "ps_t", [128, 256], BF16))
        ps_g = [ec_(nc.psum_tensor(pre + f"ps_g{i}", [128, 512], F32)) for i in range(2)]
        ps_u = [ec_(nc.psum_tensor(pre + f"ps_u{i}", [128, 512], F32)) for i in range(2)]
        po = ec_(nc.psum_tensor(pre + "po", [128, 1024], F32))

        ldw = Chan(cc, nc, pre + "ldw")
        lr3 = [Chan(cc, nc, pre + f"lr3{i}") for i in range(2)]
        lrf = [Chan(cc, nc, pre + f"lrf{i}") for i in range(2)]
        mtp = Chan(cc, nc, pre + "mtp")
        etp = Chan(cc, nc, pre + "etp")
        mg = Chan(cc, nc, pre + "mg")
        mu = Chan(cc, nc, pre + "mu")
        gsl = Chan(cc, nc, pre + "gsl")
        ehh = Chan(cc, nc, pre + "ehh")
        mdo = Chan(cc, nc, pre + "mdo")
        lnv = Chan(cc, nc, pre + "lnv")
        lns = Chan(cc, nc, pre + "lns")
        st = [Chan(cc, nc, pre + f"st{i}") for i in range(2)]
        ldm = Chan(cc, nc, pre + "ldm")
        block = ec_(nc.Block())
        LV = 5

        @block.gpsimd
        def _(g_):
            g_.memset(epsb[:], 2.0 * EPS0).then_inc(ldm.sem, 1)
            g_.memset(l2h[:], LN2H).then_inc(ldm.sem, 1)
            g_.dma_start(out=wg3[:].rearrange("p (c f) -> p c f", f=3072),
                         in_=wgT_s.rearrange("(c p) f -> p c f", p=128)).then_inc(ldw.sem, 16)
            g_.dma_start(out=wu3[:].rearrange("p (c f) -> p c f", f=3072),
                         in_=wuT_s.rearrange("(c p) f -> p c f", p=128)).then_inc(ldw.sem, 16)
            g_.dma_start(out=wd3[:].rearrange("p (c f) -> p c f", f=D),
                         in_=wdT_s.rearrange("(c p) f -> p c f", p=128)).then_inc(ldw.sem, 16)
            g_.dma_start(out=id_bf[:], in_=ident_f).then_inc(ldw.sem, 16)
            for g in range(NG):
                if g >= 2:
                    g_.wait_ge(mtp.sem, 8 * (g - 1))
                g_.dma_start(
                    out=r3g[g % 2][:].rearrange("p (c d) -> p c d", d=D),
                    in_=r3_i[g * 256:(g + 1) * 256, :].rearrange("(c p) d -> p c d", p=128),
                ).then_inc(lr3[g % 2].sem, 16)
        LDW = 64

        @block.sync
        def _(sy):
            for g in range(NG):
                for ti in range(2):
                    u = g * 2 + ti
                    if u >= 2:
                        sy.wait_ge(lnv.sem, (u - 2) * LV + 1)
                    sy.dma_start(out=r3f[u % 2][:],
                                 in_=r3_i[u * 128:(u + 1) * 128, :]).then_inc(lrf[u % 2].sem, 16)
                    sy.wait_ge(lnv.sem, (u + 1) * LV)
                    sy.dma_start(out=out_o[u * 128:(u + 1) * 128, :],
                                 in_=r4t[u % 2][:]).then_inc(st[u % 2].sem, 16)
            sy.wait_ge(st[0].sem, 16 * 16)
            sy.wait_ge(st[1].sem, 16 * 16)

        @block.tensor
        def _(t):
            t.wait_ge(ldw.sem, LDW)
            for g in range(NG):
                t.wait_ge(lr3[g % 2].sem, 16 * (g // 2 + 1))
                for dc in range(8):
                    pk = g * 8 + dc
                    if pk >= 1:
                        t.wait_ge(etp.sem, pk)
                    for ti in range(2):
                        tr = t.transpose(
                            ps_t[:, ti * 128:(ti + 1) * 128],
                            r3g[g % 2][:, ti * D + dc * 128: ti * D + (dc + 1) * 128],
                            id_bf[:])
                    tr.then_inc(mtp.sem, 1)
                t.wait_ge(etp.sem, (g + 1) * 8)
                for sp in range(12):
                    s = g * 12 + sp
                    if s >= 2:
                        t.wait_ge(gsl.sem, s - 1)
                    for sub in range(2):
                        hc = sp * 2 + sub
                        for dc in range(8):
                            mm = t.matmul(ps_g[s % 2][:, sub * 256:(sub + 1) * 256],
                                          wg3[:, dc * 3072 + hc * 128: dc * 3072 + (hc + 1) * 128],
                                          rT[g % 2][:, dc * 256:(dc + 1) * 256],
                                          start=(dc == 0), stop=(dc == 7))
                    mm.then_inc(mg.sem, 1)
                    if s >= 2:
                        t.wait_ge(ehh.sem, s - 1)
                    for sub in range(2):
                        hc = sp * 2 + sub
                        for dc in range(8):
                            mm = t.matmul(ps_u[s % 2][:, sub * 256:(sub + 1) * 256],
                                          wu3[:, dc * 3072 + hc * 128: dc * 3072 + (hc + 1) * 128],
                                          rT[g % 2][:, dc * 256:(dc + 1) * 256],
                                          start=(dc == 0), stop=(dc == 7))
                    mm.then_inc(mu.sem, 1)
                t.wait_ge(ehh.sem, (g + 1) * 12)
                for ti in range(2):
                    u = g * 2 + ti
                    if u >= 1:
                        t.wait_ge(lnv.sem, (u - 1) * LV + 1)
                    for hc in range(24):
                        for nh in range(2):
                            mm = t.matmul(
                                po[:, nh * 512:(nh + 1) * 512],
                                hh[:, hc * 256 + ti * 128: hc * 256 + (ti + 1) * 128],
                                wd3[:, hc * D + nh * 512: hc * D + (nh + 1) * 512],
                                start=(hc == 0), stop=(hc == 23))
                    mm.then_inc(mdo.sem, 1)

        @block.scalar
        def _(sc):
            for g in range(NG):
                for sp in range(12):
                    s = g * 12 + sp
                    sc.wait_ge(mg.sem, s + 1)
                    if s >= 2:
                        sc.wait_ge(ehh.sem, s - 1)
                    sc.activation(sg_sb[s % 2][:], ps_g[s % 2][:], AF.Silu,
                                  scale=QD).then_inc(gsl.sem, 1)
                    gsl.n += 1
                sc.wait_ge(ldm.sem, 2)
                for ti in range(2):
                    u = g * 2 + ti
                    ln_scalar(sc, lnv, lns, u * LV + 4, rstd, bn2, epsb, l2h)

        @block.vector
        def _(v):
            for g in range(NG):
                if g >= 2:
                    v.wait_ge(mu.sem, 12 * (g - 1))
                for dc in range(8):
                    v.wait_ge(mtp.sem, g * 8 + dc + 1)
                    v.tensor_copy(rT[g % 2][:, dc * 256:(dc + 1) * 256], ps_t[:]).then_inc(etp.sem, 1)
                    etp.n += 1
                if g >= 1:
                    v.wait_ge(mdo.sem, 2 * g)
                for sp in range(12):
                    s = g * 12 + sp
                    v.wait_ge(mu.sem, s + 1)
                    v.wait_ge(gsl.sem, s + 1)
                    v.scalar_tensor_tensor(hh[:, sp * 512:(sp + 1) * 512], ps_u[s % 2][:],
                                           QD, sg_sb[s % 2][:], ALU.mult, ALU.mult).then_inc(ehh.sem, 1)
                    ehh.n += 1
                for ti in range(2):
                    u = g * 2 + ti
                    v.wait_ge(mdo.sem, u + 1)
                    v.wait_ge(lrf[u % 2].sem, 16 * (u // 2 + 1))
                    v.wait_ge(lnv.sem, lnv.n)
                    v.scalar_tensor_tensor(radd[:], po[:], QD, r3f[u % 2][:],
                                           ALU.mult, ALU.add).then_inc(lnv.sem, 1)
                    lnv.n += 1
                    v.wait_ge(lnv.sem, u * LV + 1)
                    if u >= 2:
                        v.wait_ge(st[u % 2].sem, 16 * (u // 2))
                    ln_vec(v, lnv, lns, 2 * (u + 1), radd, r4t[u % 2][:], bn6, bn2, rstd)

    return nc



def build_fused():
    nc = bass.Bass("TRN2", target_bir_lowering=False, debug=False)

    def ein(name, shape, dt=F32):
        return nc.dram_tensor(name, shape, dt, kind="ExternalInput").ap()

    io = dict(
        x_full=ein("x_full", [B, S, D]),
        xT_full=ein("xT_full", [B, D, S]),
        xT_own=ein("xT_own", [B, D, CH]),
        x_own=ein("x_own", [B, CH, D]),
        Wq=ein("Wq", [D, D]),
        WkT=ein("WkT", [D, D]),
        bq_t=ein("bq_t", [128, 8]),
        Wv=ein("Wv", [D, D]),
        bv_r=ein("bv_r", [1, D]),
        ones_m=ein("ones_m", [128, 128]),
        ident_f=ein("ident_f", [128, 128]),
        U1=ein("U1", [D, 256]),
        V1=ein("V1", [256, 4096]),
        b1_r=ein("b1_r", [1, 4096]),
        U2=ein("U2", [4096, 256]),
        V2=ein("V2", [256, D]),
        b2_r=ein("b2_r", [1, D]),
        qkv_w=ein("qkv_w", [D, 3 * D]),
        lnsc_t=ein("lnsc_t", [128, 8]),
        lnb_t=ein("lnb_t", [128, 8]),
        Wo=ein("Wo", [D, D]),
        wgT=ein("wgT", [D, 3072], I32),
        wuT=ein("wuT", [D, 3072], I32),
        wdT=ein("wdT", [3072, D], I32),
        qpT=nc.dram_tensor("qpT", [D, NT], F32, kind="Internal").ap(),
        r1=nc.dram_tensor("r1", [NT, D], F32, kind="Internal").ap(),
        r2=nc.dram_tensor("r2", [NT, D], F32, kind="Internal").ap(),
        r3=nc.dram_tensor("r3", [NT, D], F32, kind="Internal").ap(),
        out=nc.dram_tensor("out", [NT, D], F32, kind="ExternalOutput").ap(),
    )
    with ExitStack() as sctx:
        build_l0_qkv(nc, sctx, "a_", io)
        build_l0_attn(nc, sctx, "b_", io)
        build_l0_ffn(nc, sctx, "c_", io)
        build_l1_attn(nc, sctx, "d_", io)
        build_l1_ffn(nc, sctx, "e_", io)
    return nc



def _c(a):
    return np.ascontiguousarray(a)


def kernel(
    L0_attn_x, L0_attn_Wq, L0_attn_bq, L0_attn_Wk, L0_attn_bk, L0_attn_Wv,
    L0_attn_bv, L0_ffn_U1, L0_ffn_V1, L0_ffn_b1, L0_ffn_U2, L0_ffn_V2,
    L0_ffn_b2, L1_attn_ln_scale, L1_attn_ln_bias, L1_attn_qkv_kernel,
    L1_attn_out_kernel, L1_ffn_w_gate, L1_ffn_w_up, L1_ffn_w_down,
):
    x = np.asarray(L0_attn_x, dtype=np.float32)
    xT = _c(x.transpose(0, 2, 1))
    ones_m = np.ones((128, 128), np.float32)
    ident_f = np.eye(128, dtype=np.float32)
    cores = list(range(NCORES))

    nc = build_fused()
    shared = dict(
        x_full=x, xT_full=xT,
        Wq=np.asarray(L0_attn_Wq, np.float32),
        WkT=_c(np.asarray(L0_attn_Wk, np.float32).T),
        bq_t=_c(np.asarray(L0_attn_bq, np.float32).reshape(8, 128).T),
        Wv=np.asarray(L0_attn_Wv, np.float32),
        bv_r=np.asarray(L0_attn_bv, np.float32).reshape(1, D),
        ones_m=ones_m, ident_f=ident_f,
        U1=np.asarray(L0_ffn_U1, np.float32),
        V1=np.asarray(L0_ffn_V1, np.float32),
        b1_r=np.asarray(L0_ffn_b1, np.float32).reshape(1, 4096),
        U2=np.asarray(L0_ffn_U2, np.float32),
        V2=np.asarray(L0_ffn_V2, np.float32),
        b2_r=np.asarray(L0_ffn_b2, np.float32).reshape(1, D),
        qkv_w=_c(np.asarray(L1_attn_qkv_kernel, np.float32).reshape(D, 3 * D)),
        lnsc_t=_c(np.asarray(L1_attn_ln_scale, np.float32).reshape(8, 128).T),
        lnb_t=_c(np.asarray(L1_attn_ln_bias, np.float32).reshape(8, 128).T),
        Wo=np.asarray(L1_attn_out_kernel, np.float32),
        wgT=_c(np.asarray(L1_ffn_w_gate, np.int32).T),
        wuT=_c(np.asarray(L1_ffn_w_up, np.int32).T),
        wdT=_c(np.asarray(L1_ffn_w_down, np.int32).T),
    )
    maps = [dict(shared,
                 xT_own=_c(xT[:, :, c * CH:(c + 1) * CH]),
                 x_own=_c(x[:, c * CH:(c + 1) * CH, :])) for c in cores]
    res = run_bass_kernel_spmd(nc, maps, core_ids=cores)

    full = np.empty((B, S, D), dtype=np.float32)
    for c in cores:
        oc = np.asarray(res.results[c]["out"]).reshape(B, CH, D)
        full[:, c * CH:(c + 1) * CH, :] = oc
    return full
